# version 19
# baseline (speedup 1.0000x reference)
import contextlib
import numpy as np
import ml_dtypes
import concourse.bass as bass
import concourse.mybir as mybir
from concourse.bass_utils import run_bass_kernel_spmd

F32 = mybir.dt.float32
BF16 = mybir.dt.bfloat16
AF = mybir.ActivationFunctionType
ALU = mybir.AluOpType
AX = mybir.AxisListType

S = 2048
D = 1024
H2 = 2048
NS = 32
TSC = 64
CW = 512
NCHK = S // CW
WIN = 256
HEADS = 16
DH = 64
DI = 2730
MSK = -1.0e30
GCOL = 2049

ENGINES = ("tensor", "vector", "scalar", "gpsimd", "sync")


class Chan:
    __slots__ = ("name", "sem", "count", "step")

    def __init__(self, name, sem, step):
        self.name = name
        self.sem = sem
        self.count = 0
        self.step = step

    def mark(self):
        return (self, self.count)


class KB:

    def __init__(self, npass=4, depth=6, taps=()):
        self.npass = npass
        self.depth = depth
        self.taps = set(taps)
        self.nc = bass.Bass("TRN2", target_bir_lowering=False)
        self.st = contextlib.ExitStack()
        self.ops = {e: [] for e in ENGINES}
        self._waited = {e: {} for e in ENGINES}
        self.chans = []
        self.A = {}
        self.tap_d = {}

    def chan(self, name, step=1):
        sem = self.st.enter_context(self.nc.semaphore(name))
        c = Chan(name, sem, step)
        self.chans.append(c)
        return c

    def op(self, engine, fn, reads=(), incs=()):
        waits = []
        w = self._waited[engine]
        for rd in reads:
            if rd is None:
                continue
            c, cnt = rd
            if c is None or cnt <= 0:
                continue
            if w.get(c.name, 0) < cnt:
                w[c.name] = cnt
                waits.append((c, cnt))
        incl = []
        selfw = None
        auto = {"vector": self.cv, "scalar": self.ca}.get(engine)
        if auto is not None:
            selfw = (auto, auto.count)
            auto.count += auto.step
            incl.append((auto, auto.step))
            for c in incs:
                if c is auto:
                    continue
                c.count += c.step
                incl.append((c, c.step))
        else:
            for c in incs:
                c.count += c.step
                incl.append((c, c.step))
        self.ops[engine].append((fn, waits, incl, selfw))

    def fence(self, engine):
        rd = [(c, c.count) for c in self.chans if c.count > 0]
        if rd:
            self.op(engine, lambda e: None, reads=rd)

    def bar(self, engines=ENGINES):
        rd = [(c, c.count) for c in self.chans if c.count > 0]
        for e in engines:
            self.op(e, lambda eng: None, reads=rd)

    def emit(self):
        for c in self.chans:
            assert c.count < 60000, f"chan {c.name} count {c.count} too high"
        with self.nc.Block() as block:
            def run(eng_obj, name):
                for (fn, waits, incl, selfw) in self.ops[name]:
                    if selfw is not None and selfw[1] > 0:
                        eng_obj.wait_ge(selfw[0].sem, selfw[1])
                    for (c, cnt) in waits:
                        eng_obj.wait_ge(c.sem, cnt)
                    ins = fn(eng_obj)
                    if ins is None and incl:
                        ins = eng_obj.nop()
                    for i, (c, stp) in enumerate(incl):
                        if i == 0:
                            ins.then_inc(c.sem, stp)
                        else:
                            eng_obj.nop().then_inc(c.sem, stp)
            block.tensor(lambda t: run(t, "tensor"))
            block.vector(lambda v: run(v, "vector"))
            block.scalar(lambda a: run(a, "scalar"))
            block.gpsimd(lambda g: run(g, "gpsimd"))
            block.sync(lambda s: run(s, "sync"))

    def sb(self, name, shape, dtype):
        t = self.nc.alloc_sbuf_tensor(name, shape, dtype)
        self.A[name] = t
        return t

    _uid = 0

    def sbs(self, scope, name, shape, dtype, side=None):
        KB._uid += 1
        t = scope.enter_context(self.nc.sbuf_tensor(f"{name}__{KB._uid}", shape, dtype, side=side))
        self.A[name] = t
        return t

    def bank(self, i):
        return self.PB[i].ap()

    def bankb(self, i):
        return self.PB[i].ap().bitcast(BF16)

    def mktap(self, name, shape, dtype=F32):
        if name in self.taps and name not in self.tap_d:
            self.tap_d[name] = self.nc.dram_tensor("tap_" + name, shape, dtype,
                                                   kind="ExternalOutput").ap()
        return self.tap_d.get(name)

    def build(self):
        nc = self.nc
        d = {}
        specs = [
            ("L0_attn_x", [4, S, D]), ("L0_attn_W_in", [2 * H2, D]), ("L0_attn_b_in", [2 * H2]),
            ("L0_attn_conv_w", [H2, 1, 4]), ("L0_attn_conv_b", [H2]),
            ("L0_attn_W_ssm", [NS, H2]), ("L0_attn_b_ssm", [NS]), ("L0_attn_A", [NS, NS]),
            ("L0_attn_Bv", [NS]), ("L0_attn_Cv", [NS]), ("L0_attn_Dv", [H2]),
            ("L0_attn_W_out", [D, H2]), ("L0_attn_b_out", [D]),
            ("L0_ffn_Wqkv", [6, D, 3 * D]), ("L0_ffn_q_scale", [6, DH]), ("L0_ffn_k_scale", [6, DH]),
            ("L0_ffn_Wo", [6, D, D]), ("L0_ffn_ln_g", [6, D]), ("L0_ffn_ln_b", [6, D]),
            ("L0_ffn_W1", [6, D, 2 * DI]), ("L0_ffn_b1", [6, 2 * DI]),
            ("L0_ffn_W2", [6, DI, D]), ("L0_ffn_b2", [6, D]),
        ]
        for (n, sh) in specs:
            d[n] = nc.dram_tensor(n, sh, F32, kind="ExternalInput").ap()
        self.d = d
        self.outd = nc.dram_tensor("out", [4, S, D], F32, kind="ExternalOutput").ap()
        self.wintd = nc.dram_tensor("wint_scr", [D, 2 * H2], BF16, kind="Internal").ap()
        self.wouttd = nc.dram_tensor("woutt_scr", [H2, D], BF16, kind="Internal").ap()
        self.gscrd = nc.dram_tensor("g_scr", [D, S], BF16, kind="Internal").ap()
        self.w1cd = nc.dram_tensor("w1_cache", [6, 128, 44, 8, 128], BF16, kind="Internal").ap()
        self.w2cd = nc.dram_tensor("w2_cache", [6, 22, 128, 1024], BF16, kind="Internal").ap()
        self.wqcd = nc.dram_tensor("wq_cache", [6, 2, 128, 4 * 3072], BF16, kind="Internal").ap()
        self.wocd = nc.dram_tensor("wo_cache", [6, 128, 8 * 1024], BF16, kind="Internal").ap()
        self.wb_marks = {}
        self.w1_slot_guard = {i: [] for i in range(4)}
        self.w2_slot_guard = {i: [] for i in range(2)}
        self.wq_guard = []
        self.wo_guard = []

        self.PB = [self.nc.alloc_psum_tensor(f"pbank{i}", [128, 512], F32) for i in range(8)]

        self.cp = self.chan("cp")
        self.cv = self.chan("cv")
        self.ca = self.chan("ca")
        self.cg = self.chan("cg")
        self.d_cst = self.chan("d_cst", 16)
        self.d_row = self.chan("d_row", 16)
        self.d_wst = self.chan("d_wst", 16)
        self.d_tst = self.chan("d_tst", 16)
        self.d_af = self.chan("d_af", 16)
        self.d_xa = [self.chan("d_xa0", 16), self.chan("d_xa1", 16)]
        self.d_g = [self.chan("d_g0", 16), self.chan("d_g1", 16)]
        self.d_xc = self.chan("d_xc", 16)
        self.d_gt = self.chan("d_gt", 16)
        self.d_wb = [self.chan(f"d_wb{i}", 16) for i in range(4)]
        self.d_wc = self.chan("d_wc", 16)
        self.d_wo = [self.chan("d_wo0", 16), self.chan("d_wo1", 16)]
        self.d_yr = self.chan("d_yr", 16)
        self.d_out = self.chan("d_out", 16)
        self.d_tap = self.chan("d_tap", 16)
        self.d_rot = [self.chan("d_rot0", 16), self.chan("d_rot1", 16)]
        self.d_wq = self.chan("d_wq", 16)
        self.d_woL = self.chan("d_woL", 16)
        self.d_br = [self.chan("d_br0", 16), self.chan("d_br1", 16)]
        self.d_w1 = [self.chan(f"d_w1{i}", 16) for i in range(4)]
        self.d_w2 = [self.chan("d_w20", 16), self.chan("d_w21", 16)]
        self.store_chans = [self.d_tst, self.d_g[0], self.d_g[1], self.d_out, self.d_tap]
        self.ticks = {e: self.chan("tk_" + e) for e in ENGINES}

        self.consts_setup()
        self.globals_alloc()
        with contextlib.ExitStack() as sc0:
            self.phase0(sc0)
        self.bar()
        for p in range(self.npass):
            self.cp = self.chan(f"cp{p}")
            self.cv = self.chan(f"cv{p}")
            self.ca = self.chan(f"ca{p}")
            self.cg = self.chan(f"cg{p}")
            with contextlib.ExitStack() as scAB:
                self.scAB = scAB
                self.wq_free = None
                self.phase_A(p)
                self.bar()
                for l in range(self.depth):
                    with contextlib.ExitStack() as scL:
                        self.scL = scL
                        self.layer_attn(p, l)
                    self.bar()
                    with contextlib.ExitStack() as scL:
                        self.scL = scL
                        self.layer_ffn(p, l)
                    self.bar()
                self.g_store(p)
                self.bar()
            with contextlib.ExitStack() as scC:
                self.scC = scC
                self.phase_C(p)
                self.bar()
        self.op("sync", lambda s: None, reads=[c.mark() for c in self.store_chans])
        self.emit()
        self.st.close()
        return self

    def consts_setup(self):
        half = np.arange(0, DH, 2, dtype=np.float64)
        inv_freq = 1.0 / 10000.0 ** (half / DH)
        t = np.arange(S, dtype=np.float64)
        freqs = np.concatenate([t[:, None] * inv_freq[None, :]] * 2, -1)
        base_scale = (half + 0.4 * DH) / (1.4 * DH)
        power = (t - S // 2) / (WIN // 2)
        scale = np.concatenate([base_scale[None, :] ** power[:, None]] * 2, -1)
        cos, sin = np.cos(freqs), np.sin(freqs)
        sgn = np.where(np.arange(DH) < DH // 2, -1.0, 1.0)[None, :]
        def dup(tab):
            return np.ascontiguousarray(np.concatenate([tab, tab], 0).astype(np.float32))
        CS = dup((cos * scale).T)
        SS = dup((sin * scale * sgn).T)
        CI = dup((cos / scale).T)
        SI = dup((sin / scale * sgn).T)
        kp = np.arange(128)[:, None]
        qf = np.arange(256)[None, :]
        sTm = []
        for m in range(4):
            rel = m * 128 + kp - 256
            valid = (qf - rel >= 0) & (qf - rel <= WIN)
            sTm.append(np.where(valid, 0.0, MSK).astype(ml_dtypes.bfloat16))
        qp = np.arange(128)[:, None]
        s1m = []
        for qc in range(2):
            cols = []
            for mi in range(3):
                m = qc + mi
                kpp = np.arange(128)[None, :]
                rel = m * 128 + kpp - 256
                qg = qc * 128 + qp
                valid = (qg - rel >= 0) & (qg - rel <= WIN)
                cols.append(np.where(valid, 0.0, MSK))
            s1m.append(np.concatenate(cols, 1).astype(ml_dtypes.bfloat16))
        pswap = np.zeros((128, 128), dtype=ml_dtypes.bfloat16)
        for dd in range(128):
            blk = (dd // 64) * 64
            pswap[blk + ((dd - blk + 32) % 64), dd] = 1.0
        bones = np.zeros((128, 128), dtype=ml_dtypes.bfloat16)
        bones[0:64, 0:64] = 1.0
        bones[64:128, 64:128] = 1.0
        onesfull = np.ones((128, 128), dtype=ml_dtypes.bfloat16)
        self.consts = {
            "identb": np.eye(128, dtype=ml_dtypes.bfloat16),
            "onesfull": onesfull,
            "bones": bones, "pswap": pswap,
            "onescolb": np.ones((128, 1), dtype=ml_dtypes.bfloat16),
            "ones1b": np.ones((1, 128), dtype=ml_dtypes.bfloat16),
            "ones1f": np.ones((1, 128), dtype=np.float32),
            "one11f": np.ones((1, 1), dtype=np.float32),
            "i32f": np.eye(NS, dtype=np.float32),
            "sTm01": np.ascontiguousarray(np.concatenate([sTm[0], sTm[1]], 1)),
            "sTm23": np.ascontiguousarray(np.concatenate([sTm[2], sTm[3]], 1)),
        }
        self.cdram = {k: self.nc.inline_tensor(np.ascontiguousarray(v), name="c_" + k)
                      for k, v in self.consts.items()}
        rot4 = np.concatenate([t.astype(ml_dtypes.bfloat16)[None] for t in (CS, SS, CI, SI)], 0)
        self.rot4d = self.nc.inline_tensor(np.ascontiguousarray(rot4), name="c_rot4").ap()

    def globals_alloc(self):
        for k, v in self.consts.items():
            dt = BF16 if v.dtype == ml_dtypes.bfloat16 else F32
            self.sb(k, list(v.shape), dt)
        for (n, sh) in [("bin_c", [128, 32]), ("convw_c", [128, 64]), ("convb_c", [128, 16]),
                        ("dv_c", [128, 16]), ("bssm_c", [NS, 1]), ("bv_c", [NS, 1]),
                        ("cv_c", [NS, 1]), ("lng_c", [128, 48]),
                        ("lnb_c", [128, 48]), ("b1_c", [128, 264]), ("b2_c", [128, 48]),
                        ("qs_c", [128, 6]), ("ks_c", [128, 6]), ("q64f", [NS, NS]),
                        ("a_f", [NS, NS]), ("wchain", [NS, TSC + 1]), ("hcur", [NS, 1]),
                        ("hdsb", [NS, 8]), ("Hsb", [NS, 9]), ("lncol", [128, 16]), ("epsc", [128, 1]), ("epsq", [128, 1])]:
            self.sb(n, sh, F32)
        self.sb("bout_rowb", [1, D], BF16)
        for (n, sh) in [("wssmt", [128, 16 * NS]), ("k0ext", [NS, 2 * TSC]),
                        ("pbext", [NS, TSC * NS]), ("w2t", [NS, TSC]), ("HBb", [NS, 8])]:
            self.sb(n, sh, BF16)

    def phase0(self, sc0):
        A, d = self.A, self.d
        cp, cv, ca = self.cp, self.cv, self.ca
        d_cst, d_row, d_wst, d_tst = self.d_cst, self.d_row, self.d_wst, self.d_tst
        self.op("vector", lambda v: v.memset(A["epsc"][:, 0:1], 1e-5))
        self.op("vector", lambda v: v.memset(A["epsq"][:, 0:1], 1e-24))
        for k in self.consts:
            src = self.cdram[k].ap()
            npart = self.consts[k].shape[0]
            self.op("gpsimd", (lambda g, a=A[k], s=src, n=npart: g.dma_start(out=a[0:n, :], in_=s)),
                    incs=[d_cst])
        const_ld = d_cst.mark()
        rb = self.sbs(sc0, "rowbuf", [1, 8192], F32)
        qtmp = self.sbs(sc0, "qtmp", [NS, 2 * NS], F32)

        def load_row(dst, src):
            self.op("sync", (lambda s, dd=dst, ii=src: s.dma_start(out=dd, in_=ii)),
                    reads=[cp.mark()], incs=[d_row])

        def colz(dst_fn, srcs, psel=0):
            ps = self.bank(psel)
            for j, src in enumerate(srcs):
                w = src.shape[-1]
                self.op("tensor", (lambda t, ap=src, w=w, j=j, ps=ps: t.matmul(
                    ps[0:w, j:j + 1], ap, A["one11f"][0:1, 0:1], start=True, stop=True)),
                    reads=[d_row.mark(), cv.mark(), const_ld] if j == 0 else [],
                    incs=[cp] if j == len(srcs) - 1 else [])
            n = len(srcs)
            widths = [s.shape[-1] for s in srcs]
            def ev(v, n=n, ps=ps, widths=widths):
                last = None
                for j in range(n):
                    dst = dst_fn(j)
                    last = v.tensor_copy(dst, ps[0:widths[j], j:j + 1])
                return last
            self.op("vector", ev, reads=[cp.mark()], incs=[cv])

        load_row(rb[0:1, 0:4096], d["L0_attn_b_in"].rearrange("(o f) -> o f", o=1))
        colz(lambda j: A["bin_c"][:, j:j + 1], [rb[0:1, j * 128:(j + 1) * 128] for j in range(32)])
        load_row(rb[0:1, 0:8192], d["L0_attn_conv_w"].rearrange("c o k -> o (c k)"))
        for k in range(4):
            colz((lambda j, k=k: A["convw_c"][:, k * 16 + j:k * 16 + j + 1]),
                 [rb[0:1, (j * 512 + k):(j * 512 + k) + 509:4] for j in range(16)], psel=k % 2)
        load_row(rb[0:1, 0:2048], d["L0_attn_conv_b"].rearrange("(o f) -> o f", o=1))
        colz(lambda j: A["convb_c"][:, j:j + 1], [rb[0:1, j * 128:(j + 1) * 128] for j in range(16)])
        load_row(rb[0:1, 0:2048], d["L0_attn_Dv"].rearrange("(o f) -> o f", o=1))
        colz(lambda j: A["dv_c"][:, j:j + 1], [rb[0:1, j * 128:(j + 1) * 128] for j in range(16)], psel=1)
        load_row(rb[0:1, 0:32], d["L0_attn_b_ssm"].rearrange("(o f) -> o f", o=1))
        load_row(rb[0:1, 32:64], d["L0_attn_Bv"].rearrange("(o f) -> o f", o=1))
        load_row(rb[0:1, 64:96], d["L0_attn_Cv"].rearrange("(o f) -> o f", o=1))
        colz(lambda j: [A["bssm_c"], A["bv_c"], A["cv_c"]][j][0:NS, 0:1],
             [rb[0:1, 0:NS], rb[0:1, 32:64], rb[0:1, 64:96]])
        load_row(rb[0:1, 0:1024], d["L0_attn_b_out"].rearrange("(o f) -> o f", o=1))
        self.op("vector", (lambda v: v.tensor_copy(A["bout_rowb"][0:1, 0:D], rb[0:1, 0:1024])),
                reads=[d_row.mark()], incs=[cv])
        self.op("tensor", lambda t: t.nop(), reads=[cv.mark()], incs=[cp])
        for l in range(6):
            load_row(rb[0:1, 0:1024], d["L0_ffn_ln_g"][l].rearrange("(o f) -> o f", o=1))
            colz((lambda j, l=l: A["lng_c"][:, l * 8 + j:l * 8 + j + 1]),
                 [rb[0:1, j * 128:(j + 1) * 128] for j in range(8)])
            load_row(rb[0:1, 0:1024], d["L0_ffn_ln_b"][l].rearrange("(o f) -> o f", o=1))
            colz((lambda j, l=l: A["lnb_c"][:, l * 8 + j:l * 8 + j + 1]),
                 [rb[0:1, j * 128:(j + 1) * 128] for j in range(8)], psel=1)
            load_row(rb[0:1, 0:1024], d["L0_ffn_b2"][l].rearrange("(o f) -> o f", o=1))
            colz((lambda j, l=l: A["b2_c"][:, l * 8 + j:l * 8 + j + 1]),
                 [rb[0:1, j * 128:(j + 1) * 128] for j in range(8)])
            load_row(rb[0:1, 0:5460], d["L0_ffn_b1"][l].rearrange("(o f) -> o f", o=1))
            srcs = []
            for j in range(22):
                w = min(128, DI - j * 128)
                srcs.append(rb[0:1, j * 128:j * 128 + w])
            for j in range(22):
                w = min(128, DI - j * 128)
                srcs.append(rb[0:1, DI + j * 128:DI + j * 128 + w])
            colz((lambda j, l=l: A["b1_c"][0:(128 if (j % 22) < 21 else DI - 21 * 128),
                                           l * 44 + j:l * 44 + j + 1]), srcs, psel=1)
            load_row(rb[0:1, 0:64], d["L0_ffn_q_scale"][l].rearrange("(o f) -> o f", o=1))
            load_row(rb[0:1, 64:128], d["L0_ffn_k_scale"][l].rearrange("(o f) -> o f", o=1))
            for (nm, off) in (("qs_c", 0), ("ks_c", 64)):
                self.op("tensor", (lambda t, o=off: t.matmul(
                    self.bank(0)[0:64, 0:1], rb[0:1, o:o + 64], A["one11f"][0:1, 0:1],
                    start=True, stop=True)),
                    reads=[d_row.mark(), cv.mark()])
                self.op("tensor", (lambda t, o=off: t.matmul(
                    self.bank(0)[64:128, 0:1], rb[0:1, o:o + 64], A["one11f"][0:1, 0:1],
                    start=True, stop=True)), incs=[cp])
                self.op("vector", (lambda v, nm=nm, l=l: v.tensor_copy(
                    A[nm][:, l:l + 1], self.bank(0)[0:128, 0:1])), reads=[cp.mark()], incs=[cv])

        wstage = self.sbs(sc0, "wstage", [128, 2048], BF16)
        tstage = self.sbs(sc0, "tstage", [128, 2048], BF16)

        def transpose_weight(src2d, rows, cols, dst_dram):
            nrc, ncc = rows // 128, cols // 128
            for rc in range(nrc):
                self.op("gpsimd", (lambda g, rc=rc, src2d=src2d, cols=cols: g.dma_start(
                    out=wstage[:, 0:cols], in_=src2d[rc * 128:(rc + 1) * 128, :])),
                    reads=[cp.mark()], incs=[d_wst])
                ld = d_wst.mark()
                for cc in range(ncc):
                    psb = self.bankb(4) if (cc // 8) % 2 == 0 else self.bankb(5)
                    self.op("tensor", (lambda t, cc=cc, psb=psb: t.transpose(
                        psb[:, (cc % 8) * 128:(cc % 8) * 128 + 128],
                        wstage[:, cc * 128:(cc + 1) * 128], A["identb"][:, :])),
                        reads=[ld, cv.mark()] if cc % 8 == 0 else [],
                        incs=[cp] if (cc % 8 == 7 or cc == ncc - 1) else [])
                    if cc % 8 == 7 or cc == ncc - 1:
                        lo = (cc // 8) * 8
                        def ev(v, lo=lo, hi=cc, psb=psb):
                            last = None
                            for c2 in range(lo, hi + 1):
                                last = v.tensor_copy(tstage[:, c2 * 128:(c2 + 1) * 128],
                                                     psb[:, (c2 % 8) * 128:(c2 % 8) * 128 + 128])
                            return last
                        self.op("vector", ev, reads=[cp.mark(), d_tst.mark()], incs=[cv])
                self.op("sync", (lambda s, rc=rc, dst=dst_dram, ncc=ncc: s.dma_start(
                    out=dst.rearrange("(cc p) r -> p cc r", p=128)[:, :, rc * 128:(rc + 1) * 128],
                    in_=tstage[:, 0:ncc * 128].rearrange("p (cc r) -> p cc r", cc=ncc))),
                    reads=[cv.mark()], incs=[d_tst])

        transpose_weight(d["L0_attn_W_in"], 2 * H2, D, self.wintd)
        transpose_weight(d["L0_attn_W_out"], D, H2, self.wouttd)
        self.op("gpsimd", (lambda g: g.dma_start(out=wstage[0:NS, 0:H2], in_=d["L0_attn_W_ssm"])),
                reads=[cp.mark()], incs=[d_wst])
        wld = d_wst.mark()
        for cc in range(16):
            psb = self.bankb(4) if cc % 2 == 0 else self.bankb(5)
            self.op("tensor", (lambda t, cc=cc, psb=psb: t.transpose(
                psb[:, 0:NS], wstage[0:NS, cc * 128:(cc + 1) * 128], A["identb"][0:NS, 0:NS])),
                reads=[wld, cv.mark()], incs=[cp])
            self.op("vector", (lambda v, cc=cc, psb=psb: v.tensor_copy(
                A["wssmt"][:, cc * NS:(cc + 1) * NS], psb[:, 0:NS])),
                reads=[cp.mark()], incs=[cv])

        self.op("sync", (lambda s: s.dma_start(out=A["a_f"][0:NS, 0:NS], in_=d["L0_attn_A"])),
                incs=[self.d_af])
        ph32 = self.bank(6)
        self.op("vector", lambda v: v.tensor_copy(A["wchain"][0:NS, 0:1], A["cv_c"][0:NS, 0:1]),
                reads=[cv.mark()], incs=[cv])
        for d2 in range(TSC):
            self.op("tensor", (lambda t, d2=d2: t.matmul(
                ph32[0:NS, 0:1], A["a_f"][0:NS, 0:NS], A["wchain"][0:NS, d2:d2 + 1],
                start=True, stop=True)),
                reads=[cv.mark(), self.d_af.mark()], incs=[cp])
            self.op("vector", (lambda v, d2=d2: v.tensor_copy(
                A["wchain"][0:NS, d2 + 1:d2 + 2], ph32[0:NS, 0:1])),
                reads=[cp.mark()], incs=[cv])
        self.op("vector", lambda v: v.memset(A["k0ext"][0:NS, 0:TSC], 0.0))
        self.op("vector", (lambda v: v.tensor_scalar(
            A["k0ext"][0:NS, TSC:2 * TSC], A["wchain"][0:NS, 0:TSC], A["bv_c"][0:NS, 0:1], None,
            op0=ALU.mult)))
        self.op("vector", lambda v: v.tensor_copy(A["w2t"][0:NS, 0:TSC], A["wchain"][0:NS, 1:TSC + 1]))
        self.op("vector", lambda v: v.tensor_copy(qtmp[0:NS, 0:NS], A["i32f"][0:NS, 0:NS]), incs=[cv])
        for e in range(TSC + 1):
            cur = (e % 2) * NS
            nxt = ((e + 1) % 2) * NS
            if e < TSC:
                self.op("vector", (lambda v, e=e, cur=cur: v.tensor_scalar(
                    A["pbext"][0:NS, e * NS:(e + 1) * NS], qtmp[0:NS, cur:cur + NS],
                    A["bv_c"][0:NS, 0:1], None, op0=ALU.mult)))
            if e == TSC:
                self.op("vector", (lambda v, cur=cur: v.tensor_copy(
                    A["q64f"][0:NS, 0:NS], qtmp[0:NS, cur:cur + NS])), incs=[cv])
                break
            self.op("tensor", (lambda t, cur=cur: t.matmul(
                ph32[0:NS, 0:NS], A["a_f"][0:NS, 0:NS], qtmp[0:NS, cur:cur + NS],
                start=True, stop=True)),
                reads=[cv.mark()], incs=[cp])
            self.op("vector", (lambda v, nxt=nxt: v.tensor_copy(
                qtmp[0:NS, nxt:nxt + NS], ph32[0:NS, 0:NS])),
                reads=[cp.mark()], incs=[cv])

    def ln_tok(self, src_ap, dst_ap_bf, parity, extra_reads=()):
        A, cv, ca = self.A, self.cv, self.ca
        col = A["lncol"][:, parity * 8:parity * 8 + 8]
        self.op("vector", (lambda v, s=src_ap, col=col: v.reduce_sum(col[:, 0:1], s, axis=AX.X)),
                reads=list(extra_reads))
        self.op("vector", (lambda v, s=src_ap, col=col: v.scalar_tensor_tensor(
            A["lnsq"][:, 0:1024], s, 1.0, s, op0=ALU.mult, op1=ALU.mult,
            accum_out=col[:, 1:2])))
        self.op("vector", (lambda v, col=col: v.tensor_scalar(col[:, 0:1], col[:, 0:1], 1.0 / D, None, op0=ALU.mult)))
        self.op("vector", (lambda v, col=col: v.tensor_scalar(col[:, 1:2], col[:, 1:2], 1.0 / D, None, op0=ALU.mult)))
        self.op("vector", (lambda v, col=col: v.tensor_scalar(col[:, 2:3], col[:, 0:1], col[:, 0:1], None, op0=ALU.mult)))
        self.op("vector", (lambda v, col=col: v.tensor_sub(col[:, 3:4], col[:, 1:2], col[:, 2:3])), incs=[cv])
        self.op("scalar", (lambda a, col=col: a.activation(col[:, 4:5], col[:, 3:4], AF.Ln, bias=A["epsc"][:, 0:1])),
                reads=[cv.mark()], incs=[ca])
        self.op("scalar", (lambda a, col=col: a.activation(col[:, 5:6], col[:, 4:5], AF.Exp, scale=-0.5)),
                incs=[ca])
        self.op("vector", (lambda v, s=src_ap, dd=dst_ap_bf, col=col: v.tensor_scalar(
            dd, s, col[:, 0:1], col[:, 5:6], op0=ALU.subtract, op1=ALU.mult)),
            reads=[ca.mark()], incs=[cv])

    def phase_A(self, p):
        cp, cv = self.cp, self.cv
        d_xa = self.d_xa
        r = self.sbs(self.scAB, "r", [128, 8 * GCOL], F32, side="right")
        self.scA_local = contextlib.ExitStack()
        xa = self.sbs(self.scA_local, "xa", [128, 2 * 1024], F32)
        xnb = self.sbs(self.scA_local, "xnb", [128, 2 * 1024], BF16)
        self.sbs(self.scA_local, "lnsq", [128, 1024], BF16)
        identb = self.A["identb"]
        self.bar()
        for cc in range(8):
            self.op("vector", (lambda v, cc=cc, r=r: v.memset(r[:, cc * GCOL:cc * GCOL + 1], 0.0)))
        xa_free = [None, None]
        xnb_free = [None, None]
        pt_free = [None, None]
        for tt in range(16):
            b = tt % 2
            self.op("sync", (lambda s, tt=tt, b=b, p=p, xa=xa: s.dma_start(
                out=xa[:, b * 1024:(b + 1) * 1024],
                in_=self.d["L0_attn_x"][p, tt * 128:(tt + 1) * 128, :])),
                reads=[xa_free[b]], incs=[d_xa[b]])
            self.ln_tok(xa[:, b * 1024:(b + 1) * 1024], xnb[:, b * 1024:(b + 1) * 1024],
                        b, extra_reads=[d_xa[b].mark(), xnb_free[b]])
            xa_free[b] = cv.mark()
            psb = self.bankb(4) if b == 0 else self.bankb(5)
            for cc in range(8):
                self.op("tensor", (lambda t, cc=cc, b=b, psb=psb, xnb=xnb, identb=identb: t.transpose(
                    psb[:, cc * 128:(cc + 1) * 128],
                    xnb[:, b * 1024 + cc * 128:b * 1024 + (cc + 1) * 128], identb[:, :])),
                    reads=[cv.mark(), pt_free[b]] if cc == 0 else [],
                    incs=[cp] if cc == 7 else [])
            xnb_free[b] = cp.mark()
            def ev(v, tt=tt, psb=psb, r=r):
                last = None
                for cc in range(8):
                    last = v.tensor_copy(r[:, cc * GCOL + 1 + tt * 128:cc * GCOL + 1 + (tt + 1) * 128],
                                         psb[:, cc * 128:(cc + 1) * 128])
                return last
            self.op("vector", ev, reads=[cp.mark()], incs=[cv])
            pt_free[b] = cv.mark()
        self.r = r
        self.scA_local.close()

    def emit_xw_cast(self, wg, xw, r, scol, xw_free):
        cv = self.cv
        self.op("vector", (lambda v, wg=wg, xw=xw, r=r: v.tensor_copy(
            xw[:, 0:2048].rearrange("p (b c) -> p b c", b=4),
            r[:, :].rearrange("p (b c) -> p b c", b=8)[:, 0:4,
                                                      1 + wg * 512:1 + wg * 512 + 512])),
            reads=[xw_free])
        self.op("vector", (lambda v, wg=wg, xw=xw, r=r: v.tensor_copy(
            xw[:, 2048:4096].rearrange("p (b c) -> p b c", b=4),
            r[:, :].rearrange("p (b c) -> p b c", b=8)[:, 4:8,
                                                      wg * 512:wg * 512 + 512])))
        if wg > 0:
            for cc in range(4, 8):
                self.op("vector", (lambda v, cc=cc, wg=wg, xw=xw, scol=scol: v.tensor_copy(
                    xw[:, cc * 512:cc * 512 + 1],
                    scol[:, (wg - 1) * 4 + (cc - 4):(wg - 1) * 4 + (cc - 4) + 1])))
        self.op("vector", lambda v: v.nop(), incs=[cv])
        return cv.mark()

    def layer_attn(self, p, l):
        A = self.A
        cp, cv, ca, cg = self.cp, self.cv, self.ca, self.cg
        d_wq, d_woL, d_br = self.d_wq, self.d_woL, self.d_br
        sc = self.scL
        r = self.r
        wqkv_b = self.sbs(sc, "wqkv_b", [128, 8 * 3072], BF16)
        wo_b = self.sbs(sc, "wo_b", [128, 8 * 1024], BF16)
        xw = self.sbs(sc, "xw", [128, 8 * 512], BF16)
        qb = self.sbs(sc, "qb", [128, 8 * 512], BF16)
        kbc = self.sbs(sc, "kbc", [128, 8 * 512], BF16)
        kbp = self.sbs(sc, "kbp", [128, 8 * 256], BF16)
        vbc = self.sbs(sc, "vbc", [128, 8 * 512], BF16)
        vbp = self.sbs(sc, "vbp", [128, 4 * 512], BF16)
        pT = self.sbs(sc, "pT", [128, 2 * 4 * 256], BF16)
        ao = self.sbs(sc, "ao", [128, 8 * 256], BF16)
        sq_s = self.sbs(sc, "sq_s", [128, 512], BF16)
        rn_s = self.sbs(sc, "rn_s", [128, 2 * 512], F32)
        qh_s = sq_s
        t1_s = self.sbs(sc, "t1_s", [128, 2 * 512], F32)
        zr = self.sbs(sc, "zr", [1, 512], F32)
        zb = self.sbs(sc, "zb", [128, 2 * 256], F32)
        scol = self.sbs(sc, "scol", [128, 4 * 4], F32)
        rotb = self.sbs(sc, "rotb", [128, 4 * 512], BF16)
        QK8 = 8.0

        for half in range(2):
            key = ("wq", l, half)
            if key not in self.wb_marks:
                self.op("gpsimd", (lambda g, half=half, l=l, wqkv_b=wqkv_b: g.dma_start(
                    out=wqkv_b[:, half * 4 * 3072:(half + 1) * 4 * 3072].rearrange(
                        "p (k f) -> p k f", k=4),
                    in_=self.d["L0_ffn_Wqkv"][l][half * 512:(half + 1) * 512, :].rearrange(
                        "(k p) f -> p k f", p=128))),
                    reads=[self.wq_free] + self.wq_guard, incs=[d_wq])
                ld = d_wq.mark()
                self.op("sync", (lambda s, half=half, l=l, wqkv_b=wqkv_b: s.dma_start(
                    out=self.wqcd[l][half],
                    in_=wqkv_b[:, half * 4 * 3072:(half + 1) * 4 * 3072])),
                    reads=[ld], incs=[self.d_wc])
                self.wb_marks[key] = self.d_wc.mark()
                self.wq_guard = [self.wb_marks[key]]
            else:
                qeng = "gpsimd" if half == 0 else "sync"
                self.op(qeng, (lambda g, half=half, l=l, wqkv_b=wqkv_b: g.dma_start(
                    out=wqkv_b[:, half * 4 * 3072:(half + 1) * 4 * 3072],
                    in_=self.wqcd[l][half])),
                    reads=[self.wq_free, self.wb_marks[key]], incs=[d_wq])
        key = ("wo", l)
        if key not in self.wb_marks:
            self.op("gpsimd", (lambda g, l=l, wo_b=wo_b: g.dma_start(
                out=wo_b[:, :].rearrange("p (k f) -> p k f", k=8),
                in_=self.d["L0_ffn_Wo"][l].rearrange("(k p) f -> p k f", p=128))),
                reads=self.wo_guard, incs=[d_woL])
            ld = d_woL.mark()
            self.op("sync", (lambda s, l=l, wo_b=wo_b: s.dma_start(
                out=self.wocd[l], in_=wo_b[:, :])),
                reads=[ld], incs=[self.d_wc])
            self.wb_marks[key] = self.d_wc.mark()
            self.wo_guard = [self.wb_marks[key]]
        else:
            for hh in range(2):
                qeng = "gpsimd" if hh == 0 else "sync"
                self.op(qeng, (lambda g, hh=hh, l=l, wo_b=wo_b: g.dma_start(
                    out=wo_b[:, hh * 4096:(hh + 1) * 4096],
                    in_=self.wocd[l][:, hh * 4096:(hh + 1) * 4096])),
                    reads=[self.wb_marks[key]] if hh == 0 else [], incs=[d_woL])
        wq_ld = d_wq.mark()
        wo_ld = d_woL.mark()

        pbf = {i: [] for i in range(8)}
        xw_free = None
        sqs_free = [[], []]
        t1_free = [[], []]
        nrm_free = None
        rn_free = [[], []]
        qh_free = [[], []]
        zr_free = []
        zb_free = [[], []]
        pending_norm = None
        gq_prev = []
        pT_free = [[], []]
        rot_free = []
        rot_ld = None
        for wg in range(4):
            for tb in range(4):
                self.op("gpsimd", (lambda g, tb=tb, wg=wg, rotb=rotb: g.dma_start(
                    out=rotb[:, tb * 512:(tb + 1) * 512],
                    in_=self.rot4d[tb][:, wg * 512:(wg + 1) * 512])),
                    reads=(rot_free if tb == 0 else []), incs=[self.d_rot[0]])
            rot_ld = self.d_rot[0].mark()
            if wg == 0:
                xw_done = self.emit_xw_cast(0, xw, r, scol, xw_free)
            else:
                xw_done = self.xw_done_next

            for g in range(2):
                sched = []
                for fc in range(4):
                    sched.append(("q", fc))
                    sched.append(("k", fc))
                    sched.append(("v", fc))
                ei = 0
                for (kind, fc) in sched:
                    if kind == "v":
                        tc = fc
                        bsel = 4 + (tc % 2)
                        ps = self.bank(bsel)
                        for k in range(8):
                            self.op("tensor", (lambda t, g=g, tc=tc, k=k, ps=ps, wqkv_b=wqkv_b,
                                               xw=xw: t.matmul(
                                ps[:, 0:512],
                                xw[:, k * 512 + tc * 128:k * 512 + tc * 128 + 128],
                                wqkv_b[:, k * 3072 + 2048 + g * 512:k * 3072 + 2048 + g * 512 + 512],
                                start=(k == 0), stop=(k == 7))),
                                reads=([xw_done, wq_ld] + pbf[bsel]) if k == 0 else [],
                                incs=[cp] if k == 7 else [])
                        self.op("vector", (lambda v, g=g, tc=tc, ps=ps, vbc=vbc: v.tensor_copy(
                            vbc[:, (g * 4 + tc) * 512:(g * 4 + tc) * 512 + 512], ps[:, 0:512])),
                            reads=[cp.mark()], incs=[cv])
                        pbf[bsel] = [cv.mark()]
                        continue
                    qk = 0 if kind == "q" else 1
                    if True:
                        gp = ei % 2
                        ei += 1
                        c0 = gp * 512
                        bsel = gp
                        b2 = 2 if gp == 0 else 6
                        b3 = 3 if gp == 0 else 7
                        ps = self.bank(bsel)
                        for k in range(8):
                            self.op("tensor", (lambda t, g=g, qk=qk, fc=fc, k=k, ps=ps,
                                               wqkv_b=wqkv_b, xw=xw: t.matmul(
                                ps[:, 0:512],
                                wqkv_b[:, k * 3072 + qk * 1024 + g * 512 + fc * 128:
                                       k * 3072 + qk * 1024 + g * 512 + fc * 128 + 128],
                                xw[:, k * 512:(k + 1) * 512],
                                start=(k == 0), stop=(k == 7))),
                                reads=([xw_done, wq_ld] + pbf[bsel]) if k == 0 else [],
                                incs=[cp] if k == 7 else [])
                        qkp_done = cp.mark()
                        self.op("scalar", (lambda a, ps=ps, sq_s=sq_s: a.activation(
                            sq_s[:, 0:512], ps[:, 0:512], AF.Square)),
                            reads=[qkp_done] + sqs_free[0], incs=[ca])
                        self.op("tensor", (lambda t, sq_s=sq_s, b2=b2: t.matmul(
                            self.bank(b2)[:, 0:512], A["bones"][:, :], sq_s[:, 0:512],
                            start=True, stop=True)),
                            reads=[ca.mark()] + pbf[b2], incs=[cp])
                        self.op("scalar", (lambda a, t1_s=t1_s, c0=c0, b2=b2: a.activation(
                            t1_s[:, c0:c0 + 512], self.bank(b2)[:, 0:512], AF.Ln,
                            bias=A["epsq"][:, 0:1])),
                            reads=[cp.mark()] + t1_free[gp], incs=[ca])
                        pbf[b2] = [ca.mark()]
                        self.op("scalar", (lambda a, t1_s=t1_s, rn_s=rn_s, c0=c0: a.activation(
                            rn_s[:, c0:c0 + 512], t1_s[:, c0:c0 + 512], AF.Exp, scale=-0.5)),
                            reads=rn_free[gp], incs=[ca])
                        nrm_free = ca.mark()
                        scale_col = A["qs_c"] if qk == 0 else A["ks_c"]
                        self.op("vector", (lambda v, ps=ps, rn_s=rn_s, qh_s=qh_s, c0=c0,
                                           scale_col=scale_col, l=l: v.scalar_tensor_tensor(
                            qh_s[:, 0:512], rn_s[:, c0:c0 + 512], scale_col[:, l:l + 1],
                            ps[:, 0:512], op0=ALU.mult, op1=ALU.mult)),
                            reads=[ca.mark()] + qh_free[0], incs=[cv])
                        rn_free[gp] = [cv.mark()]
                        pbf[bsel] = [cv.mark()]
                        self.op("tensor", (lambda t, qh_s=qh_s, b3=b3: t.matmul(
                            self.bank(b3)[:, 0:512], A["pswap"][:, :], qh_s[:, 0:512],
                            start=True, stop=True)),
                            reads=[cv.mark()] + pbf[b3], incs=[cp])
                        swp_done = cp.mark()
                        qh_free[0] = [swp_done]
                        ctab = rotb[:, (0 if qk == 0 else 2) * 512:
                                    (0 if qk == 0 else 2) * 512 + 512]
                        stabt = rotb[:, (1 if qk == 0 else 3) * 512:
                                     (1 if qk == 0 else 3) * 512 + 512]
                        stt_mark = cv.mark()
                        self.op("vector", (lambda v, t1_s=t1_s, stabt=stabt, c0=c0, b3=b3:
                                v.tensor_mul(t1_s[:, c0:c0 + 512], self.bank(b3)[:, 0:512],
                                             stabt)),
                                reads=[swp_done, rot_ld], incs=[cv])
                        t1mul_mark = cv.mark()
                        pbf[b3] = [t1mul_mark]
                        dest = (qb if qk == 0 else kbc)
                        dap = dest[0:128, (g * 4 + fc) * 512:(g * 4 + fc) * 512 + 512]
                        self.op("gpsimd", (lambda gg, dap=dap, qh_s=qh_s, ctab=ctab:
                                gg.tensor_mul(dap, qh_s[:, 0:512], ctab)),
                                reads=[stt_mark, rot_ld] + gq_prev, incs=[cg])
                        sqs_free[0] = [cp.mark(), cg.mark()]
                        self.op("gpsimd", (lambda gg, dap=dap, t1_s=t1_s, c0=c0:
                                gg.tensor_add(dap, dap, t1_s[:, c0:c0 + 512])),
                                reads=[cg.mark(), t1mul_mark], incs=[cg])
                        gq_prev = [cg.mark()]
                        t1_free[gp] = [cg.mark()]
            prod_done = cv.mark()
            prod_g_done = cg.mark()
            if wg == 3:
                self.wq_free = cp.mark()
            xw_free = cp.mark()
            rot_free = [cv.mark(), cg.mark()]

            for wi in range(2):
                w = 2 * wg + wi
                def flush_norm():
                    nonlocal pending_norm, zr_free
                    if pending_norm is None:
                        return
                    fobank, fob, j, z_done, pair_done = pending_norm
                    pending_norm = None
                    zsl = j % 2
                    self.op("scalar", (lambda a, fobank=fobank, zb=zb, zsl=zsl: a.activation(
                        zb[0:1, zsl * 256:(zsl + 1) * 256], fobank[0:1, 256:512],
                        AF.Ln)),
                        reads=[z_done] + zr_free + zb_free[zsl])
                    self.op("scalar", (lambda a, zb=zb, zr=zr, zsl=zsl: a.activation(
                        zr[0:1, 0:256], zb[0:1, zsl * 256:(zsl + 1) * 256],
                        AF.Exp, scale=-1.0)))
                    self.op("scalar", (lambda a, fobank=fobank, zb=zb, zsl=zsl: a.activation(
                        zb[0:1, zsl * 256:(zsl + 1) * 256], fobank[64:65, 256:512],
                        AF.Ln)))
                    self.op("scalar", (lambda a, zb=zb, zr=zr, zsl=zsl: a.activation(
                        zr[0:1, 256:512], zb[0:1, zsl * 256:(zsl + 1) * 256],
                        AF.Exp, scale=-1.0)),
                        incs=[ca])
                    zrec_done = ca.mark()
                    zps = self.bank(7)
                    for hh in range(2):
                        self.op("tensor", (lambda t, zps=zps, zr=zr, hh=hh: t.matmul(
                            zps[hh * 64:(hh + 1) * 64, 0:256], A["ones1f"][0:1, 0:64],
                            zr[0:1, hh * 256:hh * 256 + 256],
                            start=True, stop=True)),
                            reads=([zrec_done] + pbf[7]) if hh == 0 else [],
                            incs=[cp] if hh == 1 else [])
                    zbc_done = cp.mark()
                    zr_free = [zbc_done]
                    self.op("vector", (lambda v, zps=zps, zb=zb, zsl=zsl: v.tensor_copy(
                        zb[:, zsl * 256:(zsl + 1) * 256], zps[0:128, 0:256])),
                        reads=[zbc_done] + zb_free[zsl])
                    self.op("vector", (lambda v, fobank=fobank, zb=zb, zsl=zsl, j=j, ao=ao:
                            v.tensor_mul(ao[:, j * 256:(j + 1) * 256],
                                         fobank[0:128, 0:256],
                                         zb[:, zsl * 256:(zsl + 1) * 256])),
                        reads=[pair_done], incs=[cv])
                    pbf[fob] = [cv.mark()]
                    pbf[7] = [cv.mark()]
                    zb_free[zsl] = [cv.mark()]
                def emit_masks(hh):
                    sset = (1, 2) if hh % 2 == 0 else (3, 4)
                    mss = [2, 3] if w == 0 else [0, 1, 2, 3]
                    for mi, m in enumerate(mss):
                        ps = self.bank(sset[mi // 2])
                        colr = (mi % 2) * 256
                        self.op("tensor", (lambda t, ps=ps, colr=colr, m=m: t.matmul(
                            ps[:, colr:colr + 256], A["identb"][:, :],
                            A["sTm%d" % m][:, :], start=True, stop=False)),
                            reads=([prod_done, prod_g_done] + pbf[sset[0]] + pbf[sset[1]])
                            if mi == 0 else [])
                ms = [2, 3] if w == 0 else [0, 1, 2, 3]
                nm = len(ms)

                def emit_scores(hh):
                    g, hq = hh // 8, hh % 8
                    fc, pb = hq // 2, (hq % 2) * 64
                    sset = (1, 2) if hh % 2 == 0 else (3, 4)
                    psl = (hh % 2) * 1024
                    for pi in range(nm // 2):
                        ps = self.bank(sset[pi])
                        mname = "sTm01" if (nm == 4 and pi == 0) else "sTm23"
                        self.op("tensor", (lambda t, ps=ps, mname=mname: t.matmul(
                            ps[:, 0:512], A["identb"][:, :],
                            A[mname][:, :], start=True, stop=False)),
                            reads=([prod_done, prod_g_done] + pbf[sset[0]] + pbf[sset[1]])
                            if pi == 0 else [])
                        for sub in range(2):
                            mi = pi * 2 + sub
                            m = ms[mi]
                            colr = sub * 256
                            gc = 2 * w - 2 + m
                            if gc >= 4 * wg:
                                seg = gc - 4 * wg
                                kap = kbc[pb:pb + 64, (g * 4 + fc) * 512 + seg * 128:
                                          (g * 4 + fc) * 512 + seg * 128 + 128]
                            else:
                                seg = gc - (4 * wg - 2)
                                kap = kbp[pb:pb + 64, (g * 4 + fc) * 256 + seg * 128:
                                          (g * 4 + fc) * 256 + seg * 128 + 128]
                            qap = qb[pb:pb + 64, (g * 4 + fc) * 512 + wi * 256:
                                     (g * 4 + fc) * 512 + wi * 256 + 256]
                            self.op("tensor", (lambda t, ps=ps, colr=colr, kap=kap, qap=qap:
                                    t.matmul(ps[:, colr:colr + 256], kap, qap,
                                             start=False, stop=(colr == 256))),
                                    incs=[cp] if mi == nm - 1 else [])
                    return cp.mark()

                sT_next = emit_scores(0)
                for h in range(16):
                    if h % 2 == 1:
                        flush_norm()
                    g, hq = h // 8, h % 8
                    fc, pb = hq // 2, (hq % 2) * 64
                    stset = (1, 2) if h % 2 == 0 else (3, 4)
                    psl = (h % 2) * 1024
                    ob = 5 if (h // 2) % 2 == 0 else 6
                    obank = self.bank(ob)
                    sT_done = sT_next
                    if h < 15:
                        sT_next = emit_scores(h + 1)
                    exp_marks = []
                    for pi in range(nm // 2):
                        ps = self.bank(stset[pi])
                        self.op("scalar", (lambda a, ps=ps, pi=pi, pT=pT, psl=psl:
                                a.activation(
                            pT[:, psl + pi * 512:psl + (pi + 1) * 512], ps[:, 0:512],
                            AF.Exp, scale=QK8)),
                            reads=([sT_done] + pT_free[h % 2]) if pi == 0 else [], incs=[ca])
                        exp_marks.append(ca.mark())
                        exp_marks.append(ca.mark())
                    pbf[stset[0]] = [exp_marks[-1]]
                    pbf[stset[1]] = [exp_marks[-1]]
                    for mi, m in enumerate(ms):
                        self.op("tensor", (lambda t, mi=mi, pT=pT, obank=obank, pb=pb,
                                           nm=nm, psl=psl: t.matmul(
                            obank[pb:pb + 1, 256:512], A["onescolb"][:, 0:1],
                            pT[:, psl + mi * 256:psl + (mi + 1) * 256],
                            start=(mi == 0), stop=(mi == nm - 1))),
                            reads=([exp_marks[mi]] + (pbf[ob] if mi == 0 else [])),
                            incs=[cp] if mi == nm - 1 else [])
                    z_done = cp.mark()
                    for mi, m in enumerate(ms):
                        gc = 2 * w - 2 + m
                        if gc >= 4 * wg:
                            seg = gc - 4 * wg
                            vap = vbc[:, (g * 4 + seg) * 512 + hq * 64:
                                      (g * 4 + seg) * 512 + hq * 64 + 64]
                        else:
                            seg = gc - (4 * wg - 2)
                            vap = vbp[:, (g * 2 + seg) * 512 + hq * 64:
                                      (g * 2 + seg) * 512 + hq * 64 + 64]
                        self.op("tensor", (lambda t, vap=vap, mi=mi, pT=pT, obank=obank, pb=pb,
                                           nm=nm, psl=psl: t.matmul(
                            obank[pb:pb + 64, 0:256], vap,
                            pT[:, psl + mi * 256:psl + (mi + 1) * 256],
                            start=(mi == 0), stop=(mi == nm - 1))),
                            incs=[cp] if mi == nm - 1 else [])
                    pT_free[h % 2] = [cp.mark()]
                    if h % 2 == 1:
                        pending_norm = (obank, ob, h // 2, z_done, cp.mark())
                flush_norm()
                ao_done = cv.mark()
                if wi == 1 and wg < 3:
                    for cc in range(4, 8):
                        off = cc * GCOL + 1 + (wg + 1) * 512 - 1
                        self.op("vector", (lambda v, cc=cc, wg=wg, off=off, scol=scol, r=r:
                                v.tensor_copy(scol[:, wg * 4 + (cc - 4):wg * 4 + (cc - 4) + 1],
                                              r[:, off:off + 1])))
                    self.op("vector", lambda v: v.nop(), incs=[cv])
                    self.xw_done_next = self.emit_xw_cast(wg + 1, xw, r, scol, xw_free)
                for oc in range(8):
                    ps = self.bank(oc)
                    for c2 in range(8):
                        self.op("tensor", (lambda t, oc=oc, c2=c2, ps=ps, wo_b=wo_b, ao=ao:
                                t.matmul(ps[:, 0:256],
                                         wo_b[:, c2 * 1024 + oc * 128:c2 * 1024 + oc * 128 + 128],
                                         ao[:, c2 * 256:(c2 + 1) * 256],
                                         start=(c2 == 0), stop=(c2 == 7))),
                                reads=(([ao_done, wo_ld] + sum((pbf[i] for i in range(8)), []))
                                       if (oc == 0 and c2 == 0) else []),
                                incs=[cp] if c2 == 7 else [])
                    self.op("vector", (lambda v, oc=oc, ps=ps, w=w, r=r: v.tensor_add(
                        r[:, oc * GCOL + 1 + w * 256:oc * GCOL + 1 + (w + 1) * 256],
                        ps[:, 0:256],
                        r[:, oc * GCOL + 1 + w * 256:oc * GCOL + 1 + (w + 1) * 256])),
                        reads=[cp.mark()], incs=[cv])
                    pbf[oc] = [cv.mark()]
            if wg < 3:
                self.op("vector", (lambda v, kbc=kbc, kbp=kbp: v.tensor_copy(
                    kbp[:, :].rearrange("p (b c) -> p b c", b=8),
                    kbc[:, :].rearrange("p (b c) -> p b c", b=8)[:, :, 256:512])),
                    reads=[prod_g_done])
                self.op("vector", (lambda v, vbc=vbc, vbp=vbp: v.tensor_copy(
                    vbp[:, :].rearrange("p (gb u c) -> p gb u c", gb=2, u=2),
                    vbc[:, :].rearrange("p (gb u c) -> p gb u c", gb=2, u=4)[:, :, 2:4, :])))
                self.op("vector", lambda v: v.nop(), incs=[cv])


    def layer_ffn(self, p, l):
        A = self.A
        cp, cv, ca = self.cp, self.cv, self.ca
        d_w1, d_w2 = self.d_w1, self.d_w2
        sc = self.scL
        r = self.r
        xn2 = self.sbs(sc, "xn2", [128, 8 * 1024], BF16)
        ag = self.sbs(sc, "ag", [128, 22 * 1024], BF16)
        w1b = self.sbs(sc, "w1b", [128, 4 * 1024], BF16)
        w2b = self.sbs(sc, "w2b", [128, 2 * 1024], BF16)
        gel = self.sbs(sc, "gel", [128, 1024], BF16)
        m_b = self.sbs(sc, "m_b", [128, 1024], F32)
        rstd_b = self.sbs(sc, "rstd_b", [128, 1024], F32)
        t1f = self.sbs(sc, "t1f", [128, 1024], F32)
        t2f = self.sbs(sc, "t2f", [128, 1024], F32)
        sqs = self.sbs(sc, "sqs", [128, 2 * 1024], BF16)

        CHW = [128] * 21 + [DI - 21 * 128]
        w1_free = [None] * 4
        w2_free = [None, None]
        pbf = {i: [] for i in range(8)}
        xn2_free = None
        prev_w2 = None
        gel_free = [None, None]
        for half in (1, 0):
            T0 = half * 1024
            for cc in range(8):
                off = cc * GCOL + 1 + T0 - (1 if cc >= 4 else 0)
                self.op("vector", (lambda v, cc=cc, off=off, xn2=xn2, r=r: v.tensor_copy(
                    xn2[:, cc * 1024:(cc + 1) * 1024], r[:, off:off + 1024])),
                    reads=[xn2_free] if cc == 0 else [])
            self.op("vector", lambda v: v.nop(), incs=[cv])
            rsb_done = cv.mark()
            for cc in range(8):
                for tt in range(2):
                    self.op("tensor", (lambda t, cc=cc, tt=tt, xn2=xn2: t.matmul(
                        self.bank(tt)[:, 0:512], A["onesfull"][:, :],
                        xn2[:, cc * 1024 + tt * 512:cc * 1024 + tt * 512 + 512],
                        start=(cc == 0), stop=(cc == 7))),
                        reads=(([rsb_done] + pbf[tt]) if cc == 0 else []),
                        incs=[cp] if cc == 7 else [])
            sq_free = [None, None]
            for cc in range(8):
                b = cc % 2
                self.op("scalar", (lambda a, cc=cc, b=b, xn2=xn2, sqs=sqs: a.activation(
                    sqs[:, b * 1024:(b + 1) * 1024], xn2[:, cc * 1024:(cc + 1) * 1024],
                    AF.Square)),
                    reads=[rsb_done, sq_free[b]], incs=[ca])
                sq_ca = ca.mark()
                for tt in range(2):
                    self.op("tensor", (lambda t, cc=cc, tt=tt, b=b, sqs=sqs: t.matmul(
                        self.bank(2 + tt)[:, 0:512], A["onesfull"][:, :],
                        sqs[:, b * 1024 + tt * 512:b * 1024 + tt * 512 + 512],
                        start=(cc == 0), stop=(cc == 7))),
                        reads=([sq_ca] if tt == 0 else []) + (pbf[2 + tt] if cc == 0 else []),
                        incs=[cp] if tt == 1 else [])
                sq_free[b] = cp.mark()
            stat_done = cp.mark()
            for tt in range(2):
                self.op("vector", (lambda v, tt=tt, m_b=m_b: v.tensor_scalar(
                    m_b[:, tt * 512:(tt + 1) * 512], self.bank(tt)[:, 0:512],
                    1.0 / D, None, op0=ALU.mult)), reads=[stat_done])
                self.op("vector", (lambda v, tt=tt, t1f=t1f: v.tensor_scalar(
                    t1f[:, tt * 512:(tt + 1) * 512], self.bank(2 + tt)[:, 0:512],
                    1.0 / D, None, op0=ALU.mult)))
                self.op("vector", (lambda v, tt=tt, m_b=m_b, t2f=t2f: v.tensor_mul(
                    t2f[:, tt * 512:(tt + 1) * 512], m_b[:, tt * 512:(tt + 1) * 512],
                    m_b[:, tt * 512:(tt + 1) * 512])))
                self.op("vector", (lambda v, tt=tt, t1f=t1f, t2f=t2f: v.tensor_sub(
                    t1f[:, tt * 512:(tt + 1) * 512], t1f[:, tt * 512:(tt + 1) * 512],
                    t2f[:, tt * 512:(tt + 1) * 512])), incs=[cv])
            self.op("scalar", (lambda a, t1f=t1f, t2f=t2f: a.activation(
                t2f[:, 0:1024], t1f[:, 0:1024], AF.Ln, bias=A["epsc"][:, 0:1])),
                reads=[cv.mark()], incs=[ca])
            self.op("scalar", (lambda a, t2f=t2f, rstd_b=rstd_b: a.activation(
                rstd_b[:, 0:1024], t2f[:, 0:1024], AF.Exp, scale=-0.5)), incs=[ca])
            self.op("vector", lambda v: v.nop(), reads=[ca.mark()], incs=[cv])
            var_done = cv.mark()
            for i in range(4):
                pbf[i] = [var_done]
            for cc in range(8):
                off = cc * GCOL + 1 + T0 - (1 if cc >= 4 else 0)
                self.op("vector", (lambda v, cc=cc, off=off, t1f=t1f, m_b=m_b, r=r: v.tensor_sub(
                    t1f[:, 0:1024], r[:, off:off + 1024], m_b[:, 0:1024])),
                    reads=[var_done] if cc == 0 else [])
                self.op("vector", (lambda v, t1f=t1f, rstd_b=rstd_b: v.tensor_mul(
                    t1f[:, 0:1024], t1f[:, 0:1024], rstd_b[:, 0:1024])))
                self.op("vector", (lambda v, cc=cc, l=l, t1f=t1f, xn2=xn2: v.tensor_scalar(
                    xn2[:, cc * 1024:(cc + 1) * 1024], t1f[:, 0:1024],
                    A["lng_c"][:, l * 8 + cc:l * 8 + cc + 1],
                    A["lnb_c"][:, l * 8 + cc:l * 8 + cc + 1],
                    op0=ALU.mult, op1=ALU.add)), incs=[cv])
            xn2_done = cv.mark()
            first_ag = True
            for j in range(22):
                wda = CHW[j]
                for agx in range(2):
                    slot = (j % 2) * 2 + agx
                    c0 = agx * DI + j * 128
                    jj = j * 2 + agx
                    key = ("w1", l, jj)
                    if key not in self.wb_marks:
                        self.op("gpsimd", (lambda g, l=l, c0=c0, wda=wda, slot=slot, w1b=w1b:
                                g.dma_start(
                            out=w1b[:, slot * 1024:slot * 1024 + 8 * wda].rearrange(
                                "p (k f) -> p k f", k=8),
                            in_=self.d["L0_ffn_W1"][l][:, c0:c0 + wda].rearrange(
                                "(k p) f -> p k f", p=128))),
                            reads=[w1_free[slot]] + self.w1_slot_guard[slot],
                            incs=[d_w1[slot]])
                        ld = d_w1[slot].mark()
                        self.op("sync", (lambda s, l=l, jj=jj, wda=wda, slot=slot, w1b=w1b:
                                s.dma_start(
                            out=self.w1cd[l][:, jj, :, 0:wda],
                            in_=w1b[:, slot * 1024:slot * 1024 + 8 * wda].rearrange(
                                "p (k f) -> p k f", k=8))),
                            reads=[ld], incs=[self.d_wc])
                        self.wb_marks[key] = self.d_wc.mark()
                        self.w1_slot_guard[slot] = [self.wb_marks[key]]
                    else:
                        qeng = "sync" if j % 2 == 0 else "gpsimd"
                        self.op(qeng, (lambda g, l=l, jj=jj, wda=wda, slot=slot, w1b=w1b:
                                g.dma_start(
                            out=w1b[:, slot * 1024:slot * 1024 + 8 * wda].rearrange(
                                "p (k f) -> p k f", k=8),
                            in_=self.w1cd[l][:, jj, :, 0:wda])),
                            reads=[w1_free[slot], self.wb_marks[key]], incs=[d_w1[slot]])
                w1_ld = [d_w1[(j % 2) * 2].mark(), d_w1[(j % 2) * 2 + 1].mark()]
                for agx in range(2):
                    slot = (j % 2) * 2 + agx
                    bs0 = (0 if j % 2 == 0 else 4) + agx * 2
                    for tt in range(2):
                        ps = self.bank(bs0 + tt)
                        for k in range(8):
                            self.op("tensor", (lambda t, j=j, k=k, tt=tt, slot=slot, wda=wda,
                                               ps=ps, w1b=w1b, xn2=xn2: t.matmul(
                                ps[0:wda, 0:512],
                                w1b[:, slot * 1024 + k * wda:slot * 1024 + (k + 1) * wda],
                                xn2[:, k * 1024 + tt * 512:k * 1024 + tt * 512 + 512],
                                start=(k == 0), stop=(k == 7))),
                                reads=(([xn2_done, w1_ld[agx]] + pbf[bs0 + tt] +
                                        ([prev_w2] if first_ag else []))
                                       if k == 0 else []),
                                incs=[cp] if k == 7 else [])
                first_ag = False
                hg_done = cp.mark()
                w1_free[(j % 2) * 2] = hg_done
                w1_free[(j % 2) * 2 + 1] = hg_done
                bsa = (0 if j % 2 == 0 else 4)
                for tt in range(2):
                    psg = self.bank(bsa + 2 + tt)
                    psa = self.bank(bsa + tt)
                    self.op("scalar", (lambda a, j=j, tt=tt, l=l, wda=wda, psg=psg, gel=gel:
                            a.activation(gel[0:wda, tt * 512:(tt + 1) * 512],
                                         psg[0:wda, 0:512], AF.Gelu,
                                         bias=A["b1_c"][0:wda, l * 44 + 22 + j:l * 44 + 23 + j])),
                            reads=[hg_done, gel_free[tt]], incs=[ca])
                    self.op("vector", (lambda v, j=j, tt=tt, l=l, wda=wda, psa=psa, gel=gel,
                                       ag=ag: v.scalar_tensor_tensor(
                        ag[0:wda, j * 1024 + tt * 512:j * 1024 + tt * 512 + 512],
                        psa[0:wda, 0:512],
                        A["b1_c"][0:wda, l * 44 + j:l * 44 + j + 1],
                        gel[0:wda, tt * 512:(tt + 1) * 512],
                        op0=ALU.add, op1=ALU.mult)),
                        reads=[ca.mark()], incs=[cv])
                    pbf[bsa + tt] = [cv.mark()]
                    pbf[bsa + 2 + tt] = [cv.mark()]
                    gel_free[tt] = cv.mark()
            ag_done = cv.mark()
            for tt in range(2):
                for c2 in range(22):
                    wda = CHW[c2]
                    slot = c2 % 2
                    key = ("w2", l, c2)
                    if key not in self.wb_marks:
                        self.op("gpsimd", (lambda g, l=l, c2=c2, wda=wda, slot=slot, w2b=w2b:
                                g.dma_start(
                            out=w2b[0:wda, slot * 1024:slot * 1024 + 1024],
                            in_=self.d["L0_ffn_W2"][l][c2 * 128:c2 * 128 + wda, :])),
                            reads=[w2_free[slot]] + self.w2_slot_guard[slot],
                            incs=[d_w2[slot]])
                        ld = d_w2[slot].mark()
                        self.op("sync", (lambda s, l=l, c2=c2, wda=wda, slot=slot, w2b=w2b:
                                s.dma_start(
                            out=self.w2cd[l][c2][0:wda, :],
                            in_=w2b[0:wda, slot * 1024:slot * 1024 + 1024])),
                            reads=[ld], incs=[self.d_wc])
                        self.wb_marks[key] = self.d_wc.mark()
                        self.w2_slot_guard[slot] = [self.wb_marks[key]]
                    else:
                        qeng = "sync" if c2 % 2 == 0 else "gpsimd"
                        self.op(qeng, (lambda g, l=l, c2=c2, wda=wda, slot=slot, w2b=w2b:
                                g.dma_start(
                            out=w2b[0:wda, slot * 1024:slot * 1024 + 1024],
                            in_=self.w2cd[l][c2][0:wda, :])),
                            reads=[w2_free[slot], self.wb_marks[key]], incs=[d_w2[slot]])
                    w2_ld = d_w2[slot].mark()
                    for oc in range(8):
                        ps = self.bank(oc)
                        self.op("tensor", (lambda t, c2=c2, oc=oc, tt=tt, wda=wda, slot=slot,
                                           ps=ps, w2b=w2b, ag=ag: t.matmul(
                            ps[:, 0:512],
                            w2b[0:wda, slot * 1024 + oc * 128:slot * 1024 + oc * 128 + 128],
                            ag[0:wda, c2 * 1024 + tt * 512:c2 * 1024 + tt * 512 + 512],
                            start=(c2 == 0), stop=(c2 == 21))),
                            reads=(([ag_done, w2_ld] + sum((pbf[i] for i in range(8)), []))
                                   if (c2 == 0 and oc == 0) else
                                   ([w2_ld] if oc == 0 else [])),
                            incs=[cp] if oc == 7 else [])
                    w2_free[slot] = cp.mark()
                w2_done = cp.mark()
                for oc in range(8):
                    ps = self.bank(oc)
                    off = oc * GCOL + 1 + T0 + tt * 512
                    self.op("vector", (lambda v, oc=oc, l=l, off=off, ps=ps, r=r:
                            v.scalar_tensor_tensor(
                        r[:, off:off + 512], ps[:, 0:512],
                        A["b2_c"][:, l * 8 + oc:l * 8 + oc + 1],
                        r[:, off:off + 512], op0=ALU.add, op1=ALU.add)),
                        reads=[w2_done], incs=[cv])
                    pbf[oc] = [cv.mark()]
            xn2_free = cp.mark()
            prev_w2 = cp.mark()

    def g_store(self, p):
        ca = self.ca
        d_g = self.d_g
        r = self.r
        gsc = contextlib.ExitStack()
        gtmp = self.sbs(gsc, "gtmp", [128, 2 * 2048], BF16)
        self.fence("scalar")
        gt_free = [None, None]
        for cc in range(8):
            b = cc % 2
            self.op("scalar", (lambda a, cc=cc, b=b, r=r, gtmp=gtmp: a.activation(
                gtmp[:, b * 2048:(b + 1) * 2048],
                r[:, cc * GCOL + 1:cc * GCOL + 1 + 2048], AF.Silu)),
                reads=[gt_free[b]], incs=[ca])
            self.op("sync", (lambda s, cc=cc, b=b, gtmp=gtmp: s.dma_start(
                out=self.gscrd[cc * 128:(cc + 1) * 128, :],
                in_=gtmp[:, b * 2048:(b + 1) * 2048])),
                reads=[ca.mark()], incs=[d_g[b]])
            gt_free[b] = d_g[b].mark()
        gsc.close()

    def phase_C(self, p):
        A, d = self.A, self.d
        cp, cv, ca = self.cp, self.cv, self.ca
        d_xc, d_gt, d_wb, d_wo = self.d_xc, self.d_gt, self.d_wb, self.d_wo
        d_yr, d_out, d_tap = self.d_yr, self.d_out, self.d_tap
        sc = self.scC
        winb = self.sbs(sc, "winb", [128, 4 * 1024], BF16)
        self.sbs(sc, "lnsq", [128, 1024], BF16)
        woutt_b = self.sbs(sc, "woutt_b", [128, 2 * 1024], BF16)
        xc = self.sbs(sc, "xc", [128, 4 * 1024], F32)
        xntc = self.sbs(sc, "xntc", [128, 4 * 1024], BF16)
        xnfc = self.sbs(sc, "xnfc", [128, 8 * 512], BF16)
        gtok = self.sbs(sc, "gtok", [128, 4 * 1024], BF16)
        xhb = self.sbs(sc, "xhb", [128, 16 * (CW + 4)], BF16)
        xhs = self.sbs(sc, "xhs", [128, 16 * 512], BF16)
        gateb = self.sbs(sc, "gateb", [128, 16 * 512], BF16)
        ubf = self.sbs(sc, "ubf", [NS, 512], BF16)
        diag = self.sbs(sc, "diag", [128, 64 * 128], BF16)
        ylsb = self.sbs(sc, "ylsb", [8, TSC], F32)
        yrow = self.sbs(sc, "yrow", [1, 512], F32)
        m1s = self.sbs(sc, "m1s", [128, 2 * 512], F32)
        t2s = self.sbs(sc, "t2s", [128, 512], F32)
        sgb = self.sbs(sc, "sgb", [128, 2 * 512], BF16)
        outst = self.sbs(sc, "outst", [128, 4 * 1024], F32)
        identb = A["identb"]

        self.op("vector", (lambda v, xhb=xhb: v.memset(
            xhb[:, :].rearrange("p (s c) -> p s c", s=16)[:, :, 0:4], 0.0)))
        self.op("vector", lambda v: v.memset(A["hcur"][0:NS, 0:1], 0.0))
        for f in range(16):
            for k in range(4):
                self.op("vector", (lambda v, f=f, k=k, diag=diag, identb=identb: v.tensor_scalar(
                    diag[:, (f * 4 + k) * 128:(f * 4 + k) * 128 + 128],
                    identb[:, :], A["convw_c"][:, k * 16 + f:k * 16 + f + 1], None,
                    op0=ALU.mult)), incs=[cv] if (f == 15 and k == 3) else [])
        diag_done = cv.mark()

        xc_free = None
        gtok_free = None
        outst_free = None
        woutt_free = [None, None]
        winb_free = [None] * 4
        xz_free = [None, None]
        conv_free = [None, None]
        sg_free = [None, None]
        prev_convmm = None
        prev_oproj = None
        prev_comb = None
        for ch in range(NCHK):
            t0 = ch * CW
            self.op("sync", (lambda s, t0=t0, p=p, xc=xc: s.dma_start(
                out=xc[:, :].rearrange("p (n f) -> p n f", n=4),
                in_=self.d["L0_attn_x"][p, t0:t0 + CW, :].rearrange("(n p) f -> p n f", p=128))),
                reads=[xc_free], incs=[d_xc])
            xld = d_xc.mark()
            for sub in range(4):
                self.op("sync", (lambda s, t0=t0, sub=sub, gtok=gtok: s.dma_start(
                    out=gtok[:, sub * 1024:(sub + 1) * 1024],
                    in_=self.gscrd[:, t0 + sub * 128:t0 + (sub + 1) * 128], transpose=True)),
                    reads=[gtok_free] if sub == 0 else [], incs=[d_gt])
            gld = d_gt.mark()
            xn_subs = []
            for sub in range(4):
                self.ln_tok(xc[:, sub * 1024:(sub + 1) * 1024],
                            xntc[:, sub * 1024:(sub + 1) * 1024], sub % 2,
                            extra_reads=[xld])
                xn_subs.append(cv.mark())
            xn_done = cv.mark()
            pt_free = [prev_comb, prev_comb]
            for sub in range(4):
                psb = self.bankb(4 + sub % 2)
                for cc2 in range(8):
                    self.op("tensor", (lambda t, sub=sub, cc2=cc2, psb=psb, xntc=xntc, identb=identb:
                            t.transpose(
                                psb[:, cc2 * 128:(cc2 + 1) * 128],
                                xntc[:, sub * 1024 + cc2 * 128:sub * 1024 + (cc2 + 1) * 128],
                                identb[:, :])),
                            reads=[xn_subs[sub], pt_free[sub % 2]] if cc2 == 0 else [],
                            incs=[cp] if cc2 == 7 else [])
                def evt(v, sub=sub, psb=psb, xnfc=xnfc):
                    last = None
                    for cc2 in range(8):
                        last = v.tensor_copy(
                            xnfc[:, cc2 * 512 + sub * 128:cc2 * 512 + (sub + 1) * 128],
                            psb[:, cc2 * 128:(cc2 + 1) * 128])
                    return last
                self.op("vector", evt, reads=[cp.mark()], incs=[cv])
                pt_free[sub % 2] = cv.mark()
            xnf_done = cv.mark()
            for f in range(32):
                wb = f % 4
                self.op("gpsimd", (lambda g, f=f, wb=wb, winb=winb: g.dma_start(
                    out=winb[:, wb * 1024:wb * 1024 + 1024].rearrange("p (k f) -> p k f", k=8),
                    in_=self.wintd[:, f * 128:(f + 1) * 128].rearrange("(k p) f -> p k f", p=128))),
                    reads=[winb_free[wb]], incs=[d_wb[wb]])
                wld = d_wb[wb].mark()
                bsel = f % 2
                ps = self.bank(bsel)
                for k in range(8):
                    self.op("tensor", (lambda t, f=f, k=k, ps=ps, wb=wb, winb=winb, xnfc=xnfc:
                            t.matmul(
                                ps[:, 0:512], winb[:, wb * 1024 + k * 128:wb * 1024 + k * 128 + 128],
                                xnfc[:, k * 512:(k + 1) * 512],
                                start=(k == 0), stop=(k == 7))),
                            reads=[xnf_done, wld, xz_free[bsel]] if k == 0 else [],
                            incs=[cp] if k == 7 else [])
                winb_free[wb] = cp.mark()
                if f < 16:
                    self.op("vector", (lambda v, f=f, ps=ps, xhb=xhb: v.tensor_scalar(
                        xhb[:, f * (CW + 4) + 4:f * (CW + 4) + 4 + CW],
                        ps[:, 0:512], A["bin_c"][:, f:f + 1], None, op0=ALU.add)),
                        reads=[cp.mark(), prev_convmm], incs=[cv])
                else:
                    self.op("vector", (lambda v, f=f, ps=ps, gateb=gateb: v.tensor_scalar(
                        gateb[:, (f - 16) * 512:(f - 16) * 512 + 512],
                        ps[:, 0:512], A["bin_c"][:, f:f + 1], None, op0=ALU.add)),
                        reads=[cp.mark()], incs=[cv])
                xz_free[bsel] = cv.mark()
            xz_done = cv.mark()
            for f in range(16):
                bsel = 2 + f % 2
                ps = self.bank(bsel)
                for k in range(4):
                    self.op("tensor", (lambda t, f=f, k=k, ps=ps, diag=diag, xhb=xhb: t.matmul(
                        ps[:, 0:512], diag[:, (f * 4 + k) * 128:(f * 4 + k) * 128 + 128],
                        xhb[:, f * (CW + 4) + 1 + k:f * (CW + 4) + 1 + k + CW],
                        start=(k == 0), stop=(k == 3))),
                        reads=([xz_done, diag_done, conv_free[f % 2]] if k == 0 else []),
                        incs=[cp] if k == 3 else [])
                self.op("scalar", (lambda a, f=f, ps=ps, xhs=xhs: a.activation(
                    xhs[:, f * 512:(f + 1) * 512], ps[:, 0:512], AF.Silu,
                    bias=A["convb_c"][:, f:f + 1])),
                    reads=[cp.mark(), prev_oproj], incs=[ca])
                conv_free[f % 2] = ca.mark()
            conv_done = ca.mark()
            prev_convmm = cp.mark()
            if ch < NCHK - 1:
                self.op("vector", (lambda v, xhb=xhb: v.tensor_copy(
                    xhb[:, :].rearrange("p (s c) -> p s c", s=16)[:, :, 0:4],
                    xhb[:, :].rearrange("p (s c) -> p s c", s=16)[:, :, CW:CW + 4])),
                    reads=[prev_convmm])
            pu = self.bank(6)
            for k in range(16):
                self.op("tensor", (lambda t, k=k, pu=pu, xhs=xhs: t.matmul(
                    pu[0:NS, 0:512], A["wssmt"][:, k * NS:(k + 1) * NS],
                    xhs[:, k * 512:(k + 1) * 512], start=(k == 0), stop=(k == 15))),
                    reads=[conv_done, cv.mark()] if k == 0 else [],
                    incs=[cp] if k == 15 else [])
            self.op("vector", (lambda v, pu=pu, ubf=ubf: v.tensor_scalar(
                ubf[0:NS, 0:512], pu[0:NS, 0:512], A["bssm_c"][0:NS, 0:1], None,
                op0=ALU.add)), reads=[cp.mark()], incs=[cv])
            u_done = cv.mark()
            ph = self.bank(6)
            for j in range(TSC):
                self.op("tensor", (lambda t, j=j, ph=ph, ubf=ubf: t.matmul(
                    ph[0:NS, 0:8],
                    A["pbext"][0:NS, (TSC - 1 - j) * NS:(TSC - j) * NS],
                    ubf[0:NS, j:512:TSC], start=(j == 0), stop=(j == TSC - 1))),
                    reads=[u_done] if j == 0 else [],
                    incs=[cp] if j == TSC - 1 else [])
            self.op("vector", (lambda v, ph=ph: v.tensor_copy(A["hdsb"][0:NS, 0:8], ph[0:NS, 0:8])),
                    reads=[cp.mark()])
            self.op("vector", (lambda v: v.tensor_copy(A["Hsb"][0:NS, 0:1], A["hcur"][0:NS, 0:1])),
                    incs=[cv])
            hd_done = cv.mark()
            py = self.bank(7)
            for j in range(TSC):
                if j < 8:
                    self.op("tensor", (lambda t, j=j, ph=ph: t.matmul(
                        ph[0:NS, 8 + (j % 2) * 2:9 + (j % 2) * 2], A["q64f"][0:NS, 0:NS],
                        A["Hsb"][0:NS, j:j + 1], start=True, stop=False)),
                        reads=[hd_done] if j == 0 else [cv.mark()])
                    self.op("tensor", (lambda t, j=j, ph=ph: t.matmul(
                        ph[0:NS, 8 + (j % 2) * 2:9 + (j % 2) * 2], A["i32f"][0:NS, 0:NS],
                        A["hdsb"][0:NS, j:j + 1], start=False, stop=True)),
                        incs=[cp])
                    self.op("vector", (lambda v, j=j, ph=ph: v.tensor_copy(
                        A["Hsb"][0:NS, j + 1:j + 2], ph[0:NS, 8 + (j % 2) * 2:9 + (j % 2) * 2])),
                        reads=[cp.mark()], incs=[cv])
                self.op("tensor", (lambda t, j=j, py=py, ubf=ubf: t.matmul(
                    py[0:8, 0:TSC], ubf[0:NS, j:512:TSC],
                    A["k0ext"][0:NS, TSC - j:2 * TSC - j], start=(j == 0), stop=False)),
                    reads=[u_done, cv.mark()] if j == 0 else [])
            self.op("vector", (lambda v: v.tensor_copy(A["HBb"][0:NS, 0:8], A["Hsb"][0:NS, 0:8])),
                    reads=[cv.mark()], incs=[cv])
            self.op("vector", (lambda v: v.tensor_copy(A["hcur"][0:NS, 0:1], A["Hsb"][0:NS, 8:9])))
            self.op("tensor", (lambda t, py=py: t.matmul(
                py[0:8, 0:TSC], A["HBb"][0:NS, 0:8], A["w2t"][0:NS, 0:TSC],
                start=False, stop=True)), reads=[cv.mark()], incs=[cp])
            self.op("vector", (lambda v, py=py, ylsb=ylsb: v.tensor_copy(
                ylsb[0:8, 0:TSC], py[0:8, 0:TSC])),
                reads=[cp.mark()], incs=[cv])
            self.op("sync", (lambda s, ylsb=ylsb, yrow=yrow: s.dma_start(
                out=yrow[0:1, 0:512].rearrange("o (c t) -> o c t", c=8),
                in_=ylsb[0:8, 0:TSC])),
                reads=[cv.mark()], incs=[d_yr])
            pyb = self.bank(2)
            self.op("tensor", (lambda t, pyb=pyb, yrow=yrow: t.matmul(
                pyb[:, 0:512], A["ones1f"][0:1, 0:128], yrow[0:1, 0:512],
                start=True, stop=True)),
                reads=[d_yr.mark(), conv_free[0], conv_free[1]], incs=[cp])
            ybc_done = cp.mark()
            for f in range(16):
                b = f % 2
                self.op("scalar", (lambda a, f=f, b=b, sgb=sgb, gateb=gateb: a.activation(
                    sgb[:, b * 512:(b + 1) * 512], gateb[:, f * 512:(f + 1) * 512],
                    AF.Silu)),
                    reads=[sg_free[b]], incs=[ca])
                self.op("vector", (lambda v, f=f, pyb=pyb, m1s=m1s: v.tensor_scalar(
                    m1s[:, 0:512], pyb[:, 0:512], A["dv_c"][:, f:f + 1], None,
                    op0=ALU.add)), reads=[ybc_done])
                self.op("vector", (lambda v, f=f, m1s=m1s, t2s=t2s, xhs=xhs: v.tensor_mul(
                    t2s[:, 0:512], m1s[:, 0:512], xhs[:, f * 512:(f + 1) * 512])))
                self.op("vector", (lambda v, f=f, b=b, t2s=t2s, xhs=xhs, sgb=sgb: v.tensor_mul(
                    xhs[:, f * 512:(f + 1) * 512], t2s[:, 0:512],
                    sgb[:, b * 512:(b + 1) * 512])),
                    reads=[ca.mark()], incs=[cv])
                sg_free[b] = cv.mark()
            yfin_done = cv.mark()
            for sub in range(4):
                for half in range(2):
                    bsel = sub * 2 + half
                    ps = self.bank(bsel)
                    self.op("tensor", (lambda t, half=half, ps=ps: t.matmul(
                        ps[:, 0:512], A["ones1b"][0:1, :],
                        A["bout_rowb"][0:1, half * 512:(half + 1) * 512],
                        start=True, stop=False)),
                        reads=([yfin_done, xz_free[0], xz_free[1]]
                               if (sub == 0 and half == 0) else []))
            for hk in range(16):
                b = hk % 2
                self.op("gpsimd", (lambda g, hk=hk, b=b, woutt_b=woutt_b: g.dma_start(
                    out=woutt_b[:, b * 1024:(b + 1) * 1024],
                    in_=self.wouttd[hk * 128:(hk + 1) * 128, :])),
                    reads=[woutt_free[b]], incs=[d_wo[b]])
                woutt_ld = d_wo[b].mark()
                for sub in range(4):
                    for half in range(2):
                        bsel = sub * 2 + half
                        ps = self.bank(bsel)
                        first = (hk == 0 and sub == 0 and half == 0)
                        self.op("tensor", (lambda t, hk=hk, b=b, sub=sub, half=half, ps=ps,
                                           xhs=xhs, woutt_b=woutt_b:
                                t.matmul(ps[:, 0:512],
                                         xhs[:, hk * 512 + sub * 128:hk * 512 + (sub + 1) * 128],
                                         woutt_b[:, b * 1024 + half * 512:b * 1024 + (half + 1) * 512],
                                         start=False, stop=(hk == 15))),
                                reads=([yfin_done, woutt_ld, xz_free[0], xz_free[1]]
                                       if first else ([woutt_ld] if (sub == 0 and half == 0) else [])),
                                incs=[cp] if (sub == 3 and half == 1) else [])
                woutt_free[b] = cp.mark()
            oproj_done = cp.mark()
            prev_oproj = oproj_done
            for sub in range(4):
                for half in range(2):
                    bsel = sub * 2 + half
                    ps = self.bank(bsel)
                    self.op("vector", (lambda v, sub=sub, half=half, ps=ps, m1s=m1s, xntc=xntc:
                            v.tensor_add(
                        m1s[:, 512:1024], ps[:, 0:512],
                        xntc[:, sub * 1024 + half * 512:sub * 1024 + (half + 1) * 512])),
                        reads=[oproj_done, gld, outst_free] if (sub == 0 and half == 0) else [])
                    self.op("vector", (lambda v, sub=sub, half=half, t2s=t2s, m1s=m1s, gtok=gtok:
                            v.tensor_mul(
                                t2s[:, 0:512], m1s[:, 512:1024],
                                gtok[:, sub * 1024 + half * 512:sub * 1024 + (half + 1) * 512])))
                    self.op("vector", (lambda v, sub=sub, half=half, outst=outst, t2s=t2s, xc=xc:
                            v.tensor_add(
                                outst[:, sub * 1024 + half * 512:sub * 1024 + (half + 1) * 512],
                                t2s[:, 0:512],
                                xc[:, sub * 1024 + half * 512:sub * 1024 + (half + 1) * 512])),
                            incs=[cv] if half == 1 else [])
            comb_done = cv.mark()
            prev_comb = comb_done
            xz_free = [comb_done, comb_done]
            conv_free = [comb_done, comb_done]
            self.op("sync", (lambda s, t0=t0, p=p, outst=outst: s.dma_start(
                out=self.outd[p, t0:t0 + CW, :].rearrange("(n p) f -> p n f", p=128),
                in_=outst[:, :].rearrange("p (n f) -> p n f", n=4))),
                reads=[comb_done], incs=[d_out])
            outst_free = d_out.mark()
            xc_free = comb_done
            gtok_free = comb_done

        tapsrc = {"xnf": xnfc, "xh": xhb, "gate": gateb, "u": ubf, "yrowt": yrow, "yfin": xhs}
        for nm, hnd in tapsrc.items():
            if nm in self.taps:
                sh = list(hnd.shape)
                dt = hnd.ap().dtype
                tpp = self.mktap(nm, [self.npass] + sh, dt)
                n = sh[0]
                self.op("sync", (lambda s, tpp=tpp, hnd=hnd, n=n, p=p: s.dma_start(
                    out=tpp[p][0:n, :], in_=hnd[0:n, :])),
                    reads=[cv.mark(), ca.mark()], incs=[d_tap])



_BUILD_CACHE = {}


def get_build(npass=4, depth=6, taps=()):
    key = (npass, depth, tuple(sorted(taps)))
    if key not in _BUILD_CACHE:
        _BUILD_CACHE[key] = KB(npass, depth, taps).build()
    return _BUILD_CACHE[key]


def kernel(**inputs):
    kb = get_build(4, 6)
    x = np.ascontiguousarray(inputs["L0_attn_x"])
    in_maps = []
    for c in range(8):
        m = {k: np.ascontiguousarray(v) for k, v in inputs.items() if k != "L0_attn_x"}
        m["L0_attn_x"] = np.ascontiguousarray(x[c * 4:(c + 1) * 4])
        in_maps.append(m)
    res = run_bass_kernel_spmd(kb.nc, in_maps, core_ids=list(range(8)))
    return np.concatenate([r["out"] for r in res.results], axis=0)


# version 22
# speedup vs baseline: 1.0197x; 1.0197x over previous
import contextlib
import numpy as np
import ml_dtypes
import concourse.bass as bass
import concourse.mybir as mybir
from concourse.bass_utils import run_bass_kernel_spmd

F32 = mybir.dt.float32
BF16 = mybir.dt.bfloat16
AF = mybir.ActivationFunctionType
ALU = mybir.AluOpType
AX = mybir.AxisListType

S = 2048
D = 1024
H2 = 2048
NS = 32
TSC = 64
CW = 512
NCHK = S // CW
WIN = 256
HEADS = 16
DH = 64
DI = 2730
MSK = -1.0e30
GCOL = 2049

ENGINES = ("tensor", "vector", "scalar", "gpsimd", "sync")


class Chan:
    __slots__ = ("name", "sem", "count", "step")

    def __init__(self, name, sem, step):
        self.name = name
        self.sem = sem
        self.count = 0
        self.step = step

    def mark(self):
        return (self, self.count)


class KB:

    def __init__(self, npass=4, depth=6, taps=()):
        self.npass = npass
        self.depth = depth
        self.taps = set(taps)
        self.nc = bass.Bass("TRN2", target_bir_lowering=False)
        self.st = contextlib.ExitStack()
        self.ops = {e: [] for e in ENGINES}
        self._waited = {e: {} for e in ENGINES}
        self.chans = []
        self.A = {}
        self.tap_d = {}

    def chan(self, name, step=1):
        sem = self.st.enter_context(self.nc.semaphore(name))
        c = Chan(name, sem, step)
        self.chans.append(c)
        return c

    def op(self, engine, fn, reads=(), incs=()):
        waits = []
        w = self._waited[engine]
        for rd in reads:
            if rd is None:
                continue
            c, cnt = rd
            if c is None or cnt <= 0:
                continue
            if w.get(c.name, 0) < cnt:
                w[c.name] = cnt
                waits.append((c, cnt))
        incl = []
        selfw = None
        auto = {"vector": self.cv, "scalar": self.ca}.get(engine)
        if auto is not None:
            selfw = (auto, auto.count)
            auto.count += auto.step
            incl.append((auto, auto.step))
            for c in incs:
                if c is auto:
                    continue
                c.count += c.step
                incl.append((c, c.step))
        else:
            for c in incs:
                c.count += c.step
                incl.append((c, c.step))
        self.ops[engine].append((fn, waits, incl, selfw))

    def fence(self, engine):
        rd = [(c, c.count) for c in self.chans if c.count > 0]
        if rd:
            self.op(engine, lambda e: None, reads=rd)

    def bar(self, engines=ENGINES):
        rd = [(c, c.count) for c in self.chans if c.count > 0]
        for e in engines:
            self.op(e, lambda eng: None, reads=rd)

    def emit(self):
        for c in self.chans:
            assert c.count < 60000, f"chan {c.name} count {c.count} too high"
        with self.nc.Block() as block:
            def run(eng_obj, name):
                for (fn, waits, incl, selfw) in self.ops[name]:
                    if selfw is not None and selfw[1] > 0:
                        eng_obj.wait_ge(selfw[0].sem, selfw[1])
                    for (c, cnt) in waits:
                        eng_obj.wait_ge(c.sem, cnt)
                    ins = fn(eng_obj)
                    if ins is None and incl:
                        ins = eng_obj.nop()
                    for i, (c, stp) in enumerate(incl):
                        if i == 0:
                            ins.then_inc(c.sem, stp)
                        else:
                            eng_obj.nop().then_inc(c.sem, stp)
            block.tensor(lambda t: run(t, "tensor"))
            block.vector(lambda v: run(v, "vector"))
            block.scalar(lambda a: run(a, "scalar"))
            block.gpsimd(lambda g: run(g, "gpsimd"))
            block.sync(lambda s: run(s, "sync"))

    def sb(self, name, shape, dtype):
        t = self.nc.alloc_sbuf_tensor(name, shape, dtype)
        self.A[name] = t
        return t

    _uid = 0

    def sbs(self, scope, name, shape, dtype, side=None):
        KB._uid += 1
        t = scope.enter_context(self.nc.sbuf_tensor(f"{name}__{KB._uid}", shape, dtype, side=side))
        self.A[name] = t
        return t

    def bank(self, i):
        return self.PB[i].ap()

    def bankb(self, i):
        return self.PB[i].ap().bitcast(BF16)

    def mktap(self, name, shape, dtype=F32):
        if name in self.taps and name not in self.tap_d:
            self.tap_d[name] = self.nc.dram_tensor("tap_" + name, shape, dtype,
                                                   kind="ExternalOutput").ap()
        return self.tap_d.get(name)

    def build(self):
        nc = self.nc
        d = {}
        specs = [
            ("L0_attn_x", [4, S, D]), ("L0_attn_W_in", [2 * H2, D]), ("L0_attn_b_in", [2 * H2]),
            ("L0_attn_conv_w", [H2, 1, 4]), ("L0_attn_conv_b", [H2]),
            ("L0_attn_W_ssm", [NS, H2]), ("L0_attn_b_ssm", [NS]), ("L0_attn_A", [NS, NS]),
            ("L0_attn_Bv", [NS]), ("L0_attn_Cv", [NS]), ("L0_attn_Dv", [H2]),
            ("L0_attn_W_out", [D, H2]), ("L0_attn_b_out", [D]),
            ("L0_ffn_Wqkv", [6, D, 3 * D]), ("L0_ffn_q_scale", [6, DH]), ("L0_ffn_k_scale", [6, DH]),
            ("L0_ffn_Wo", [6, D, D]), ("L0_ffn_ln_g", [6, D]), ("L0_ffn_ln_b", [6, D]),
            ("L0_ffn_W1", [6, D, 2 * DI]), ("L0_ffn_b1", [6, 2 * DI]),
            ("L0_ffn_W2", [6, DI, D]), ("L0_ffn_b2", [6, D]),
        ]
        for (n, sh) in specs:
            d[n] = nc.dram_tensor(n, sh, F32, kind="ExternalInput").ap()
        self.d = d
        self.outd = nc.dram_tensor("out", [4, S, D], F32, kind="ExternalOutput").ap()
        self.wintd = nc.dram_tensor("wint_scr", [D, 2 * H2], BF16, kind="Internal").ap()
        self.wouttd = nc.dram_tensor("woutt_scr", [H2, D], BF16, kind="Internal").ap()
        self.gscrd = nc.dram_tensor("g_scr", [D, S], BF16, kind="Internal").ap()
        self.w1cd = nc.dram_tensor("w1_cache", [6, 128, 44, 8, 128], BF16, kind="Internal").ap()
        self.w2cd = nc.dram_tensor("w2_cache", [6, 22, 128, 1024], BF16, kind="Internal").ap()
        self.wqcd = nc.dram_tensor("wq_cache", [6, 2, 128, 4 * 3072], BF16, kind="Internal").ap()
        self.wocd = nc.dram_tensor("wo_cache", [6, 128, 8 * 1024], BF16, kind="Internal").ap()
        self.wb_marks = {}
        self.w1_slot_guard = {i: [] for i in range(6)}
        self.w2_slot_guard = {i: [] for i in range(2)}
        self.wq_guard = []
        self.wo_guard = []

        self.PB = [self.nc.alloc_psum_tensor(f"pbank{i}", [128, 512], F32) for i in range(8)]

        self.cp = self.chan("cp")
        self.cv = self.chan("cv")
        self.ca = self.chan("ca")
        self.cg = self.chan("cg")
        self.d_cst = self.chan("d_cst", 16)
        self.d_row = self.chan("d_row", 16)
        self.d_wst = self.chan("d_wst", 16)
        self.d_tst = self.chan("d_tst", 16)
        self.d_af = self.chan("d_af", 16)
        self.d_xa = [self.chan("d_xa0", 16), self.chan("d_xa1", 16)]
        self.d_g = [self.chan("d_g0", 16), self.chan("d_g1", 16)]
        self.d_xc = self.chan("d_xc", 16)
        self.d_gt = self.chan("d_gt", 16)
        self.d_wb = [self.chan(f"d_wb{i}", 16) for i in range(4)]
        self.d_wc = self.chan("d_wc", 16)
        self.d_wo = [self.chan("d_wo0", 16), self.chan("d_wo1", 16)]
        self.d_yr = self.chan("d_yr", 16)
        self.d_out = self.chan("d_out", 16)
        self.d_tap = self.chan("d_tap", 16)
        self.d_rot = [self.chan("d_rot0", 16), self.chan("d_rot1", 16)]
        self.d_wq = self.chan("d_wq", 16)
        self.d_woL = self.chan("d_woL", 16)
        self.d_br = [self.chan("d_br0", 16), self.chan("d_br1", 16)]
        self.d_w1 = [self.chan(f"d_w1{i}", 16) for i in range(6)]
        self.d_w2 = [self.chan("d_w20", 16), self.chan("d_w21", 16)]
        self.store_chans = [self.d_tst, self.d_g[0], self.d_g[1], self.d_out, self.d_tap]
        self.ticks = {e: self.chan("tk_" + e) for e in ENGINES}

        self.consts_setup()
        self.globals_alloc()
        with contextlib.ExitStack() as sc0:
            self.phase0(sc0)
        self.bar()
        for p in range(self.npass):
            self.cp = self.chan(f"cp{p}")
            self.cv = self.chan(f"cv{p}")
            self.ca = self.chan(f"ca{p}")
            self.cg = self.chan(f"cg{p}")
            with contextlib.ExitStack() as scAB:
                self.scAB = scAB
                self.wq_free = None
                self.phase_A(p)
                self.bar()
                for l in range(self.depth):
                    with contextlib.ExitStack() as scL:
                        self.scL = scL
                        self.layer_attn(p, l)
                    self.bar()
                    with contextlib.ExitStack() as scL:
                        self.scL = scL
                        self.layer_ffn(p, l)
                    self.bar()
                self.g_store(p)
                self.bar()
            with contextlib.ExitStack() as scC:
                self.scC = scC
                self.phase_C(p)
                self.bar()
        self.op("sync", lambda s: None, reads=[c.mark() for c in self.store_chans])
        self.emit()
        self.st.close()
        return self

    def consts_setup(self):
        half = np.arange(0, DH, 2, dtype=np.float64)
        inv_freq = 1.0 / 10000.0 ** (half / DH)
        t = np.arange(S, dtype=np.float64)
        freqs = np.concatenate([t[:, None] * inv_freq[None, :]] * 2, -1)
        base_scale = (half + 0.4 * DH) / (1.4 * DH)
        power = (t - S // 2) / (WIN // 2)
        scale = np.concatenate([base_scale[None, :] ** power[:, None]] * 2, -1)
        cos, sin = np.cos(freqs), np.sin(freqs)
        sgn = np.where(np.arange(DH) < DH // 2, -1.0, 1.0)[None, :]
        def dup(tab):
            return np.ascontiguousarray(np.concatenate([tab, tab], 0).astype(np.float32))
        CS = dup((cos * scale).T)
        SS = dup((sin * scale * sgn).T)
        CI = dup((cos / scale).T)
        SI = dup((sin / scale * sgn).T)
        kp = np.arange(128)[:, None]
        qf = np.arange(256)[None, :]
        sTm = []
        for m in range(4):
            rel = m * 128 + kp - 256
            valid = (qf - rel >= 0) & (qf - rel <= WIN)
            sTm.append(np.where(valid, 0.0, MSK).astype(ml_dtypes.bfloat16))
        qp = np.arange(128)[:, None]
        s1m = []
        for qc in range(2):
            cols = []
            for mi in range(3):
                m = qc + mi
                kpp = np.arange(128)[None, :]
                rel = m * 128 + kpp - 256
                qg = qc * 128 + qp
                valid = (qg - rel >= 0) & (qg - rel <= WIN)
                cols.append(np.where(valid, 0.0, MSK))
            s1m.append(np.concatenate(cols, 1).astype(ml_dtypes.bfloat16))
        pswap = np.zeros((128, 128), dtype=ml_dtypes.bfloat16)
        for dd in range(128):
            blk = (dd // 64) * 64
            pswap[blk + ((dd - blk + 32) % 64), dd] = 1.0
        bones = np.zeros((128, 128), dtype=ml_dtypes.bfloat16)
        bones[0:64, 0:64] = 1.0
        bones[64:128, 64:128] = 1.0
        onesfull = np.ones((128, 128), dtype=ml_dtypes.bfloat16)
        self.consts = {
            "identb": np.eye(128, dtype=ml_dtypes.bfloat16),
            "onesfull": onesfull,
            "bones": bones, "pswap": pswap,
            "onescolb": np.ones((128, 1), dtype=ml_dtypes.bfloat16),
            "ones1b": np.ones((1, 128), dtype=ml_dtypes.bfloat16),
            "ones1f": np.ones((1, 128), dtype=np.float32),
            "one11f": np.ones((1, 1), dtype=np.float32),
            "i32f": np.eye(NS, dtype=np.float32),
            "sTm01": np.ascontiguousarray(np.concatenate([sTm[0], sTm[1]], 1)),
            "sTm23": np.ascontiguousarray(np.concatenate([sTm[2], sTm[3]], 1)),
        }
        self.cdram = {k: self.nc.inline_tensor(np.ascontiguousarray(v), name="c_" + k)
                      for k, v in self.consts.items()}
        rot4 = np.concatenate([t.astype(ml_dtypes.bfloat16)[None] for t in (CS, SS, CI, SI)], 0)
        self.rot4d = self.nc.inline_tensor(np.ascontiguousarray(rot4), name="c_rot4").ap()

    def globals_alloc(self):
        for k, v in self.consts.items():
            dt = BF16 if v.dtype == ml_dtypes.bfloat16 else F32
            self.sb(k, list(v.shape), dt)
        for (n, sh) in [("bin_c", [128, 32]), ("convw_c", [128, 64]), ("convb_c", [128, 16]),
                        ("dv_c", [128, 16]), ("bssm_c", [NS, 1]), ("bv_c", [NS, 1]),
                        ("cv_c", [NS, 1]), ("lng_c", [128, 48]),
                        ("lnb_c", [128, 48]), ("b1_c", [128, 264]), ("b2_c", [128, 48]),
                        ("qs_c", [128, 6]), ("ks_c", [128, 6]), ("q64f", [NS, NS]),
                        ("a_f", [NS, NS]), ("wchain", [NS, TSC + 1]), ("hcur", [NS, 1]),
                        ("hdsb", [NS, 8]), ("Hsb", [NS, 9]), ("lncol", [128, 16]), ("epsc", [128, 1]), ("epsq", [128, 1])]:
            self.sb(n, sh, F32)
        self.sb("bout_rowb", [1, D], BF16)
        for (n, sh) in [("wssmt", [128, 16 * NS]), ("k0ext", [NS, 2 * TSC]),
                        ("pbext", [NS, TSC * NS]), ("w2t", [NS, TSC]), ("HBb", [NS, 8])]:
            self.sb(n, sh, BF16)

    def phase0(self, sc0):
        A, d = self.A, self.d
        cp, cv, ca = self.cp, self.cv, self.ca
        d_cst, d_row, d_wst, d_tst = self.d_cst, self.d_row, self.d_wst, self.d_tst
        self.op("vector", lambda v: v.memset(A["epsc"][:, 0:1], 1e-5))
        self.op("vector", lambda v: v.memset(A["epsq"][:, 0:1], 1e-24))
        for k in self.consts:
            src = self.cdram[k].ap()
            npart = self.consts[k].shape[0]
            self.op("gpsimd", (lambda g, a=A[k], s=src, n=npart: g.dma_start(out=a[0:n, :], in_=s)),
                    incs=[d_cst])
        const_ld = d_cst.mark()
        rb = self.sbs(sc0, "rowbuf", [1, 8192], F32)
        qtmp = self.sbs(sc0, "qtmp", [NS, 2 * NS], F32)

        def load_row(dst, src):
            self.op("sync", (lambda s, dd=dst, ii=src: s.dma_start(out=dd, in_=ii)),
                    reads=[cp.mark()], incs=[d_row])

        def colz(dst_fn, srcs, psel=0):
            ps = self.bank(psel)
            for j, src in enumerate(srcs):
                w = src.shape[-1]
                self.op("tensor", (lambda t, ap=src, w=w, j=j, ps=ps: t.matmul(
                    ps[0:w, j:j + 1], ap, A["one11f"][0:1, 0:1], start=True, stop=True)),
                    reads=[d_row.mark(), cv.mark(), const_ld] if j == 0 else [],
                    incs=[cp] if j == len(srcs) - 1 else [])
            n = len(srcs)
            widths = [s.shape[-1] for s in srcs]
            def ev(v, n=n, ps=ps, widths=widths):
                last = None
                for j in range(n):
                    dst = dst_fn(j)
                    last = v.tensor_copy(dst, ps[0:widths[j], j:j + 1])
                return last
            self.op("vector", ev, reads=[cp.mark()], incs=[cv])

        load_row(rb[0:1, 0:4096], d["L0_attn_b_in"].rearrange("(o f) -> o f", o=1))
        colz(lambda j: A["bin_c"][:, j:j + 1], [rb[0:1, j * 128:(j + 1) * 128] for j in range(32)])
        load_row(rb[0:1, 0:8192], d["L0_attn_conv_w"].rearrange("c o k -> o (c k)"))
        for k in range(4):
            colz((lambda j, k=k: A["convw_c"][:, k * 16 + j:k * 16 + j + 1]),
                 [rb[0:1, (j * 512 + k):(j * 512 + k) + 509:4] for j in range(16)], psel=k % 2)
        load_row(rb[0:1, 0:2048], d["L0_attn_conv_b"].rearrange("(o f) -> o f", o=1))
        colz(lambda j: A["convb_c"][:, j:j + 1], [rb[0:1, j * 128:(j + 1) * 128] for j in range(16)])
        load_row(rb[0:1, 0:2048], d["L0_attn_Dv"].rearrange("(o f) -> o f", o=1))
        colz(lambda j: A["dv_c"][:, j:j + 1], [rb[0:1, j * 128:(j + 1) * 128] for j in range(16)], psel=1)
        load_row(rb[0:1, 0:32], d["L0_attn_b_ssm"].rearrange("(o f) -> o f", o=1))
        load_row(rb[0:1, 32:64], d["L0_attn_Bv"].rearrange("(o f) -> o f", o=1))
        load_row(rb[0:1, 64:96], d["L0_attn_Cv"].rearrange("(o f) -> o f", o=1))
        colz(lambda j: [A["bssm_c"], A["bv_c"], A["cv_c"]][j][0:NS, 0:1],
             [rb[0:1, 0:NS], rb[0:1, 32:64], rb[0:1, 64:96]])
        load_row(rb[0:1, 0:1024], d["L0_attn_b_out"].rearrange("(o f) -> o f", o=1))
        self.op("vector", (lambda v: v.tensor_copy(A["bout_rowb"][0:1, 0:D], rb[0:1, 0:1024])),
                reads=[d_row.mark()], incs=[cv])
        self.op("tensor", lambda t: t.nop(), reads=[cv.mark()], incs=[cp])
        for l in range(6):
            load_row(rb[0:1, 0:1024], d["L0_ffn_ln_g"][l].rearrange("(o f) -> o f", o=1))
            colz((lambda j, l=l: A["lng_c"][:, l * 8 + j:l * 8 + j + 1]),
                 [rb[0:1, j * 128:(j + 1) * 128] for j in range(8)])
            load_row(rb[0:1, 0:1024], d["L0_ffn_ln_b"][l].rearrange("(o f) -> o f", o=1))
            colz((lambda j, l=l: A["lnb_c"][:, l * 8 + j:l * 8 + j + 1]),
                 [rb[0:1, j * 128:(j + 1) * 128] for j in range(8)], psel=1)
            load_row(rb[0:1, 0:1024], d["L0_ffn_b2"][l].rearrange("(o f) -> o f", o=1))
            colz((lambda j, l=l: A["b2_c"][:, l * 8 + j:l * 8 + j + 1]),
                 [rb[0:1, j * 128:(j + 1) * 128] for j in range(8)])
            load_row(rb[0:1, 0:5460], d["L0_ffn_b1"][l].rearrange("(o f) -> o f", o=1))
            srcs = []
            for j in range(22):
                w = min(128, DI - j * 128)
                srcs.append(rb[0:1, j * 128:j * 128 + w])
            for j in range(22):
                w = min(128, DI - j * 128)
                srcs.append(rb[0:1, DI + j * 128:DI + j * 128 + w])
            colz((lambda j, l=l: A["b1_c"][0:(128 if (j % 22) < 21 else DI - 21 * 128),
                                           l * 44 + j:l * 44 + j + 1]), srcs, psel=1)
            load_row(rb[0:1, 0:64], d["L0_ffn_q_scale"][l].rearrange("(o f) -> o f", o=1))
            load_row(rb[0:1, 64:128], d["L0_ffn_k_scale"][l].rearrange("(o f) -> o f", o=1))
            for (nm, off) in (("qs_c", 0), ("ks_c", 64)):
                self.op("tensor", (lambda t, o=off: t.matmul(
                    self.bank(0)[0:64, 0:1], rb[0:1, o:o + 64], A["one11f"][0:1, 0:1],
                    start=True, stop=True)),
                    reads=[d_row.mark(), cv.mark()])
                self.op("tensor", (lambda t, o=off: t.matmul(
                    self.bank(0)[64:128, 0:1], rb[0:1, o:o + 64], A["one11f"][0:1, 0:1],
                    start=True, stop=True)), incs=[cp])
                self.op("vector", (lambda v, nm=nm, l=l: v.tensor_copy(
                    A[nm][:, l:l + 1], self.bank(0)[0:128, 0:1])), reads=[cp.mark()], incs=[cv])

        wstage = self.sbs(sc0, "wstage", [128, 2048], BF16)
        tstage = self.sbs(sc0, "tstage", [128, 2048], BF16)

        def transpose_weight(src2d, rows, cols, dst_dram):
            nrc, ncc = rows // 128, cols // 128
            for rc in range(nrc):
                self.op("gpsimd", (lambda g, rc=rc, src2d=src2d, cols=cols: g.dma_start(
                    out=wstage[:, 0:cols], in_=src2d[rc * 128:(rc + 1) * 128, :])),
                    reads=[cp.mark()], incs=[d_wst])
                ld = d_wst.mark()
                for cc in range(ncc):
                    psb = self.bankb(4) if (cc // 8) % 2 == 0 else self.bankb(5)
                    self.op("tensor", (lambda t, cc=cc, psb=psb: t.transpose(
                        psb[:, (cc % 8) * 128:(cc % 8) * 128 + 128],
                        wstage[:, cc * 128:(cc + 1) * 128], A["identb"][:, :])),
                        reads=[ld, cv.mark()] if cc % 8 == 0 else [],
                        incs=[cp] if (cc % 8 == 7 or cc == ncc - 1) else [])
                    if cc % 8 == 7 or cc == ncc - 1:
                        lo = (cc // 8) * 8
                        def ev(v, lo=lo, hi=cc, psb=psb):
                            last = None
                            for c2 in range(lo, hi + 1):
                                last = v.tensor_copy(tstage[:, c2 * 128:(c2 + 1) * 128],
                                                     psb[:, (c2 % 8) * 128:(c2 % 8) * 128 + 128])
                            return last
                        self.op("vector", ev, reads=[cp.mark(), d_tst.mark()], incs=[cv])
                self.op("sync", (lambda s, rc=rc, dst=dst_dram, ncc=ncc: s.dma_start(
                    out=dst.rearrange("(cc p) r -> p cc r", p=128)[:, :, rc * 128:(rc + 1) * 128],
                    in_=tstage[:, 0:ncc * 128].rearrange("p (cc r) -> p cc r", cc=ncc))),
                    reads=[cv.mark()], incs=[d_tst])

        transpose_weight(d["L0_attn_W_in"], 2 * H2, D, self.wintd)
        transpose_weight(d["L0_attn_W_out"], D, H2, self.wouttd)
        self.op("gpsimd", (lambda g: g.dma_start(out=wstage[0:NS, 0:H2], in_=d["L0_attn_W_ssm"])),
                reads=[cp.mark()], incs=[d_wst])
        wld = d_wst.mark()
        for cc in range(16):
            psb = self.bankb(4) if cc % 2 == 0 else self.bankb(5)
            self.op("tensor", (lambda t, cc=cc, psb=psb: t.transpose(
                psb[:, 0:NS], wstage[0:NS, cc * 128:(cc + 1) * 128], A["identb"][0:NS, 0:NS])),
                reads=[wld, cv.mark()], incs=[cp])
            self.op("vector", (lambda v, cc=cc, psb=psb: v.tensor_copy(
                A["wssmt"][:, cc * NS:(cc + 1) * NS], psb[:, 0:NS])),
                reads=[cp.mark()], incs=[cv])

        self.op("sync", (lambda s: s.dma_start(out=A["a_f"][0:NS, 0:NS], in_=d["L0_attn_A"])),
                incs=[self.d_af])
        ph32 = self.bank(6)
        self.op("vector", lambda v: v.tensor_copy(A["wchain"][0:NS, 0:1], A["cv_c"][0:NS, 0:1]),
                reads=[cv.mark()], incs=[cv])
        for d2 in range(TSC):
            self.op("tensor", (lambda t, d2=d2: t.matmul(
                ph32[0:NS, 0:1], A["a_f"][0:NS, 0:NS], A["wchain"][0:NS, d2:d2 + 1],
                start=True, stop=True)),
                reads=[cv.mark(), self.d_af.mark()], incs=[cp])
            self.op("vector", (lambda v, d2=d2: v.tensor_copy(
                A["wchain"][0:NS, d2 + 1:d2 + 2], ph32[0:NS, 0:1])),
                reads=[cp.mark()], incs=[cv])
        self.op("vector", lambda v: v.memset(A["k0ext"][0:NS, 0:TSC], 0.0))
        self.op("vector", (lambda v: v.tensor_scalar(
            A["k0ext"][0:NS, TSC:2 * TSC], A["wchain"][0:NS, 0:TSC], A["bv_c"][0:NS, 0:1], None,
            op0=ALU.mult)))
        self.op("vector", lambda v: v.tensor_copy(A["w2t"][0:NS, 0:TSC], A["wchain"][0:NS, 1:TSC + 1]))
        self.op("vector", lambda v: v.tensor_copy(qtmp[0:NS, 0:NS], A["i32f"][0:NS, 0:NS]), incs=[cv])
        for e in range(TSC + 1):
            cur = (e % 2) * NS
            nxt = ((e + 1) % 2) * NS
            if e < TSC:
                self.op("vector", (lambda v, e=e, cur=cur: v.tensor_scalar(
                    A["pbext"][0:NS, e * NS:(e + 1) * NS], qtmp[0:NS, cur:cur + NS],
                    A["bv_c"][0:NS, 0:1], None, op0=ALU.mult)))
            if e == TSC:
                self.op("vector", (lambda v, cur=cur: v.tensor_copy(
                    A["q64f"][0:NS, 0:NS], qtmp[0:NS, cur:cur + NS])), incs=[cv])
                break
            self.op("tensor", (lambda t, cur=cur: t.matmul(
                ph32[0:NS, 0:NS], A["a_f"][0:NS, 0:NS], qtmp[0:NS, cur:cur + NS],
                start=True, stop=True)),
                reads=[cv.mark()], incs=[cp])
            self.op("vector", (lambda v, nxt=nxt: v.tensor_copy(
                qtmp[0:NS, nxt:nxt + NS], ph32[0:NS, 0:NS])),
                reads=[cp.mark()], incs=[cv])

    def ln_tok(self, src_ap, dst_ap_bf, parity, extra_reads=()):
        A, cv, ca = self.A, self.cv, self.ca
        col = A["lncol"][:, parity * 8:parity * 8 + 8]
        self.op("vector", (lambda v, s=src_ap, col=col: v.reduce_sum(col[:, 0:1], s, axis=AX.X)),
                reads=list(extra_reads))
        self.op("vector", (lambda v, s=src_ap, col=col: v.scalar_tensor_tensor(
            A["lnsq"][:, 0:1024], s, 1.0, s, op0=ALU.mult, op1=ALU.mult,
            accum_out=col[:, 1:2])))
        self.op("vector", (lambda v, col=col: v.tensor_scalar(col[:, 0:1], col[:, 0:1], 1.0 / D, None, op0=ALU.mult)))
        self.op("vector", (lambda v, col=col: v.tensor_scalar(col[:, 1:2], col[:, 1:2], 1.0 / D, None, op0=ALU.mult)))
        self.op("vector", (lambda v, col=col: v.tensor_scalar(col[:, 2:3], col[:, 0:1], col[:, 0:1], None, op0=ALU.mult)))
        self.op("vector", (lambda v, col=col: v.tensor_sub(col[:, 3:4], col[:, 1:2], col[:, 2:3])), incs=[cv])
        self.op("scalar", (lambda a, col=col: a.activation(col[:, 4:5], col[:, 3:4], AF.Ln, bias=A["epsc"][:, 0:1])),
                reads=[cv.mark()], incs=[ca])
        self.op("scalar", (lambda a, col=col: a.activation(col[:, 5:6], col[:, 4:5], AF.Exp, scale=-0.5)),
                incs=[ca])
        self.op("vector", (lambda v, s=src_ap, dd=dst_ap_bf, col=col: v.tensor_scalar(
            dd, s, col[:, 0:1], col[:, 5:6], op0=ALU.subtract, op1=ALU.mult)),
            reads=[ca.mark()], incs=[cv])

    def phase_A(self, p):
        cp, cv = self.cp, self.cv
        d_xa = self.d_xa
        r = self.sbs(self.scAB, "r", [128, 8 * GCOL], F32, side="right")
        self.scA_local = contextlib.ExitStack()
        xa = self.sbs(self.scA_local, "xa", [128, 2 * 1024], F32)
        xnb = self.sbs(self.scA_local, "xnb", [128, 2 * 1024], BF16)
        self.sbs(self.scA_local, "lnsq", [128, 1024], BF16)
        identb = self.A["identb"]
        self.bar()
        for cc in range(8):
            self.op("vector", (lambda v, cc=cc, r=r: v.memset(r[:, cc * GCOL:cc * GCOL + 1], 0.0)))
        xa_free = [None, None]
        xnb_free = [None, None]
        pt_free = [None, None]
        for tt in range(16):
            b = tt % 2
            self.op("sync", (lambda s, tt=tt, b=b, p=p, xa=xa: s.dma_start(
                out=xa[:, b * 1024:(b + 1) * 1024],
                in_=self.d["L0_attn_x"][p, tt * 128:(tt + 1) * 128, :])),
                reads=[xa_free[b]], incs=[d_xa[b]])
            self.ln_tok(xa[:, b * 1024:(b + 1) * 1024], xnb[:, b * 1024:(b + 1) * 1024],
                        b, extra_reads=[d_xa[b].mark(), xnb_free[b]])
            xa_free[b] = cv.mark()
            psb = self.bankb(4) if b == 0 else self.bankb(5)
            for cc in range(8):
                self.op("tensor", (lambda t, cc=cc, b=b, psb=psb, xnb=xnb, identb=identb: t.transpose(
                    psb[:, cc * 128:(cc + 1) * 128],
                    xnb[:, b * 1024 + cc * 128:b * 1024 + (cc + 1) * 128], identb[:, :])),
                    reads=[cv.mark(), pt_free[b]] if cc == 0 else [],
                    incs=[cp] if cc == 7 else [])
            xnb_free[b] = cp.mark()
            def ev(v, tt=tt, psb=psb, r=r):
                last = None
                for cc in range(8):
                    last = v.tensor_copy(r[:, cc * GCOL + 1 + tt * 128:cc * GCOL + 1 + (tt + 1) * 128],
                                         psb[:, cc * 128:(cc + 1) * 128])
                return last
            self.op("vector", ev, reads=[cp.mark()], incs=[cv])
            pt_free[b] = cv.mark()
        self.r = r
        self.scA_local.close()

    def emit_xw_cast(self, wg, xw, r, scol, xw_free):
        cv = self.cv
        for cc in range(8):
            off = cc * GCOL + 1 + wg * 512 - (1 if cc >= 4 else 0)
            self.op("vector", (lambda v, cc=cc, off=off, xw=xw, r=r: v.tensor_copy(
                xw[:, cc * 512:(cc + 1) * 512], r[:, off:off + 512])),
                reads=[xw_free] if cc == 0 else [])
        if wg > 0:
            for cc in range(4, 8):
                self.op("vector", (lambda v, cc=cc, wg=wg, xw=xw, scol=scol: v.tensor_copy(
                    xw[:, cc * 512:cc * 512 + 1],
                    scol[:, (wg - 1) * 4 + (cc - 4):(wg - 1) * 4 + (cc - 4) + 1])))
        self.op("vector", lambda v: v.nop(), incs=[cv])
        return cv.mark()

    def layer_attn(self, p, l):
        A = self.A
        cp, cv, ca, cg = self.cp, self.cv, self.ca, self.cg
        d_wq, d_woL, d_br = self.d_wq, self.d_woL, self.d_br
        sc = self.scL
        r = self.r
        wqkv_b = self.sbs(sc, "wqkv_b", [128, 8 * 3072], BF16)
        wo_b = self.sbs(sc, "wo_b", [128, 8 * 1024], BF16)
        xw = self.sbs(sc, "xw", [128, 8 * 512], BF16)
        qb = self.sbs(sc, "qb", [128, 8 * 512], BF16)
        kbc = self.sbs(sc, "kbc", [128, 8 * 512], BF16)
        kbp = self.sbs(sc, "kbp", [128, 8 * 256], BF16)
        vbc = self.sbs(sc, "vbc", [128, 8 * 512], BF16)
        vbp = self.sbs(sc, "vbp", [128, 4 * 512], BF16)
        pT = self.sbs(sc, "pT", [128, 2 * 4 * 256], BF16)
        ao = self.sbs(sc, "ao", [128, 8 * 256], BF16)
        sq_s = self.sbs(sc, "sq_s", [128, 512], BF16)
        rn_s = self.sbs(sc, "rn_s", [128, 2 * 512], F32)
        qh_s = sq_s
        t1_s = self.sbs(sc, "t1_s", [128, 2 * 512], F32)
        zr = self.sbs(sc, "zr", [1, 512], F32)
        zb = self.sbs(sc, "zb", [128, 2 * 256], F32)
        scol = self.sbs(sc, "scol", [128, 4 * 4], F32)
        rotb = self.sbs(sc, "rotb", [128, 4 * 512], BF16)
        QK8 = 8.0

        for half in range(2):
            key = ("wq", l, half)
            if key not in self.wb_marks:
                self.op("gpsimd", (lambda g, half=half, l=l, wqkv_b=wqkv_b: g.dma_start(
                    out=wqkv_b[:, half * 4 * 3072:(half + 1) * 4 * 3072].rearrange(
                        "p (k f) -> p k f", k=4),
                    in_=self.d["L0_ffn_Wqkv"][l][half * 512:(half + 1) * 512, :].rearrange(
                        "(k p) f -> p k f", p=128))),
                    reads=[self.wq_free] + self.wq_guard, incs=[d_wq])
                ld = d_wq.mark()
                self.op("sync", (lambda s, half=half, l=l, wqkv_b=wqkv_b: s.dma_start(
                    out=self.wqcd[l][half],
                    in_=wqkv_b[:, half * 4 * 3072:(half + 1) * 4 * 3072])),
                    reads=[ld], incs=[self.d_wc])
                self.wb_marks[key] = self.d_wc.mark()
                self.wq_guard = [self.wb_marks[key]]
            else:
                qeng = "gpsimd" if half == 0 else "sync"
                self.op(qeng, (lambda g, half=half, l=l, wqkv_b=wqkv_b: g.dma_start(
                    out=wqkv_b[:, half * 4 * 3072:(half + 1) * 4 * 3072],
                    in_=self.wqcd[l][half])),
                    reads=[self.wq_free, self.wb_marks[key]], incs=[d_wq])
        key = ("wo", l)
        if key not in self.wb_marks:
            self.op("gpsimd", (lambda g, l=l, wo_b=wo_b: g.dma_start(
                out=wo_b[:, :].rearrange("p (k f) -> p k f", k=8),
                in_=self.d["L0_ffn_Wo"][l].rearrange("(k p) f -> p k f", p=128))),
                reads=self.wo_guard, incs=[d_woL])
            ld = d_woL.mark()
            self.op("sync", (lambda s, l=l, wo_b=wo_b: s.dma_start(
                out=self.wocd[l], in_=wo_b[:, :])),
                reads=[ld], incs=[self.d_wc])
            self.wb_marks[key] = self.d_wc.mark()
            self.wo_guard = [self.wb_marks[key]]
        else:
            for hh in range(2):
                qeng = "gpsimd" if hh == 0 else "sync"
                self.op(qeng, (lambda g, hh=hh, l=l, wo_b=wo_b: g.dma_start(
                    out=wo_b[:, hh * 4096:(hh + 1) * 4096],
                    in_=self.wocd[l][:, hh * 4096:(hh + 1) * 4096])),
                    reads=[self.wb_marks[key]] if hh == 0 else [], incs=[d_woL])
        wq_ld = d_wq.mark()
        wo_ld = d_woL.mark()

        pbf = {i: [] for i in range(8)}
        xw_free = None
        sqs_free = [[], []]
        t1_free = [[], []]
        nrm_free = None
        rn_free = [[], []]
        qh_free = [[], []]
        zr_free = []
        zb_free = [[], []]
        pending_norm = None
        gq_prev = []
        pT_free = [[], []]
        rot_free = []
        rot_ld = None
        for wg in range(4):
            for tb in range(4):
                self.op("gpsimd", (lambda g, tb=tb, wg=wg, rotb=rotb: g.dma_start(
                    out=rotb[:, tb * 512:(tb + 1) * 512],
                    in_=self.rot4d[tb][:, wg * 512:(wg + 1) * 512])),
                    reads=(rot_free if tb == 0 else []), incs=[self.d_rot[0]])
            rot_ld = self.d_rot[0].mark()
            if wg == 0:
                xw_done = self.emit_xw_cast(0, xw, r, scol, xw_free)
            else:
                xw_done = self.xw_done_next

            for g in range(2):
                sched = []
                for fc in range(4):
                    sched.append(("q", fc))
                    sched.append(("k", fc))
                    sched.append(("v", fc))
                ei = 0
                for (kind, fc) in sched:
                    if kind == "v":
                        tc = fc
                        bsel = 4 + (tc % 2)
                        ps = self.bank(bsel)
                        for k in range(8):
                            self.op("tensor", (lambda t, g=g, tc=tc, k=k, ps=ps, wqkv_b=wqkv_b,
                                               xw=xw: t.matmul(
                                ps[:, 0:512],
                                xw[:, k * 512 + tc * 128:k * 512 + tc * 128 + 128],
                                wqkv_b[:, k * 3072 + 2048 + g * 512:k * 3072 + 2048 + g * 512 + 512],
                                start=(k == 0), stop=(k == 7))),
                                reads=([xw_done, wq_ld] + pbf[bsel]) if k == 0 else [],
                                incs=[cp] if k == 7 else [])
                        self.op("vector", (lambda v, g=g, tc=tc, ps=ps, vbc=vbc: v.tensor_copy(
                            vbc[:, (g * 4 + tc) * 512:(g * 4 + tc) * 512 + 512], ps[:, 0:512])),
                            reads=[cp.mark()], incs=[cv])
                        pbf[bsel] = [cv.mark()]
                        continue
                    qk = 0 if kind == "q" else 1
                    if True:
                        gp = ei % 2
                        ei += 1
                        c0 = gp * 512
                        bsel = gp
                        b2 = 2 if gp == 0 else 6
                        b3 = 3 if gp == 0 else 7
                        ps = self.bank(bsel)
                        for k in range(8):
                            self.op("tensor", (lambda t, g=g, qk=qk, fc=fc, k=k, ps=ps,
                                               wqkv_b=wqkv_b, xw=xw: t.matmul(
                                ps[:, 0:512],
                                wqkv_b[:, k * 3072 + qk * 1024 + g * 512 + fc * 128:
                                       k * 3072 + qk * 1024 + g * 512 + fc * 128 + 128],
                                xw[:, k * 512:(k + 1) * 512],
                                start=(k == 0), stop=(k == 7))),
                                reads=([xw_done, wq_ld] + pbf[bsel]) if k == 0 else [],
                                incs=[cp] if k == 7 else [])
                        qkp_done = cp.mark()
                        self.op("scalar", (lambda a, ps=ps, sq_s=sq_s: a.activation(
                            sq_s[:, 0:512], ps[:, 0:512], AF.Square)),
                            reads=[qkp_done] + sqs_free[0], incs=[ca])
                        self.op("tensor", (lambda t, sq_s=sq_s, b2=b2: t.matmul(
                            self.bank(b2)[:, 0:512], A["bones"][:, :], sq_s[:, 0:512],
                            start=True, stop=True)),
                            reads=[ca.mark()] + pbf[b2], incs=[cp])
                        self.op("scalar", (lambda a, t1_s=t1_s, c0=c0, b2=b2: a.activation(
                            t1_s[:, c0:c0 + 512], self.bank(b2)[:, 0:512], AF.Ln,
                            bias=A["epsq"][:, 0:1])),
                            reads=[cp.mark()] + t1_free[gp], incs=[ca])
                        pbf[b2] = [ca.mark()]
                        self.op("scalar", (lambda a, t1_s=t1_s, rn_s=rn_s, c0=c0: a.activation(
                            rn_s[:, c0:c0 + 512], t1_s[:, c0:c0 + 512], AF.Exp, scale=-0.5)),
                            reads=rn_free[gp], incs=[ca])
                        nrm_free = ca.mark()
                        scale_col = A["qs_c"] if qk == 0 else A["ks_c"]
                        self.op("vector", (lambda v, ps=ps, rn_s=rn_s, qh_s=qh_s, c0=c0,
                                           scale_col=scale_col, l=l: v.scalar_tensor_tensor(
                            qh_s[:, 0:512], rn_s[:, c0:c0 + 512], scale_col[:, l:l + 1],
                            ps[:, 0:512], op0=ALU.mult, op1=ALU.mult)),
                            reads=[ca.mark()] + qh_free[0], incs=[cv])
                        rn_free[gp] = [cv.mark()]
                        pbf[bsel] = [cv.mark()]
                        self.op("tensor", (lambda t, qh_s=qh_s, b3=b3: t.matmul(
                            self.bank(b3)[:, 0:512], A["pswap"][:, :], qh_s[:, 0:512],
                            start=True, stop=True)),
                            reads=[cv.mark()] + pbf[b3], incs=[cp])
                        swp_done = cp.mark()
                        qh_free[0] = [swp_done]
                        ctab = rotb[:, (0 if qk == 0 else 2) * 512:
                                    (0 if qk == 0 else 2) * 512 + 512]
                        stabt = rotb[:, (1 if qk == 0 else 3) * 512:
                                     (1 if qk == 0 else 3) * 512 + 512]
                        stt_mark = cv.mark()
                        self.op("vector", (lambda v, t1_s=t1_s, stabt=stabt, c0=c0, b3=b3:
                                v.tensor_mul(t1_s[:, c0:c0 + 512], self.bank(b3)[:, 0:512],
                                             stabt)),
                                reads=[swp_done, rot_ld], incs=[cv])
                        t1mul_mark = cv.mark()
                        pbf[b3] = [t1mul_mark]
                        dest = (qb if qk == 0 else kbc)
                        dap = dest[0:128, (g * 4 + fc) * 512:(g * 4 + fc) * 512 + 512]
                        self.op("gpsimd", (lambda gg, dap=dap, qh_s=qh_s, ctab=ctab:
                                gg.tensor_mul(dap, qh_s[:, 0:512], ctab)),
                                reads=[stt_mark, rot_ld] + gq_prev, incs=[cg])
                        sqs_free[0] = [cp.mark(), cg.mark()]
                        self.op("gpsimd", (lambda gg, dap=dap, t1_s=t1_s, c0=c0:
                                gg.tensor_add(dap, dap, t1_s[:, c0:c0 + 512])),
                                reads=[cg.mark(), t1mul_mark], incs=[cg])
                        gq_prev = [cg.mark()]
                        t1_free[gp] = [cg.mark()]
            prod_done = cv.mark()
            prod_g_done = cg.mark()
            if wg == 3:
                self.wq_free = cp.mark()
            xw_free = cp.mark()
            rot_free = [cv.mark(), cg.mark()]

            for wi in range(2):
                w = 2 * wg + wi
                def flush_norm():
                    nonlocal pending_norm, zr_free
                    if pending_norm is None:
                        return
                    fobank, fob, j, z_done, pair_done = pending_norm
                    pending_norm = None
                    zsl = j % 2
                    self.op("scalar", (lambda a, fobank=fobank, zb=zb, zsl=zsl: a.activation(
                        zb[0:1, zsl * 256:(zsl + 1) * 256], fobank[0:1, 256:512],
                        AF.Ln)),
                        reads=[z_done] + zr_free + zb_free[zsl])
                    self.op("scalar", (lambda a, zb=zb, zr=zr, zsl=zsl: a.activation(
                        zr[0:1, 0:256], zb[0:1, zsl * 256:(zsl + 1) * 256],
                        AF.Exp, scale=-1.0)))
                    self.op("scalar", (lambda a, fobank=fobank, zb=zb, zsl=zsl: a.activation(
                        zb[0:1, zsl * 256:(zsl + 1) * 256], fobank[64:65, 256:512],
                        AF.Ln)))
                    self.op("scalar", (lambda a, zb=zb, zr=zr, zsl=zsl: a.activation(
                        zr[0:1, 256:512], zb[0:1, zsl * 256:(zsl + 1) * 256],
                        AF.Exp, scale=-1.0)),
                        incs=[ca])
                    zrec_done = ca.mark()
                    zps = self.bank(7)
                    for hh in range(2):
                        self.op("tensor", (lambda t, zps=zps, zr=zr, hh=hh: t.matmul(
                            zps[hh * 64:(hh + 1) * 64, 0:256], A["ones1f"][0:1, 0:64],
                            zr[0:1, hh * 256:hh * 256 + 256],
                            start=True, stop=True)),
                            reads=([zrec_done] + pbf[7]) if hh == 0 else [],
                            incs=[cp] if hh == 1 else [])
                    zbc_done = cp.mark()
                    zr_free = [zbc_done]
                    self.op("vector", (lambda v, zps=zps, zb=zb, zsl=zsl: v.tensor_copy(
                        zb[:, zsl * 256:(zsl + 1) * 256], zps[0:128, 0:256])),
                        reads=[zbc_done] + zb_free[zsl])
                    self.op("vector", (lambda v, fobank=fobank, zb=zb, zsl=zsl, j=j, ao=ao:
                            v.tensor_mul(ao[:, j * 256:(j + 1) * 256],
                                         fobank[0:128, 0:256],
                                         zb[:, zsl * 256:(zsl + 1) * 256])),
                        reads=[pair_done], incs=[cv])
                    pbf[fob] = [cv.mark()]
                    pbf[7] = [cv.mark()]
                    zb_free[zsl] = [cv.mark()]
                def emit_masks(hh):
                    sset = (1, 2) if hh % 2 == 0 else (3, 4)
                    mss = [2, 3] if w == 0 else [0, 1, 2, 3]
                    for mi, m in enumerate(mss):
                        ps = self.bank(sset[mi // 2])
                        colr = (mi % 2) * 256
                        self.op("tensor", (lambda t, ps=ps, colr=colr, m=m: t.matmul(
                            ps[:, colr:colr + 256], A["identb"][:, :],
                            A["sTm%d" % m][:, :], start=True, stop=False)),
                            reads=([prod_done, prod_g_done] + pbf[sset[0]] + pbf[sset[1]])
                            if mi == 0 else [])
                ms = [2, 3] if w == 0 else [0, 1, 2, 3]
                nm = len(ms)

                def emit_scores(hh):
                    g, hq = hh // 8, hh % 8
                    fc, pb = hq // 2, (hq % 2) * 64
                    sset = (1, 2) if hh % 2 == 0 else (3, 4)
                    psl = (hh % 2) * 1024
                    for pi in range(nm // 2):
                        ps = self.bank(sset[pi])
                        mname = "sTm01" if (nm == 4 and pi == 0) else "sTm23"
                        self.op("tensor", (lambda t, ps=ps, mname=mname: t.matmul(
                            ps[:, 0:512], A["identb"][:, :],
                            A[mname][:, :], start=True, stop=False)),
                            reads=([prod_done, prod_g_done] + pbf[sset[0]] + pbf[sset[1]])
                            if pi == 0 else [])
                        for sub in range(2):
                            mi = pi * 2 + sub
                            m = ms[mi]
                            colr = sub * 256
                            gc = 2 * w - 2 + m
                            if gc >= 4 * wg:
                                seg = gc - 4 * wg
                                kap = kbc[pb:pb + 64, (g * 4 + fc) * 512 + seg * 128:
                                          (g * 4 + fc) * 512 + seg * 128 + 128]
                            else:
                                seg = gc - (4 * wg - 2)
                                kap = kbp[pb:pb + 64, (g * 4 + fc) * 256 + seg * 128:
                                          (g * 4 + fc) * 256 + seg * 128 + 128]
                            qap = qb[pb:pb + 64, (g * 4 + fc) * 512 + wi * 256:
                                     (g * 4 + fc) * 512 + wi * 256 + 256]
                            self.op("tensor", (lambda t, ps=ps, colr=colr, kap=kap, qap=qap:
                                    t.matmul(ps[:, colr:colr + 256], kap, qap,
                                             start=False, stop=(colr == 256))),
                                    incs=[cp] if mi == nm - 1 else [])
                    return cp.mark()

                sT_next = emit_scores(0)
                for h in range(16):
                    if h % 2 == 1:
                        flush_norm()
                    g, hq = h // 8, h % 8
                    fc, pb = hq // 2, (hq % 2) * 64
                    stset = (1, 2) if h % 2 == 0 else (3, 4)
                    psl = (h % 2) * 1024
                    ob = 5 if (h // 2) % 2 == 0 else 6
                    obank = self.bank(ob)
                    sT_done = sT_next
                    if h < 15:
                        sT_next = emit_scores(h + 1)
                    exp_marks = []
                    for pi in range(nm // 2):
                        ps = self.bank(stset[pi])
                        self.op("scalar", (lambda a, ps=ps, pi=pi, pT=pT, psl=psl:
                                a.activation(
                            pT[:, psl + pi * 512:psl + (pi + 1) * 512], ps[:, 0:512],
                            AF.Exp, scale=QK8)),
                            reads=([sT_done] + pT_free[h % 2]) if pi == 0 else [], incs=[ca])
                        exp_marks.append(ca.mark())
                        exp_marks.append(ca.mark())
                    pbf[stset[0]] = [exp_marks[-1]]
                    pbf[stset[1]] = [exp_marks[-1]]
                    for mi, m in enumerate(ms):
                        self.op("tensor", (lambda t, mi=mi, pT=pT, obank=obank, pb=pb,
                                           nm=nm, psl=psl: t.matmul(
                            obank[pb:pb + 1, 256:512], A["onescolb"][:, 0:1],
                            pT[:, psl + mi * 256:psl + (mi + 1) * 256],
                            start=(mi == 0), stop=(mi == nm - 1))),
                            reads=([exp_marks[mi]] + (pbf[ob] if mi == 0 else [])),
                            incs=[cp] if mi == nm - 1 else [])
                    z_done = cp.mark()
                    for mi, m in enumerate(ms):
                        gc = 2 * w - 2 + m
                        if gc >= 4 * wg:
                            seg = gc - 4 * wg
                            vap = vbc[:, (g * 4 + seg) * 512 + hq * 64:
                                      (g * 4 + seg) * 512 + hq * 64 + 64]
                        else:
                            seg = gc - (4 * wg - 2)
                            vap = vbp[:, (g * 2 + seg) * 512 + hq * 64:
                                      (g * 2 + seg) * 512 + hq * 64 + 64]
                        self.op("tensor", (lambda t, vap=vap, mi=mi, pT=pT, obank=obank, pb=pb,
                                           nm=nm, psl=psl: t.matmul(
                            obank[pb:pb + 64, 0:256], vap,
                            pT[:, psl + mi * 256:psl + (mi + 1) * 256],
                            start=(mi == 0), stop=(mi == nm - 1))),
                            incs=[cp] if mi == nm - 1 else [])
                    pT_free[h % 2] = [cp.mark()]
                    if h % 2 == 1:
                        pending_norm = (obank, ob, h // 2, z_done, cp.mark())
                flush_norm()
                ao_done = cv.mark()
                if wi == 1 and wg < 3:
                    for cc in range(4, 8):
                        off = cc * GCOL + 1 + (wg + 1) * 512 - 1
                        self.op("vector", (lambda v, cc=cc, wg=wg, off=off, scol=scol, r=r:
                                v.tensor_copy(scol[:, wg * 4 + (cc - 4):wg * 4 + (cc - 4) + 1],
                                              r[:, off:off + 1])))
                    self.op("vector", lambda v: v.nop(), incs=[cv])
                    self.xw_done_next = self.emit_xw_cast(wg + 1, xw, r, scol, xw_free)
                for oc in range(8):
                    ps = self.bank(oc)
                    for c2 in range(8):
                        self.op("tensor", (lambda t, oc=oc, c2=c2, ps=ps, wo_b=wo_b, ao=ao:
                                t.matmul(ps[:, 0:256],
                                         wo_b[:, c2 * 1024 + oc * 128:c2 * 1024 + oc * 128 + 128],
                                         ao[:, c2 * 256:(c2 + 1) * 256],
                                         start=(c2 == 0), stop=(c2 == 7))),
                                reads=(([ao_done, wo_ld] + sum((pbf[i] for i in range(8)), []))
                                       if (oc == 0 and c2 == 0) else []),
                                incs=[cp] if c2 == 7 else [])
                    self.op("vector", (lambda v, oc=oc, ps=ps, w=w, r=r: v.tensor_add(
                        r[:, oc * GCOL + 1 + w * 256:oc * GCOL + 1 + (w + 1) * 256],
                        ps[:, 0:256],
                        r[:, oc * GCOL + 1 + w * 256:oc * GCOL + 1 + (w + 1) * 256])),
                        reads=[cp.mark()], incs=[cv])
                    pbf[oc] = [cv.mark()]
            if wg < 3:
                for g in range(2):
                    for fc in range(4):
                        self.op("vector", (lambda v, g=g, fc=fc, kbc=kbc, kbp=kbp: v.tensor_copy(
                            kbp[:, (g * 4 + fc) * 256:(g * 4 + fc) * 256 + 256],
                            kbc[:, (g * 4 + fc) * 512 + 256:(g * 4 + fc) * 512 + 512])),
                            reads=[prod_g_done] if (g == 0 and fc == 0) else [])
                    for u in range(2):
                        self.op("vector", (lambda v, g=g, u=u, vbc=vbc, vbp=vbp: v.tensor_copy(
                            vbp[:, (g * 2 + u) * 512:(g * 2 + u) * 512 + 512],
                            vbc[:, (g * 4 + 2 + u) * 512:(g * 4 + 2 + u) * 512 + 512])))
                self.op("vector", lambda v: v.nop(), incs=[cv])


    def layer_ffn(self, p, l):
        A = self.A
        cp, cv, ca = self.cp, self.cv, self.ca
        d_w1, d_w2 = self.d_w1, self.d_w2
        sc = self.scL
        r = self.r
        xn2 = self.sbs(sc, "xn2", [128, 8 * 1024], BF16)
        ag = self.sbs(sc, "ag", [128, 22 * 1024], BF16)
        w1b = self.sbs(sc, "w1b", [128, 6 * 1024], BF16)
        w2b = self.sbs(sc, "w2b", [128, 2 * 1024], BF16)
        gel = self.sbs(sc, "gel", [128, 1024], BF16)
        m_b = self.sbs(sc, "m_b", [128, 1024], F32)
        rstd_b = self.sbs(sc, "rstd_b", [128, 1024], F32)
        t1f = self.sbs(sc, "t1f", [128, 1024], F32)
        t2f = self.sbs(sc, "t2f", [128, 1024], F32)
        sqs = self.sbs(sc, "sqs", [128, 2 * 1024], BF16)

        CHW = [128] * 21 + [DI - 21 * 128]
        w1_free = [None] * 6
        w2_free = [None, None]
        pbf = {i: [] for i in range(8)}
        xn2_free = None
        prev_w2 = None
        gel_free = [None, None]
        for half in (1, 0):
            T0 = half * 1024
            for cc in range(8):
                off = cc * GCOL + 1 + T0 - (1 if cc >= 4 else 0)
                self.op("vector", (lambda v, cc=cc, off=off, xn2=xn2, r=r: v.tensor_copy(
                    xn2[:, cc * 1024:(cc + 1) * 1024], r[:, off:off + 1024])),
                    reads=[xn2_free] if cc == 0 else [])
            self.op("vector", lambda v: v.nop(), incs=[cv])
            rsb_done = cv.mark()
            for cc in range(8):
                for tt in range(2):
                    self.op("tensor", (lambda t, cc=cc, tt=tt, xn2=xn2: t.matmul(
                        self.bank(tt)[:, 0:512], A["onesfull"][:, :],
                        xn2[:, cc * 1024 + tt * 512:cc * 1024 + tt * 512 + 512],
                        start=(cc == 0), stop=(cc == 7))),
                        reads=(([rsb_done] + pbf[tt]) if cc == 0 else []),
                        incs=[cp] if cc == 7 else [])
            sq_free = [None, None]
            for cc in range(8):
                b = cc % 2
                self.op("scalar", (lambda a, cc=cc, b=b, xn2=xn2, sqs=sqs: a.activation(
                    sqs[:, b * 1024:(b + 1) * 1024], xn2[:, cc * 1024:(cc + 1) * 1024],
                    AF.Square)),
                    reads=[rsb_done, sq_free[b]], incs=[ca])
                sq_ca = ca.mark()
                for tt in range(2):
                    self.op("tensor", (lambda t, cc=cc, tt=tt, b=b, sqs=sqs: t.matmul(
                        self.bank(2 + tt)[:, 0:512], A["onesfull"][:, :],
                        sqs[:, b * 1024 + tt * 512:b * 1024 + tt * 512 + 512],
                        start=(cc == 0), stop=(cc == 7))),
                        reads=([sq_ca] if tt == 0 else []) + (pbf[2 + tt] if cc == 0 else []),
                        incs=[cp] if tt == 1 else [])
                sq_free[b] = cp.mark()
            stat_done = cp.mark()
            for tt in range(2):
                self.op("vector", (lambda v, tt=tt, m_b=m_b: v.tensor_scalar(
                    m_b[:, tt * 512:(tt + 1) * 512], self.bank(tt)[:, 0:512],
                    1.0 / D, None, op0=ALU.mult)), reads=[stat_done])
                self.op("vector", (lambda v, tt=tt, t1f=t1f: v.tensor_scalar(
                    t1f[:, tt * 512:(tt + 1) * 512], self.bank(2 + tt)[:, 0:512],
                    1.0 / D, None, op0=ALU.mult)))
                self.op("vector", (lambda v, tt=tt, m_b=m_b, t2f=t2f: v.tensor_mul(
                    t2f[:, tt * 512:(tt + 1) * 512], m_b[:, tt * 512:(tt + 1) * 512],
                    m_b[:, tt * 512:(tt + 1) * 512])))
                self.op("vector", (lambda v, tt=tt, t1f=t1f, t2f=t2f: v.tensor_sub(
                    t1f[:, tt * 512:(tt + 1) * 512], t1f[:, tt * 512:(tt + 1) * 512],
                    t2f[:, tt * 512:(tt + 1) * 512])), incs=[cv])
            self.op("scalar", (lambda a, t1f=t1f, t2f=t2f: a.activation(
                t2f[:, 0:1024], t1f[:, 0:1024], AF.Ln, bias=A["epsc"][:, 0:1])),
                reads=[cv.mark()], incs=[ca])
            self.op("scalar", (lambda a, t2f=t2f, rstd_b=rstd_b: a.activation(
                rstd_b[:, 0:1024], t2f[:, 0:1024], AF.Exp, scale=-0.5)), incs=[ca])
            self.op("vector", lambda v: v.nop(), reads=[ca.mark()], incs=[cv])
            var_done = cv.mark()
            for i in range(4):
                pbf[i] = [var_done]
            for cc in range(8):
                off = cc * GCOL + 1 + T0 - (1 if cc >= 4 else 0)
                self.op("vector", (lambda v, cc=cc, off=off, t1f=t1f, m_b=m_b, r=r: v.tensor_sub(
                    t1f[:, 0:1024], r[:, off:off + 1024], m_b[:, 0:1024])),
                    reads=[var_done] if cc == 0 else [])
                self.op("vector", (lambda v, t1f=t1f, rstd_b=rstd_b: v.tensor_mul(
                    t1f[:, 0:1024], t1f[:, 0:1024], rstd_b[:, 0:1024])))
                self.op("vector", (lambda v, cc=cc, l=l, t1f=t1f, xn2=xn2: v.tensor_scalar(
                    xn2[:, cc * 1024:(cc + 1) * 1024], t1f[:, 0:1024],
                    A["lng_c"][:, l * 8 + cc:l * 8 + cc + 1],
                    A["lnb_c"][:, l * 8 + cc:l * 8 + cc + 1],
                    op0=ALU.mult, op1=ALU.add)), incs=[cv])
            xn2_done = cv.mark()
            first_ag = True
            for j in range(22):
                wda = CHW[j]
                for agx in range(2):
                    slot = (j % 3) * 2 + agx
                    c0 = agx * DI + j * 128
                    jj = j * 2 + agx
                    key = ("w1", l, jj)
                    if key not in self.wb_marks:
                        self.op("gpsimd", (lambda g, l=l, c0=c0, wda=wda, slot=slot, w1b=w1b:
                                g.dma_start(
                            out=w1b[:, slot * 1024:slot * 1024 + 8 * wda].rearrange(
                                "p (k f) -> p k f", k=8),
                            in_=self.d["L0_ffn_W1"][l][:, c0:c0 + wda].rearrange(
                                "(k p) f -> p k f", p=128))),
                            reads=[w1_free[slot]] + self.w1_slot_guard[slot],
                            incs=[d_w1[slot]])
                        ld = d_w1[slot].mark()
                        self.op("sync", (lambda s, l=l, jj=jj, wda=wda, slot=slot, w1b=w1b:
                                s.dma_start(
                            out=self.w1cd[l][:, jj, :, 0:wda],
                            in_=w1b[:, slot * 1024:slot * 1024 + 8 * wda].rearrange(
                                "p (k f) -> p k f", k=8))),
                            reads=[ld], incs=[self.d_wc])
                        self.wb_marks[key] = self.d_wc.mark()
                        self.w1_slot_guard[slot] = [self.wb_marks[key]]
                    else:
                        qeng = "sync" if j % 2 == 0 else "gpsimd"
                        self.op(qeng, (lambda g, l=l, jj=jj, wda=wda, slot=slot, w1b=w1b:
                                g.dma_start(
                            out=w1b[:, slot * 1024:slot * 1024 + 8 * wda].rearrange(
                                "p (k f) -> p k f", k=8),
                            in_=self.w1cd[l][:, jj, :, 0:wda])),
                            reads=[w1_free[slot], self.wb_marks[key]], incs=[d_w1[slot]])
                w1_ld = [d_w1[(j % 3) * 2].mark(), d_w1[(j % 3) * 2 + 1].mark()]
                for agx in range(2):
                    slot = (j % 3) * 2 + agx
                    bs0 = (0 if j % 2 == 0 else 4) + agx * 2
                    for tt in range(2):
                        ps = self.bank(bs0 + tt)
                        for k in range(8):
                            self.op("tensor", (lambda t, j=j, k=k, tt=tt, slot=slot, wda=wda,
                                               ps=ps, w1b=w1b, xn2=xn2: t.matmul(
                                ps[0:wda, 0:512],
                                w1b[:, slot * 1024 + k * wda:slot * 1024 + (k + 1) * wda],
                                xn2[:, k * 1024 + tt * 512:k * 1024 + tt * 512 + 512],
                                start=(k == 0), stop=(k == 7))),
                                reads=(([xn2_done, w1_ld[agx]] + pbf[bs0 + tt] +
                                        ([prev_w2] if first_ag else []))
                                       if k == 0 else []),
                                incs=[cp] if k == 7 else [])
                first_ag = False
                hg_done = cp.mark()
                w1_free[(j % 3) * 2] = hg_done
                w1_free[(j % 3) * 2 + 1] = hg_done
                bsa = (0 if j % 2 == 0 else 4)
                for tt in range(2):
                    psg = self.bank(bsa + 2 + tt)
                    psa = self.bank(bsa + tt)
                    self.op("scalar", (lambda a, j=j, tt=tt, l=l, wda=wda, psg=psg, gel=gel:
                            a.activation(gel[0:wda, tt * 512:(tt + 1) * 512],
                                         psg[0:wda, 0:512], AF.Gelu,
                                         bias=A["b1_c"][0:wda, l * 44 + 22 + j:l * 44 + 23 + j])),
                            reads=[hg_done, gel_free[tt]], incs=[ca])
                    self.op("vector", (lambda v, j=j, tt=tt, l=l, wda=wda, psa=psa, gel=gel,
                                       ag=ag: v.scalar_tensor_tensor(
                        ag[0:wda, j * 1024 + tt * 512:j * 1024 + tt * 512 + 512],
                        psa[0:wda, 0:512],
                        A["b1_c"][0:wda, l * 44 + j:l * 44 + j + 1],
                        gel[0:wda, tt * 512:(tt + 1) * 512],
                        op0=ALU.add, op1=ALU.mult)),
                        reads=[ca.mark()], incs=[cv])
                    pbf[bsa + tt] = [cv.mark()]
                    pbf[bsa + 2 + tt] = [cv.mark()]
                    gel_free[tt] = cv.mark()
            ag_done = cv.mark()
            for tt in range(2):
                for c2 in range(22):
                    wda = CHW[c2]
                    slot = c2 % 2
                    key = ("w2", l, c2)
                    if key not in self.wb_marks:
                        self.op("gpsimd", (lambda g, l=l, c2=c2, wda=wda, slot=slot, w2b=w2b:
                                g.dma_start(
                            out=w2b[0:wda, slot * 1024:slot * 1024 + 1024],
                            in_=self.d["L0_ffn_W2"][l][c2 * 128:c2 * 128 + wda, :])),
                            reads=[w2_free[slot]] + self.w2_slot_guard[slot],
                            incs=[d_w2[slot]])
                        ld = d_w2[slot].mark()
                        self.op("sync", (lambda s, l=l, c2=c2, wda=wda, slot=slot, w2b=w2b:
                                s.dma_start(
                            out=self.w2cd[l][c2][0:wda, :],
                            in_=w2b[0:wda, slot * 1024:slot * 1024 + 1024])),
                            reads=[ld], incs=[self.d_wc])
                        self.wb_marks[key] = self.d_wc.mark()
                        self.w2_slot_guard[slot] = [self.wb_marks[key]]
                    else:
                        qeng = "sync" if c2 % 2 == 0 else "gpsimd"
                        self.op(qeng, (lambda g, l=l, c2=c2, wda=wda, slot=slot, w2b=w2b:
                                g.dma_start(
                            out=w2b[0:wda, slot * 1024:slot * 1024 + 1024],
                            in_=self.w2cd[l][c2][0:wda, :])),
                            reads=[w2_free[slot], self.wb_marks[key]], incs=[d_w2[slot]])
                    w2_ld = d_w2[slot].mark()
                    for oc in range(8):
                        ps = self.bank(oc)
                        self.op("tensor", (lambda t, c2=c2, oc=oc, tt=tt, wda=wda, slot=slot,
                                           ps=ps, w2b=w2b, ag=ag: t.matmul(
                            ps[:, 0:512],
                            w2b[0:wda, slot * 1024 + oc * 128:slot * 1024 + oc * 128 + 128],
                            ag[0:wda, c2 * 1024 + tt * 512:c2 * 1024 + tt * 512 + 512],
                            start=(c2 == 0), stop=(c2 == 21))),
                            reads=(([ag_done, w2_ld] + sum((pbf[i] for i in range(8)), []))
                                   if (c2 == 0 and oc == 0) else
                                   ([w2_ld] if oc == 0 else [])),
                            incs=[cp] if oc == 7 else [])
                    w2_free[slot] = cp.mark()
                w2_done = cp.mark()
                for oc in range(8):
                    ps = self.bank(oc)
                    off = oc * GCOL + 1 + T0 + tt * 512
                    self.op("vector", (lambda v, oc=oc, l=l, off=off, ps=ps, r=r:
                            v.scalar_tensor_tensor(
                        r[:, off:off + 512], ps[:, 0:512],
                        A["b2_c"][:, l * 8 + oc:l * 8 + oc + 1],
                        r[:, off:off + 512], op0=ALU.add, op1=ALU.add)),
                        reads=[w2_done], incs=[cv])
                    pbf[oc] = [cv.mark()]
            xn2_free = cp.mark()
            prev_w2 = cp.mark()

    def g_store(self, p):
        ca = self.ca
        d_g = self.d_g
        r = self.r
        gsc = contextlib.ExitStack()
        gtmp = self.sbs(gsc, "gtmp", [128, 2 * 2048], BF16)
        self.fence("scalar")
        gt_free = [None, None]
        for cc in range(8):
            b = cc % 2
            self.op("scalar", (lambda a, cc=cc, b=b, r=r, gtmp=gtmp: a.activation(
                gtmp[:, b * 2048:(b + 1) * 2048],
                r[:, cc * GCOL + 1:cc * GCOL + 1 + 2048], AF.Silu)),
                reads=[gt_free[b]], incs=[ca])
            self.op("sync", (lambda s, cc=cc, b=b, gtmp=gtmp: s.dma_start(
                out=self.gscrd[cc * 128:(cc + 1) * 128, :],
                in_=gtmp[:, b * 2048:(b + 1) * 2048])),
                reads=[ca.mark()], incs=[d_g[b]])
            gt_free[b] = d_g[b].mark()
        gsc.close()

    def phase_C(self, p):
        A, d = self.A, self.d
        cp, cv, ca = self.cp, self.cv, self.ca
        d_xc, d_gt, d_wb, d_wo = self.d_xc, self.d_gt, self.d_wb, self.d_wo
        d_yr, d_out, d_tap = self.d_yr, self.d_out, self.d_tap
        sc = self.scC
        winb = self.sbs(sc, "winb", [128, 4 * 1024], BF16)
        self.sbs(sc, "lnsq", [128, 1024], BF16)
        woutt_b = self.sbs(sc, "woutt_b", [128, 2 * 1024], BF16)
        xc = self.sbs(sc, "xc", [128, 4 * 1024], F32)
        xntc = self.sbs(sc, "xntc", [128, 4 * 1024], BF16)
        xnfc = self.sbs(sc, "xnfc", [128, 8 * 512], BF16)
        gtok = self.sbs(sc, "gtok", [128, 4 * 1024], BF16)
        xhb = self.sbs(sc, "xhb", [128, 16 * (CW + 4)], BF16)
        xhs = self.sbs(sc, "xhs", [128, 16 * 512], BF16)
        gateb = self.sbs(sc, "gateb", [128, 16 * 512], BF16)
        ubf = self.sbs(sc, "ubf", [NS, 512], BF16)
        diag = self.sbs(sc, "diag", [128, 64 * 128], BF16)
        ylsb = self.sbs(sc, "ylsb", [8, TSC], F32)
        yrow = self.sbs(sc, "yrow", [1, 512], F32)
        m1s = self.sbs(sc, "m1s", [128, 2 * 512], F32)
        t2s = self.sbs(sc, "t2s", [128, 512], F32)
        sgb = self.sbs(sc, "sgb", [128, 2 * 512], BF16)
        outst = self.sbs(sc, "outst", [128, 4 * 1024], F32)
        identb = A["identb"]

        self.op("vector", (lambda v, xhb=xhb: v.memset(
            xhb[:, :].rearrange("p (s c) -> p s c", s=16)[:, :, 0:4], 0.0)))
        self.op("vector", lambda v: v.memset(A["hcur"][0:NS, 0:1], 0.0))
        for f in range(16):
            for k in range(4):
                self.op("vector", (lambda v, f=f, k=k, diag=diag, identb=identb: v.tensor_scalar(
                    diag[:, (f * 4 + k) * 128:(f * 4 + k) * 128 + 128],
                    identb[:, :], A["convw_c"][:, k * 16 + f:k * 16 + f + 1], None,
                    op0=ALU.mult)), incs=[cv] if (f == 15 and k == 3) else [])
        diag_done = cv.mark()

        xc_free = None
        gtok_free = None
        outst_free = None
        woutt_free = [None, None]
        winb_free = [None] * 4
        xz_free = [None, None]
        conv_free = [None, None]
        sg_free = [None, None]
        prev_convmm = None
        prev_oproj = None
        prev_comb = None
        for ch in range(NCHK):
            t0 = ch * CW
            self.op("sync", (lambda s, t0=t0, p=p, xc=xc: s.dma_start(
                out=xc[:, :].rearrange("p (n f) -> p n f", n=4),
                in_=self.d["L0_attn_x"][p, t0:t0 + CW, :].rearrange("(n p) f -> p n f", p=128))),
                reads=[xc_free], incs=[d_xc])
            xld = d_xc.mark()
            for sub in range(4):
                self.op("sync", (lambda s, t0=t0, sub=sub, gtok=gtok: s.dma_start(
                    out=gtok[:, sub * 1024:(sub + 1) * 1024],
                    in_=self.gscrd[:, t0 + sub * 128:t0 + (sub + 1) * 128], transpose=True)),
                    reads=[gtok_free] if sub == 0 else [], incs=[d_gt])
            gld = d_gt.mark()
            xn_subs = []
            for sub in range(4):
                self.ln_tok(xc[:, sub * 1024:(sub + 1) * 1024],
                            xntc[:, sub * 1024:(sub + 1) * 1024], sub % 2,
                            extra_reads=[xld])
                xn_subs.append(cv.mark())
            xn_done = cv.mark()
            pt_free = [prev_comb, prev_comb]
            for sub in range(4):
                psb = self.bankb(4 + sub % 2)
                for cc2 in range(8):
                    self.op("tensor", (lambda t, sub=sub, cc2=cc2, psb=psb, xntc=xntc, identb=identb:
                            t.transpose(
                                psb[:, cc2 * 128:(cc2 + 1) * 128],
                                xntc[:, sub * 1024 + cc2 * 128:sub * 1024 + (cc2 + 1) * 128],
                                identb[:, :])),
                            reads=[xn_subs[sub], pt_free[sub % 2]] if cc2 == 0 else [],
                            incs=[cp] if cc2 == 7 else [])
                def evt(v, sub=sub, psb=psb, xnfc=xnfc):
                    last = None
                    for cc2 in range(8):
                        last = v.tensor_copy(
                            xnfc[:, cc2 * 512 + sub * 128:cc2 * 512 + (sub + 1) * 128],
                            psb[:, cc2 * 128:(cc2 + 1) * 128])
                    return last
                self.op("vector", evt, reads=[cp.mark()], incs=[cv])
                pt_free[sub % 2] = cv.mark()
            xnf_done = cv.mark()
            for f in range(32):
                wb = f % 4
                self.op("gpsimd", (lambda g, f=f, wb=wb, winb=winb: g.dma_start(
                    out=winb[:, wb * 1024:wb * 1024 + 1024].rearrange("p (k f) -> p k f", k=8),
                    in_=self.wintd[:, f * 128:(f + 1) * 128].rearrange("(k p) f -> p k f", p=128))),
                    reads=[winb_free[wb]], incs=[d_wb[wb]])
                wld = d_wb[wb].mark()
                bsel = f % 2
                ps = self.bank(bsel)
                for k in range(8):
                    self.op("tensor", (lambda t, f=f, k=k, ps=ps, wb=wb, winb=winb, xnfc=xnfc:
                            t.matmul(
                                ps[:, 0:512], winb[:, wb * 1024 + k * 128:wb * 1024 + k * 128 + 128],
                                xnfc[:, k * 512:(k + 1) * 512],
                                start=(k == 0), stop=(k == 7))),
                            reads=[xnf_done, wld, xz_free[bsel]] if k == 0 else [],
                            incs=[cp] if k == 7 else [])
                winb_free[wb] = cp.mark()
                if f < 16:
                    self.op("vector", (lambda v, f=f, ps=ps, xhb=xhb: v.tensor_scalar(
                        xhb[:, f * (CW + 4) + 4:f * (CW + 4) + 4 + CW],
                        ps[:, 0:512], A["bin_c"][:, f:f + 1], None, op0=ALU.add)),
                        reads=[cp.mark(), prev_convmm], incs=[cv])
                else:
                    self.op("vector", (lambda v, f=f, ps=ps, gateb=gateb: v.tensor_scalar(
                        gateb[:, (f - 16) * 512:(f - 16) * 512 + 512],
                        ps[:, 0:512], A["bin_c"][:, f:f + 1], None, op0=ALU.add)),
                        reads=[cp.mark()], incs=[cv])
                xz_free[bsel] = cv.mark()
            xz_done = cv.mark()
            for f in range(16):
                bsel = 2 + f % 2
                ps = self.bank(bsel)
                for k in range(4):
                    self.op("tensor", (lambda t, f=f, k=k, ps=ps, diag=diag, xhb=xhb: t.matmul(
                        ps[:, 0:512], diag[:, (f * 4 + k) * 128:(f * 4 + k) * 128 + 128],
                        xhb[:, f * (CW + 4) + 1 + k:f * (CW + 4) + 1 + k + CW],
                        start=(k == 0), stop=(k == 3))),
                        reads=([xz_done, diag_done, conv_free[f % 2]] if k == 0 else []),
                        incs=[cp] if k == 3 else [])
                self.op("scalar", (lambda a, f=f, ps=ps, xhs=xhs: a.activation(
                    xhs[:, f * 512:(f + 1) * 512], ps[:, 0:512], AF.Silu,
                    bias=A["convb_c"][:, f:f + 1])),
                    reads=[cp.mark(), prev_oproj], incs=[ca])
                conv_free[f % 2] = ca.mark()
            conv_done = ca.mark()
            prev_convmm = cp.mark()
            if ch < NCHK - 1:
                self.op("vector", (lambda v, xhb=xhb: v.tensor_copy(
                    xhb[:, :].rearrange("p (s c) -> p s c", s=16)[:, :, 0:4],
                    xhb[:, :].rearrange("p (s c) -> p s c", s=16)[:, :, CW:CW + 4])),
                    reads=[prev_convmm])
            pu = self.bank(6)
            for k in range(16):
                self.op("tensor", (lambda t, k=k, pu=pu, xhs=xhs: t.matmul(
                    pu[0:NS, 0:512], A["wssmt"][:, k * NS:(k + 1) * NS],
                    xhs[:, k * 512:(k + 1) * 512], start=(k == 0), stop=(k == 15))),
                    reads=[conv_done, cv.mark()] if k == 0 else [],
                    incs=[cp] if k == 15 else [])
            self.op("vector", (lambda v, pu=pu, ubf=ubf: v.tensor_scalar(
                ubf[0:NS, 0:512], pu[0:NS, 0:512], A["bssm_c"][0:NS, 0:1], None,
                op0=ALU.add)), reads=[cp.mark()], incs=[cv])
            u_done = cv.mark()
            ph = self.bank(6)
            for j in range(TSC):
                self.op("tensor", (lambda t, j=j, ph=ph, ubf=ubf: t.matmul(
                    ph[0:NS, 0:8],
                    A["pbext"][0:NS, (TSC - 1 - j) * NS:(TSC - j) * NS],
                    ubf[0:NS, j:512:TSC], start=(j == 0), stop=(j == TSC - 1))),
                    reads=[u_done] if j == 0 else [],
                    incs=[cp] if j == TSC - 1 else [])
            self.op("vector", (lambda v, ph=ph: v.tensor_copy(A["hdsb"][0:NS, 0:8], ph[0:NS, 0:8])),
                    reads=[cp.mark()])
            self.op("vector", (lambda v: v.tensor_copy(A["Hsb"][0:NS, 0:1], A["hcur"][0:NS, 0:1])),
                    incs=[cv])
            hd_done = cv.mark()
            py = self.bank(7)
            for j in range(TSC):
                if j < 8:
                    self.op("tensor", (lambda t, j=j, ph=ph: t.matmul(
                        ph[0:NS, 8 + (j % 2) * 2:9 + (j % 2) * 2], A["q64f"][0:NS, 0:NS],
                        A["Hsb"][0:NS, j:j + 1], start=True, stop=False)),
                        reads=[hd_done] if j == 0 else [cv.mark()])
                    self.op("tensor", (lambda t, j=j, ph=ph: t.matmul(
                        ph[0:NS, 8 + (j % 2) * 2:9 + (j % 2) * 2], A["i32f"][0:NS, 0:NS],
                        A["hdsb"][0:NS, j:j + 1], start=False, stop=True)),
                        incs=[cp])
                    self.op("vector", (lambda v, j=j, ph=ph: v.tensor_copy(
                        A["Hsb"][0:NS, j + 1:j + 2], ph[0:NS, 8 + (j % 2) * 2:9 + (j % 2) * 2])),
                        reads=[cp.mark()], incs=[cv])
                self.op("tensor", (lambda t, j=j, py=py, ubf=ubf: t.matmul(
                    py[0:8, 0:TSC], ubf[0:NS, j:512:TSC],
                    A["k0ext"][0:NS, TSC - j:2 * TSC - j], start=(j == 0), stop=False)),
                    reads=[u_done, cv.mark()] if j == 0 else [])
            self.op("vector", (lambda v: v.tensor_copy(A["HBb"][0:NS, 0:8], A["Hsb"][0:NS, 0:8])),
                    reads=[cv.mark()], incs=[cv])
            self.op("vector", (lambda v: v.tensor_copy(A["hcur"][0:NS, 0:1], A["Hsb"][0:NS, 8:9])))
            self.op("tensor", (lambda t, py=py: t.matmul(
                py[0:8, 0:TSC], A["HBb"][0:NS, 0:8], A["w2t"][0:NS, 0:TSC],
                start=False, stop=True)), reads=[cv.mark()], incs=[cp])
            self.op("vector", (lambda v, py=py, ylsb=ylsb: v.tensor_copy(
                ylsb[0:8, 0:TSC], py[0:8, 0:TSC])),
                reads=[cp.mark()], incs=[cv])
            self.op("sync", (lambda s, ylsb=ylsb, yrow=yrow: s.dma_start(
                out=yrow[0:1, 0:512].rearrange("o (c t) -> o c t", c=8),
                in_=ylsb[0:8, 0:TSC])),
                reads=[cv.mark()], incs=[d_yr])
            pyb = self.bank(2)
            self.op("tensor", (lambda t, pyb=pyb, yrow=yrow: t.matmul(
                pyb[:, 0:512], A["ones1f"][0:1, 0:128], yrow[0:1, 0:512],
                start=True, stop=True)),
                reads=[d_yr.mark(), conv_free[0], conv_free[1]], incs=[cp])
            ybc_done = cp.mark()
            for f in range(16):
                b = f % 2
                self.op("scalar", (lambda a, f=f, b=b, sgb=sgb, gateb=gateb: a.activation(
                    sgb[:, b * 512:(b + 1) * 512], gateb[:, f * 512:(f + 1) * 512],
                    AF.Silu)),
                    reads=[sg_free[b]], incs=[ca])
                self.op("vector", (lambda v, f=f, pyb=pyb, m1s=m1s: v.tensor_scalar(
                    m1s[:, 0:512], pyb[:, 0:512], A["dv_c"][:, f:f + 1], None,
                    op0=ALU.add)), reads=[ybc_done])
                self.op("vector", (lambda v, f=f, m1s=m1s, t2s=t2s, xhs=xhs: v.tensor_mul(
                    t2s[:, 0:512], m1s[:, 0:512], xhs[:, f * 512:(f + 1) * 512])))
                self.op("vector", (lambda v, f=f, b=b, t2s=t2s, xhs=xhs, sgb=sgb: v.tensor_mul(
                    xhs[:, f * 512:(f + 1) * 512], t2s[:, 0:512],
                    sgb[:, b * 512:(b + 1) * 512])),
                    reads=[ca.mark()], incs=[cv])
                sg_free[b] = cv.mark()
            yfin_done = cv.mark()
            for sub in range(4):
                for half in range(2):
                    bsel = sub * 2 + half
                    ps = self.bank(bsel)
                    self.op("tensor", (lambda t, half=half, ps=ps: t.matmul(
                        ps[:, 0:512], A["ones1b"][0:1, :],
                        A["bout_rowb"][0:1, half * 512:(half + 1) * 512],
                        start=True, stop=False)),
                        reads=([yfin_done, xz_free[0], xz_free[1]]
                               if (sub == 0 and half == 0) else []))
            for hk in range(16):
                b = hk % 2
                self.op("gpsimd", (lambda g, hk=hk, b=b, woutt_b=woutt_b: g.dma_start(
                    out=woutt_b[:, b * 1024:(b + 1) * 1024],
                    in_=self.wouttd[hk * 128:(hk + 1) * 128, :])),
                    reads=[woutt_free[b]], incs=[d_wo[b]])
                woutt_ld = d_wo[b].mark()
                for sub in range(4):
                    for half in range(2):
                        bsel = sub * 2 + half
                        ps = self.bank(bsel)
                        first = (hk == 0 and sub == 0 and half == 0)
                        self.op("tensor", (lambda t, hk=hk, b=b, sub=sub, half=half, ps=ps,
                                           xhs=xhs, woutt_b=woutt_b:
                                t.matmul(ps[:, 0:512],
                                         xhs[:, hk * 512 + sub * 128:hk * 512 + (sub + 1) * 128],
                                         woutt_b[:, b * 1024 + half * 512:b * 1024 + (half + 1) * 512],
                                         start=False, stop=(hk == 15))),
                                reads=([yfin_done, woutt_ld, xz_free[0], xz_free[1]]
                                       if first else ([woutt_ld] if (sub == 0 and half == 0) else [])),
                                incs=[cp] if (sub == 3 and half == 1) else [])
                woutt_free[b] = cp.mark()
            oproj_done = cp.mark()
            prev_oproj = oproj_done
            for sub in range(4):
                for half in range(2):
                    bsel = sub * 2 + half
                    ps = self.bank(bsel)
                    self.op("vector", (lambda v, sub=sub, half=half, ps=ps, m1s=m1s, xntc=xntc:
                            v.tensor_add(
                        m1s[:, 512:1024], ps[:, 0:512],
                        xntc[:, sub * 1024 + half * 512:sub * 1024 + (half + 1) * 512])),
                        reads=[oproj_done, gld, outst_free] if (sub == 0 and half == 0) else [])
                    self.op("vector", (lambda v, sub=sub, half=half, t2s=t2s, m1s=m1s, gtok=gtok:
                            v.tensor_mul(
                                t2s[:, 0:512], m1s[:, 512:1024],
                                gtok[:, sub * 1024 + half * 512:sub * 1024 + (half + 1) * 512])))
                    self.op("vector", (lambda v, sub=sub, half=half, outst=outst, t2s=t2s, xc=xc:
                            v.tensor_add(
                                outst[:, sub * 1024 + half * 512:sub * 1024 + (half + 1) * 512],
                                t2s[:, 0:512],
                                xc[:, sub * 1024 + half * 512:sub * 1024 + (half + 1) * 512])),
                            incs=[cv] if half == 1 else [])
            comb_done = cv.mark()
            prev_comb = comb_done
            xz_free = [comb_done, comb_done]
            conv_free = [comb_done, comb_done]
            self.op("sync", (lambda s, t0=t0, p=p, outst=outst: s.dma_start(
                out=self.outd[p, t0:t0 + CW, :].rearrange("(n p) f -> p n f", p=128),
                in_=outst[:, :].rearrange("p (n f) -> p n f", n=4))),
                reads=[comb_done], incs=[d_out])
            outst_free = d_out.mark()
            xc_free = comb_done
            gtok_free = comb_done

        tapsrc = {"xnf": xnfc, "xh": xhb, "gate": gateb, "u": ubf, "yrowt": yrow, "yfin": xhs}
        for nm, hnd in tapsrc.items():
            if nm in self.taps:
                sh = list(hnd.shape)
                dt = hnd.ap().dtype
                tpp = self.mktap(nm, [self.npass] + sh, dt)
                n = sh[0]
                self.op("sync", (lambda s, tpp=tpp, hnd=hnd, n=n, p=p: s.dma_start(
                    out=tpp[p][0:n, :], in_=hnd[0:n, :])),
                    reads=[cv.mark(), ca.mark()], incs=[d_tap])



_BUILD_CACHE = {}


def get_build(npass=4, depth=6, taps=()):
    key = (npass, depth, tuple(sorted(taps)))
    if key not in _BUILD_CACHE:
        _BUILD_CACHE[key] = KB(npass, depth, taps).build()
    return _BUILD_CACHE[key]


def kernel(**inputs):
    kb = get_build(4, 6)
    x = np.ascontiguousarray(inputs["L0_attn_x"])
    in_maps = []
    for c in range(8):
        m = {k: np.ascontiguousarray(v) for k, v in inputs.items() if k != "L0_attn_x"}
        m["L0_attn_x"] = np.ascontiguousarray(x[c * 4:(c + 1) * 4])
        in_maps.append(m)
    res = run_bass_kernel_spmd(kb.nc, in_maps, core_ids=list(range(8)))
    return np.concatenate([r["out"] for r in res.results], axis=0)


# version 23
# speedup vs baseline: 1.0515x; 1.0312x over previous
import contextlib
import numpy as np
import ml_dtypes
import concourse.bass as bass
import concourse.mybir as mybir
from concourse.bass_utils import run_bass_kernel_spmd

F32 = mybir.dt.float32
BF16 = mybir.dt.bfloat16
AF = mybir.ActivationFunctionType
ALU = mybir.AluOpType
AX = mybir.AxisListType

S = 2048
D = 1024
H2 = 2048
NS = 32
TSC = 64
CW = 512
NCHK = S // CW
WIN = 256
HEADS = 16
DH = 64
DI = 2730
MSK = -1.0e30
GCOL = 2049

ENGINES = ("tensor", "vector", "scalar", "gpsimd", "sync")


class Chan:
    __slots__ = ("name", "sem", "count", "step")

    def __init__(self, name, sem, step):
        self.name = name
        self.sem = sem
        self.count = 0
        self.step = step

    def mark(self):
        return (self, self.count)


class KB:

    def __init__(self, npass=4, depth=6, taps=()):
        self.npass = npass
        self.depth = depth
        self.taps = set(taps)
        self.nc = bass.Bass("TRN2", target_bir_lowering=False)
        self.st = contextlib.ExitStack()
        self.ops = {e: [] for e in ENGINES}
        self._waited = {e: {} for e in ENGINES}
        self.chans = []
        self.A = {}
        self.tap_d = {}

    def chan(self, name, step=1):
        sem = self.st.enter_context(self.nc.semaphore(name))
        c = Chan(name, sem, step)
        self.chans.append(c)
        return c

    def op(self, engine, fn, reads=(), incs=()):
        waits = []
        w = self._waited[engine]
        for rd in reads:
            if rd is None:
                continue
            c, cnt = rd
            if c is None or cnt <= 0:
                continue
            if w.get(c.name, 0) < cnt:
                w[c.name] = cnt
                waits.append((c, cnt))
        incl = []
        selfw = None
        auto = {"vector": self.cv, "scalar": self.ca}.get(engine)
        if auto is not None:
            selfw = (auto, auto.count)
            auto.count += auto.step
            incl.append((auto, auto.step))
            for c in incs:
                if c is auto:
                    continue
                c.count += c.step
                incl.append((c, c.step))
        else:
            for c in incs:
                c.count += c.step
                incl.append((c, c.step))
        self.ops[engine].append((fn, waits, incl, selfw))

    def fence(self, engine):
        rd = [(c, c.count) for c in self.chans if c.count > 0]
        if rd:
            self.op(engine, lambda e: None, reads=rd)

    def bar(self, engines=ENGINES):
        rd = [(c, c.count) for c in self.chans if c.count > 0]
        for e in engines:
            self.op(e, lambda eng: None, reads=rd)

    def emit(self):
        for c in self.chans:
            assert c.count < 60000, f"chan {c.name} count {c.count} too high"
        with self.nc.Block() as block:
            def run(eng_obj, name):
                for (fn, waits, incl, selfw) in self.ops[name]:
                    if selfw is not None and selfw[1] > 0:
                        eng_obj.wait_ge(selfw[0].sem, selfw[1])
                    for (c, cnt) in waits:
                        eng_obj.wait_ge(c.sem, cnt)
                    ins = fn(eng_obj)
                    if ins is None and incl:
                        ins = eng_obj.nop()
                    for i, (c, stp) in enumerate(incl):
                        if i == 0:
                            ins.then_inc(c.sem, stp)
                        else:
                            eng_obj.nop().then_inc(c.sem, stp)
            block.tensor(lambda t: run(t, "tensor"))
            block.vector(lambda v: run(v, "vector"))
            block.scalar(lambda a: run(a, "scalar"))
            block.gpsimd(lambda g: run(g, "gpsimd"))
            block.sync(lambda s: run(s, "sync"))

    def sb(self, name, shape, dtype):
        t = self.nc.alloc_sbuf_tensor(name, shape, dtype)
        self.A[name] = t
        return t

    _uid = 0

    def sbs(self, scope, name, shape, dtype, side=None):
        KB._uid += 1
        t = scope.enter_context(self.nc.sbuf_tensor(f"{name}__{KB._uid}", shape, dtype, side=side))
        self.A[name] = t
        return t

    def bank(self, i):
        return self.PB[i].ap()

    def bankb(self, i):
        return self.PB[i].ap().bitcast(BF16)

    def mktap(self, name, shape, dtype=F32):
        if name in self.taps and name not in self.tap_d:
            self.tap_d[name] = self.nc.dram_tensor("tap_" + name, shape, dtype,
                                                   kind="ExternalOutput").ap()
        return self.tap_d.get(name)

    def build(self):
        nc = self.nc
        d = {}
        specs = [
            ("L0_attn_x", [4, S, D]), ("L0_attn_W_in", [2 * H2, D]), ("L0_attn_b_in", [2 * H2]),
            ("L0_attn_conv_w", [H2, 1, 4]), ("L0_attn_conv_b", [H2]),
            ("L0_attn_W_ssm", [NS, H2]), ("L0_attn_b_ssm", [NS]), ("L0_attn_A", [NS, NS]),
            ("L0_attn_Bv", [NS]), ("L0_attn_Cv", [NS]), ("L0_attn_Dv", [H2]),
            ("L0_attn_W_out", [D, H2]), ("L0_attn_b_out", [D]),
            ("L0_ffn_Wqkv", [6, D, 3 * D]), ("L0_ffn_q_scale", [6, DH]), ("L0_ffn_k_scale", [6, DH]),
            ("L0_ffn_Wo", [6, D, D]), ("L0_ffn_ln_g", [6, D]), ("L0_ffn_ln_b", [6, D]),
            ("L0_ffn_W1", [6, D, 2 * DI]), ("L0_ffn_b1", [6, 2 * DI]),
            ("L0_ffn_W2", [6, DI, D]), ("L0_ffn_b2", [6, D]),
        ]
        for (n, sh) in specs:
            d[n] = nc.dram_tensor(n, sh, F32, kind="ExternalInput").ap()
        self.d = d
        self.outd = nc.dram_tensor("out", [4, S, D], F32, kind="ExternalOutput").ap()
        self.wintd = nc.dram_tensor("wint_scr", [D, 2 * H2], BF16, kind="Internal").ap()
        self.wouttd = nc.dram_tensor("woutt_scr", [H2, D], BF16, kind="Internal").ap()
        self.gscrd = nc.dram_tensor("g_scr", [D, S], BF16, kind="Internal").ap()
        self.w1cd = nc.dram_tensor("w1_cache", [6, 128, 44, 8, 128], BF16, kind="Internal").ap()
        self.w2cd = nc.dram_tensor("w2_cache", [6, 22, 128, 1024], BF16, kind="Internal").ap()
        self.wqcd = nc.dram_tensor("wq_cache", [6, 2, 128, 4 * 3072], BF16, kind="Internal").ap()
        self.wocd = nc.dram_tensor("wo_cache", [6, 128, 8 * 1024], BF16, kind="Internal").ap()
        self.wb_marks = {}
        self.w1_slot_guard = {i: [] for i in range(6)}
        self.w2_slot_guard = {i: [] for i in range(3)}
        self.wq_guard = []
        self.wo_guard = []

        self.PB = [self.nc.alloc_psum_tensor(f"pbank{i}", [128, 512], F32) for i in range(8)]

        self.cp = self.chan("cp")
        self.cv = self.chan("cv")
        self.ca = self.chan("ca")
        self.cg = self.chan("cg")
        self.d_cst = self.chan("d_cst", 16)
        self.d_row = self.chan("d_row", 16)
        self.d_wst = self.chan("d_wst", 16)
        self.d_tst = self.chan("d_tst", 16)
        self.d_af = self.chan("d_af", 16)
        self.d_xa = [self.chan("d_xa0", 16), self.chan("d_xa1", 16)]
        self.d_g = [self.chan("d_g0", 16), self.chan("d_g1", 16)]
        self.d_xc = self.chan("d_xc", 16)
        self.d_gt = self.chan("d_gt", 16)
        self.d_wb = [self.chan(f"d_wb{i}", 16) for i in range(4)]
        self.d_wc = self.chan("d_wc", 16)
        self.d_wo = [self.chan("d_wo0", 16), self.chan("d_wo1", 16)]
        self.d_yr = self.chan("d_yr", 16)
        self.d_out = self.chan("d_out", 16)
        self.d_tap = self.chan("d_tap", 16)
        self.d_rot = [self.chan("d_rot0", 16), self.chan("d_rot1", 16)]
        self.d_wq = self.chan("d_wq", 16)
        self.d_woL = self.chan("d_woL", 16)
        self.d_br = [self.chan("d_br0", 16), self.chan("d_br1", 16)]
        self.d_w1 = [self.chan(f"d_w1{i}", 16) for i in range(6)]
        self.d_w2 = [self.chan("d_w20", 16), self.chan("d_w21", 16), self.chan("d_w22", 16)]
        self.store_chans = [self.d_tst, self.d_g[0], self.d_g[1], self.d_out, self.d_tap]
        self.ticks = {e: self.chan("tk_" + e) for e in ENGINES}

        self.consts_setup()
        self.globals_alloc()
        with contextlib.ExitStack() as sc0:
            self.phase0(sc0)
        self.bar()
        for p in range(self.npass):
            self.cp = self.chan(f"cp{p}")
            self.cv = self.chan(f"cv{p}")
            self.ca = self.chan(f"ca{p}")
            self.cg = self.chan(f"cg{p}")
            with contextlib.ExitStack() as scAB:
                self.scAB = scAB
                self.wq_free = None
                self.phase_A(p)
                self.bar()
                for l in range(self.depth):
                    with contextlib.ExitStack() as scL:
                        self.scL = scL
                        self.layer_attn(p, l)
                    self.bar()
                    with contextlib.ExitStack() as scL:
                        self.scL = scL
                        self.layer_ffn(p, l)
                    self.bar()
                self.g_store(p)
                self.bar()
            with contextlib.ExitStack() as scC:
                self.scC = scC
                self.phase_C(p)
                self.bar()
        self.op("sync", lambda s: None, reads=[c.mark() for c in self.store_chans])
        self.emit()
        self.st.close()
        return self

    def consts_setup(self):
        half = np.arange(0, DH, 2, dtype=np.float64)
        inv_freq = 1.0 / 10000.0 ** (half / DH)
        t = np.arange(S, dtype=np.float64)
        freqs = np.concatenate([t[:, None] * inv_freq[None, :]] * 2, -1)
        base_scale = (half + 0.4 * DH) / (1.4 * DH)
        power = (t - S // 2) / (WIN // 2)
        scale = np.concatenate([base_scale[None, :] ** power[:, None]] * 2, -1)
        cos, sin = np.cos(freqs), np.sin(freqs)
        sgn = np.where(np.arange(DH) < DH // 2, -1.0, 1.0)[None, :]
        def dup(tab):
            return np.ascontiguousarray(np.concatenate([tab, tab], 0).astype(np.float32))
        CS = dup((cos * scale).T)
        SS = dup((sin * scale * sgn).T)
        CI = dup((cos / scale).T)
        SI = dup((sin / scale * sgn).T)
        kp = np.arange(128)[:, None]
        qf = np.arange(256)[None, :]
        sTm = []
        for m in range(4):
            rel = m * 128 + kp - 256
            valid = (qf - rel >= 0) & (qf - rel <= WIN)
            sTm.append(np.where(valid, 0.0, MSK).astype(ml_dtypes.bfloat16))
        qp = np.arange(128)[:, None]
        s1m = []
        for qc in range(2):
            cols = []
            for mi in range(3):
                m = qc + mi
                kpp = np.arange(128)[None, :]
                rel = m * 128 + kpp - 256
                qg = qc * 128 + qp
                valid = (qg - rel >= 0) & (qg - rel <= WIN)
                cols.append(np.where(valid, 0.0, MSK))
            s1m.append(np.concatenate(cols, 1).astype(ml_dtypes.bfloat16))
        pswap = np.zeros((128, 128), dtype=ml_dtypes.bfloat16)
        for dd in range(128):
            blk = (dd // 64) * 64
            pswap[blk + ((dd - blk + 32) % 64), dd] = 1.0
        bones = np.zeros((128, 128), dtype=ml_dtypes.bfloat16)
        bones[0:64, 0:64] = 1.0
        bones[64:128, 64:128] = 1.0
        onesfull = np.ones((128, 128), dtype=ml_dtypes.bfloat16)
        self.consts = {
            "identb": np.eye(128, dtype=ml_dtypes.bfloat16),
            "onesfull": onesfull,
            "bones": bones, "pswap": pswap,
            "onescolb": np.ones((128, 1), dtype=ml_dtypes.bfloat16),
            "ones1b": np.ones((1, 128), dtype=ml_dtypes.bfloat16),
            "ones1f": np.ones((1, 128), dtype=np.float32),
            "one11f": np.ones((1, 1), dtype=np.float32),
            "i32f": np.eye(NS, dtype=np.float32),
            "sTm01": np.ascontiguousarray(np.concatenate([sTm[0], sTm[1]], 1)),
            "sTm23": np.ascontiguousarray(np.concatenate([sTm[2], sTm[3]], 1)),
        }
        self.cdram = {k: self.nc.inline_tensor(np.ascontiguousarray(v), name="c_" + k)
                      for k, v in self.consts.items()}
        rot4 = np.concatenate([t.astype(ml_dtypes.bfloat16)[None] for t in (CS, SS, CI, SI)], 0)
        self.rot4d = self.nc.inline_tensor(np.ascontiguousarray(rot4), name="c_rot4").ap()

    def globals_alloc(self):
        for k, v in self.consts.items():
            dt = BF16 if v.dtype == ml_dtypes.bfloat16 else F32
            self.sb(k, list(v.shape), dt)
        for (n, sh) in [("bin_c", [128, 32]), ("convw_c", [128, 64]), ("convb_c", [128, 16]),
                        ("dv_c", [128, 16]), ("bssm_c", [NS, 1]), ("bv_c", [NS, 1]),
                        ("cv_c", [NS, 1]), ("lng_c", [128, 48]),
                        ("lnb_c", [128, 48]), ("b1_c", [128, 264]), ("b2_c", [128, 48]),
                        ("qs_c", [128, 6]), ("ks_c", [128, 6]), ("q64f", [NS, NS]),
                        ("a_f", [NS, NS]), ("wchain", [NS, TSC + 1]), ("hcur", [NS, 1]),
                        ("hdsb", [NS, 8]), ("Hsb", [NS, 9]), ("lncol", [128, 16]), ("epsc", [128, 1]), ("epsq", [128, 1])]:
            self.sb(n, sh, F32)
        self.sb("bout_rowb", [1, D], BF16)
        for (n, sh) in [("wssmt", [128, 16 * NS]), ("k0ext", [NS, 2 * TSC]),
                        ("pbext", [NS, TSC * NS]), ("w2t", [NS, TSC]), ("HBb", [NS, 8])]:
            self.sb(n, sh, BF16)

    def phase0(self, sc0):
        A, d = self.A, self.d
        cp, cv, ca = self.cp, self.cv, self.ca
        d_cst, d_row, d_wst, d_tst = self.d_cst, self.d_row, self.d_wst, self.d_tst
        self.op("vector", lambda v: v.memset(A["epsc"][:, 0:1], 1e-5))
        self.op("vector", lambda v: v.memset(A["epsq"][:, 0:1], 1e-24))
        for k in self.consts:
            src = self.cdram[k].ap()
            npart = self.consts[k].shape[0]
            self.op("gpsimd", (lambda g, a=A[k], s=src, n=npart: g.dma_start(out=a[0:n, :], in_=s)),
                    incs=[d_cst])
        const_ld = d_cst.mark()
        rb = self.sbs(sc0, "rowbuf", [1, 8192], F32)
        qtmp = self.sbs(sc0, "qtmp", [NS, 2 * NS], F32)

        def load_row(dst, src):
            self.op("sync", (lambda s, dd=dst, ii=src: s.dma_start(out=dd, in_=ii)),
                    reads=[cp.mark()], incs=[d_row])

        def colz(dst_fn, srcs, psel=0):
            ps = self.bank(psel)
            for j, src in enumerate(srcs):
                w = src.shape[-1]
                self.op("tensor", (lambda t, ap=src, w=w, j=j, ps=ps: t.matmul(
                    ps[0:w, j:j + 1], ap, A["one11f"][0:1, 0:1], start=True, stop=True)),
                    reads=[d_row.mark(), cv.mark(), const_ld] if j == 0 else [],
                    incs=[cp] if j == len(srcs) - 1 else [])
            n = len(srcs)
            widths = [s.shape[-1] for s in srcs]
            def ev(v, n=n, ps=ps, widths=widths):
                last = None
                for j in range(n):
                    dst = dst_fn(j)
                    last = v.tensor_copy(dst, ps[0:widths[j], j:j + 1])
                return last
            self.op("vector", ev, reads=[cp.mark()], incs=[cv])

        load_row(rb[0:1, 0:4096], d["L0_attn_b_in"].rearrange("(o f) -> o f", o=1))
        colz(lambda j: A["bin_c"][:, j:j + 1], [rb[0:1, j * 128:(j + 1) * 128] for j in range(32)])
        load_row(rb[0:1, 0:8192], d["L0_attn_conv_w"].rearrange("c o k -> o (c k)"))
        for k in range(4):
            colz((lambda j, k=k: A["convw_c"][:, k * 16 + j:k * 16 + j + 1]),
                 [rb[0:1, (j * 512 + k):(j * 512 + k) + 509:4] for j in range(16)], psel=k % 2)
        load_row(rb[0:1, 0:2048], d["L0_attn_conv_b"].rearrange("(o f) -> o f", o=1))
        colz(lambda j: A["convb_c"][:, j:j + 1], [rb[0:1, j * 128:(j + 1) * 128] for j in range(16)])
        load_row(rb[0:1, 0:2048], d["L0_attn_Dv"].rearrange("(o f) -> o f", o=1))
        colz(lambda j: A["dv_c"][:, j:j + 1], [rb[0:1, j * 128:(j + 1) * 128] for j in range(16)], psel=1)
        load_row(rb[0:1, 0:32], d["L0_attn_b_ssm"].rearrange("(o f) -> o f", o=1))
        load_row(rb[0:1, 32:64], d["L0_attn_Bv"].rearrange("(o f) -> o f", o=1))
        load_row(rb[0:1, 64:96], d["L0_attn_Cv"].rearrange("(o f) -> o f", o=1))
        colz(lambda j: [A["bssm_c"], A["bv_c"], A["cv_c"]][j][0:NS, 0:1],
             [rb[0:1, 0:NS], rb[0:1, 32:64], rb[0:1, 64:96]])
        load_row(rb[0:1, 0:1024], d["L0_attn_b_out"].rearrange("(o f) -> o f", o=1))
        self.op("vector", (lambda v: v.tensor_copy(A["bout_rowb"][0:1, 0:D], rb[0:1, 0:1024])),
                reads=[d_row.mark()], incs=[cv])
        self.op("tensor", lambda t: t.nop(), reads=[cv.mark()], incs=[cp])
        for l in range(6):
            load_row(rb[0:1, 0:1024], d["L0_ffn_ln_g"][l].rearrange("(o f) -> o f", o=1))
            colz((lambda j, l=l: A["lng_c"][:, l * 8 + j:l * 8 + j + 1]),
                 [rb[0:1, j * 128:(j + 1) * 128] for j in range(8)])
            load_row(rb[0:1, 0:1024], d["L0_ffn_ln_b"][l].rearrange("(o f) -> o f", o=1))
            colz((lambda j, l=l: A["lnb_c"][:, l * 8 + j:l * 8 + j + 1]),
                 [rb[0:1, j * 128:(j + 1) * 128] for j in range(8)], psel=1)
            load_row(rb[0:1, 0:1024], d["L0_ffn_b2"][l].rearrange("(o f) -> o f", o=1))
            colz((lambda j, l=l: A["b2_c"][:, l * 8 + j:l * 8 + j + 1]),
                 [rb[0:1, j * 128:(j + 1) * 128] for j in range(8)])
            load_row(rb[0:1, 0:5460], d["L0_ffn_b1"][l].rearrange("(o f) -> o f", o=1))
            srcs = []
            for j in range(22):
                w = min(128, DI - j * 128)
                srcs.append(rb[0:1, j * 128:j * 128 + w])
            for j in range(22):
                w = min(128, DI - j * 128)
                srcs.append(rb[0:1, DI + j * 128:DI + j * 128 + w])
            colz((lambda j, l=l: A["b1_c"][0:(128 if (j % 22) < 21 else DI - 21 * 128),
                                           l * 44 + j:l * 44 + j + 1]), srcs, psel=1)
            load_row(rb[0:1, 0:64], d["L0_ffn_q_scale"][l].rearrange("(o f) -> o f", o=1))
            load_row(rb[0:1, 64:128], d["L0_ffn_k_scale"][l].rearrange("(o f) -> o f", o=1))
            for (nm, off) in (("qs_c", 0), ("ks_c", 64)):
                self.op("tensor", (lambda t, o=off: t.matmul(
                    self.bank(0)[0:64, 0:1], rb[0:1, o:o + 64], A["one11f"][0:1, 0:1],
                    start=True, stop=True)),
                    reads=[d_row.mark(), cv.mark()])
                self.op("tensor", (lambda t, o=off: t.matmul(
                    self.bank(0)[64:128, 0:1], rb[0:1, o:o + 64], A["one11f"][0:1, 0:1],
                    start=True, stop=True)), incs=[cp])
                self.op("vector", (lambda v, nm=nm, l=l: v.tensor_copy(
                    A[nm][:, l:l + 1], self.bank(0)[0:128, 0:1])), reads=[cp.mark()], incs=[cv])

        wstage = self.sbs(sc0, "wstage", [128, 2048], BF16)
        tstage = self.sbs(sc0, "tstage", [128, 2048], BF16)

        def transpose_weight(src2d, rows, cols, dst_dram):
            nrc, ncc = rows // 128, cols // 128
            for rc in range(nrc):
                self.op("gpsimd", (lambda g, rc=rc, src2d=src2d, cols=cols: g.dma_start(
                    out=wstage[:, 0:cols], in_=src2d[rc * 128:(rc + 1) * 128, :])),
                    reads=[cp.mark()], incs=[d_wst])
                ld = d_wst.mark()
                for cc in range(ncc):
                    psb = self.bankb(4) if (cc // 8) % 2 == 0 else self.bankb(5)
                    self.op("tensor", (lambda t, cc=cc, psb=psb: t.transpose(
                        psb[:, (cc % 8) * 128:(cc % 8) * 128 + 128],
                        wstage[:, cc * 128:(cc + 1) * 128], A["identb"][:, :])),
                        reads=[ld, cv.mark()] if cc % 8 == 0 else [],
                        incs=[cp] if (cc % 8 == 7 or cc == ncc - 1) else [])
                    if cc % 8 == 7 or cc == ncc - 1:
                        lo = (cc // 8) * 8
                        def ev(v, lo=lo, hi=cc, psb=psb):
                            last = None
                            for c2 in range(lo, hi + 1):
                                last = v.tensor_copy(tstage[:, c2 * 128:(c2 + 1) * 128],
                                                     psb[:, (c2 % 8) * 128:(c2 % 8) * 128 + 128])
                            return last
                        self.op("vector", ev, reads=[cp.mark(), d_tst.mark()], incs=[cv])
                self.op("sync", (lambda s, rc=rc, dst=dst_dram, ncc=ncc: s.dma_start(
                    out=dst.rearrange("(cc p) r -> p cc r", p=128)[:, :, rc * 128:(rc + 1) * 128],
                    in_=tstage[:, 0:ncc * 128].rearrange("p (cc r) -> p cc r", cc=ncc))),
                    reads=[cv.mark()], incs=[d_tst])

        transpose_weight(d["L0_attn_W_in"], 2 * H2, D, self.wintd)
        transpose_weight(d["L0_attn_W_out"], D, H2, self.wouttd)
        self.op("gpsimd", (lambda g: g.dma_start(out=wstage[0:NS, 0:H2], in_=d["L0_attn_W_ssm"])),
                reads=[cp.mark()], incs=[d_wst])
        wld = d_wst.mark()
        for cc in range(16):
            psb = self.bankb(4) if cc % 2 == 0 else self.bankb(5)
            self.op("tensor", (lambda t, cc=cc, psb=psb: t.transpose(
                psb[:, 0:NS], wstage[0:NS, cc * 128:(cc + 1) * 128], A["identb"][0:NS, 0:NS])),
                reads=[wld, cv.mark()], incs=[cp])
            self.op("vector", (lambda v, cc=cc, psb=psb: v.tensor_copy(
                A["wssmt"][:, cc * NS:(cc + 1) * NS], psb[:, 0:NS])),
                reads=[cp.mark()], incs=[cv])

        self.op("sync", (lambda s: s.dma_start(out=A["a_f"][0:NS, 0:NS], in_=d["L0_attn_A"])),
                incs=[self.d_af])
        ph32 = self.bank(6)
        self.op("vector", lambda v: v.tensor_copy(A["wchain"][0:NS, 0:1], A["cv_c"][0:NS, 0:1]),
                reads=[cv.mark()], incs=[cv])
        for d2 in range(TSC):
            self.op("tensor", (lambda t, d2=d2: t.matmul(
                ph32[0:NS, 0:1], A["a_f"][0:NS, 0:NS], A["wchain"][0:NS, d2:d2 + 1],
                start=True, stop=True)),
                reads=[cv.mark(), self.d_af.mark()], incs=[cp])
            self.op("vector", (lambda v, d2=d2: v.tensor_copy(
                A["wchain"][0:NS, d2 + 1:d2 + 2], ph32[0:NS, 0:1])),
                reads=[cp.mark()], incs=[cv])
        self.op("vector", lambda v: v.memset(A["k0ext"][0:NS, 0:TSC], 0.0))
        self.op("vector", (lambda v: v.tensor_scalar(
            A["k0ext"][0:NS, TSC:2 * TSC], A["wchain"][0:NS, 0:TSC], A["bv_c"][0:NS, 0:1], None,
            op0=ALU.mult)))
        self.op("vector", lambda v: v.tensor_copy(A["w2t"][0:NS, 0:TSC], A["wchain"][0:NS, 1:TSC + 1]))
        self.op("vector", lambda v: v.tensor_copy(qtmp[0:NS, 0:NS], A["i32f"][0:NS, 0:NS]), incs=[cv])
        for e in range(TSC + 1):
            cur = (e % 2) * NS
            nxt = ((e + 1) % 2) * NS
            if e < TSC:
                self.op("vector", (lambda v, e=e, cur=cur: v.tensor_scalar(
                    A["pbext"][0:NS, e * NS:(e + 1) * NS], qtmp[0:NS, cur:cur + NS],
                    A["bv_c"][0:NS, 0:1], None, op0=ALU.mult)))
            if e == TSC:
                self.op("vector", (lambda v, cur=cur: v.tensor_copy(
                    A["q64f"][0:NS, 0:NS], qtmp[0:NS, cur:cur + NS])), incs=[cv])
                break
            self.op("tensor", (lambda t, cur=cur: t.matmul(
                ph32[0:NS, 0:NS], A["a_f"][0:NS, 0:NS], qtmp[0:NS, cur:cur + NS],
                start=True, stop=True)),
                reads=[cv.mark()], incs=[cp])
            self.op("vector", (lambda v, nxt=nxt: v.tensor_copy(
                qtmp[0:NS, nxt:nxt + NS], ph32[0:NS, 0:NS])),
                reads=[cp.mark()], incs=[cv])

    def ln_tok(self, src_ap, dst_ap_bf, parity, extra_reads=()):
        A, cv, ca = self.A, self.cv, self.ca
        col = A["lncol"][:, parity * 8:parity * 8 + 8]
        self.op("vector", (lambda v, s=src_ap, col=col: v.reduce_sum(col[:, 0:1], s, axis=AX.X)),
                reads=list(extra_reads))
        self.op("vector", (lambda v, s=src_ap, col=col: v.scalar_tensor_tensor(
            A["lnsq"][:, 0:1024], s, 1.0, s, op0=ALU.mult, op1=ALU.mult,
            accum_out=col[:, 1:2])))
        self.op("vector", (lambda v, col=col: v.tensor_scalar(col[:, 0:1], col[:, 0:1], 1.0 / D, None, op0=ALU.mult)))
        self.op("vector", (lambda v, col=col: v.tensor_scalar(col[:, 1:2], col[:, 1:2], 1.0 / D, None, op0=ALU.mult)))
        self.op("vector", (lambda v, col=col: v.tensor_scalar(col[:, 2:3], col[:, 0:1], col[:, 0:1], None, op0=ALU.mult)))
        self.op("vector", (lambda v, col=col: v.tensor_sub(col[:, 3:4], col[:, 1:2], col[:, 2:3])), incs=[cv])
        self.op("scalar", (lambda a, col=col: a.activation(col[:, 4:5], col[:, 3:4], AF.Ln, bias=A["epsc"][:, 0:1])),
                reads=[cv.mark()], incs=[ca])
        self.op("scalar", (lambda a, col=col: a.activation(col[:, 5:6], col[:, 4:5], AF.Exp, scale=-0.5)),
                incs=[ca])
        self.op("vector", (lambda v, s=src_ap, dd=dst_ap_bf, col=col: v.tensor_scalar(
            dd, s, col[:, 0:1], col[:, 5:6], op0=ALU.subtract, op1=ALU.mult)),
            reads=[ca.mark()], incs=[cv])

    def phase_A(self, p):
        cp, cv = self.cp, self.cv
        d_xa = self.d_xa
        r = self.sbs(self.scAB, "r", [128, 8 * GCOL], F32, side="right")
        self.scA_local = contextlib.ExitStack()
        xa = self.sbs(self.scA_local, "xa", [128, 2 * 1024], F32)
        xnb = self.sbs(self.scA_local, "xnb", [128, 2 * 1024], BF16)
        self.sbs(self.scA_local, "lnsq", [128, 1024], BF16)
        identb = self.A["identb"]
        self.bar()
        for cc in range(8):
            self.op("vector", (lambda v, cc=cc, r=r: v.memset(r[:, cc * GCOL:cc * GCOL + 1], 0.0)))
        xa_free = [None, None]
        xnb_free = [None, None]
        pt_free = [None, None]
        for tt in range(16):
            b = tt % 2
            self.op("sync", (lambda s, tt=tt, b=b, p=p, xa=xa: s.dma_start(
                out=xa[:, b * 1024:(b + 1) * 1024],
                in_=self.d["L0_attn_x"][p, tt * 128:(tt + 1) * 128, :])),
                reads=[xa_free[b]], incs=[d_xa[b]])
            self.ln_tok(xa[:, b * 1024:(b + 1) * 1024], xnb[:, b * 1024:(b + 1) * 1024],
                        b, extra_reads=[d_xa[b].mark(), xnb_free[b]])
            xa_free[b] = cv.mark()
            psb = self.bankb(4) if b == 0 else self.bankb(5)
            for cc in range(8):
                self.op("tensor", (lambda t, cc=cc, b=b, psb=psb, xnb=xnb, identb=identb: t.transpose(
                    psb[:, cc * 128:(cc + 1) * 128],
                    xnb[:, b * 1024 + cc * 128:b * 1024 + (cc + 1) * 128], identb[:, :])),
                    reads=[cv.mark(), pt_free[b]] if cc == 0 else [],
                    incs=[cp] if cc == 7 else [])
            xnb_free[b] = cp.mark()
            def ev(v, tt=tt, psb=psb, r=r):
                last = None
                for cc in range(8):
                    last = v.tensor_copy(r[:, cc * GCOL + 1 + tt * 128:cc * GCOL + 1 + (tt + 1) * 128],
                                         psb[:, cc * 128:(cc + 1) * 128])
                return last
            self.op("vector", ev, reads=[cp.mark()], incs=[cv])
            pt_free[b] = cv.mark()
        self.r = r
        self.scA_local.close()

    def emit_xw_cast(self, wg, xw, r, scol, xw_free):
        cv = self.cv
        for cc in range(8):
            off = cc * GCOL + 1 + wg * 512 - (1 if cc >= 4 else 0)
            self.op("vector", (lambda v, cc=cc, off=off, xw=xw, r=r: v.tensor_copy(
                xw[:, cc * 512:(cc + 1) * 512], r[:, off:off + 512])),
                reads=[xw_free] if cc == 0 else [])
        if wg > 0:
            for cc in range(4, 8):
                self.op("vector", (lambda v, cc=cc, wg=wg, xw=xw, scol=scol: v.tensor_copy(
                    xw[:, cc * 512:cc * 512 + 1],
                    scol[:, (wg - 1) * 4 + (cc - 4):(wg - 1) * 4 + (cc - 4) + 1])))
        self.op("vector", lambda v: v.nop(), incs=[cv])
        return cv.mark()

    def layer_attn(self, p, l):
        A = self.A
        cp, cv, ca, cg = self.cp, self.cv, self.ca, self.cg
        d_wq, d_woL, d_br = self.d_wq, self.d_woL, self.d_br
        sc = self.scL
        r = self.r
        wqkv_b = self.sbs(sc, "wqkv_b", [128, 8 * 3072], BF16)
        wo_b = self.sbs(sc, "wo_b", [128, 8 * 1024], BF16)
        xw = self.sbs(sc, "xw", [128, 8 * 512], BF16)
        qb = self.sbs(sc, "qb", [128, 8 * 512], BF16)
        kbc = self.sbs(sc, "kbc", [128, 8 * 512], BF16)
        kbp = self.sbs(sc, "kbp", [128, 8 * 256], BF16)
        vbc = self.sbs(sc, "vbc", [128, 8 * 512], BF16)
        vbp = self.sbs(sc, "vbp", [128, 4 * 512], BF16)
        pT = self.sbs(sc, "pT", [128, 2 * 4 * 256], BF16)
        ao = self.sbs(sc, "ao", [128, 8 * 256], BF16)
        sq_s = self.sbs(sc, "sq_s", [128, 512], BF16)
        rn_s = self.sbs(sc, "rn_s", [128, 2 * 512], F32)
        qh_s = sq_s
        t1_s = self.sbs(sc, "t1_s", [128, 2 * 512], F32)
        zr = self.sbs(sc, "zr", [1, 512], F32)
        zb = self.sbs(sc, "zb", [128, 2 * 256], F32)
        scol = self.sbs(sc, "scol", [128, 4 * 4], F32)
        rotb = self.sbs(sc, "rotb", [128, 4 * 512], BF16)
        QK8 = 8.0

        for half in range(2):
            key = ("wq", l, half)
            if key not in self.wb_marks:
                self.op("gpsimd", (lambda g, half=half, l=l, wqkv_b=wqkv_b: g.dma_start(
                    out=wqkv_b[:, half * 4 * 3072:(half + 1) * 4 * 3072].rearrange(
                        "p (k f) -> p k f", k=4),
                    in_=self.d["L0_ffn_Wqkv"][l][half * 512:(half + 1) * 512, :].rearrange(
                        "(k p) f -> p k f", p=128))),
                    reads=[self.wq_free] + self.wq_guard, incs=[d_wq])
                ld = d_wq.mark()
                self.op("sync", (lambda s, half=half, l=l, wqkv_b=wqkv_b: s.dma_start(
                    out=self.wqcd[l][half],
                    in_=wqkv_b[:, half * 4 * 3072:(half + 1) * 4 * 3072])),
                    reads=[ld], incs=[self.d_wc])
                self.wb_marks[key] = self.d_wc.mark()
                self.wq_guard = [self.wb_marks[key]]
            else:
                qeng = "gpsimd" if half == 0 else "sync"
                self.op(qeng, (lambda g, half=half, l=l, wqkv_b=wqkv_b: g.dma_start(
                    out=wqkv_b[:, half * 4 * 3072:(half + 1) * 4 * 3072],
                    in_=self.wqcd[l][half])),
                    reads=[self.wq_free, self.wb_marks[key]], incs=[d_wq])
        key = ("wo", l)
        if key not in self.wb_marks:
            self.op("gpsimd", (lambda g, l=l, wo_b=wo_b: g.dma_start(
                out=wo_b[:, :].rearrange("p (k f) -> p k f", k=8),
                in_=self.d["L0_ffn_Wo"][l].rearrange("(k p) f -> p k f", p=128))),
                reads=self.wo_guard, incs=[d_woL])
            ld = d_woL.mark()
            self.op("sync", (lambda s, l=l, wo_b=wo_b: s.dma_start(
                out=self.wocd[l], in_=wo_b[:, :])),
                reads=[ld], incs=[self.d_wc])
            self.wb_marks[key] = self.d_wc.mark()
            self.wo_guard = [self.wb_marks[key]]
        else:
            for hh in range(2):
                qeng = "gpsimd" if hh == 0 else "sync"
                self.op(qeng, (lambda g, hh=hh, l=l, wo_b=wo_b: g.dma_start(
                    out=wo_b[:, hh * 4096:(hh + 1) * 4096],
                    in_=self.wocd[l][:, hh * 4096:(hh + 1) * 4096])),
                    reads=[self.wb_marks[key]] if hh == 0 else [], incs=[d_woL])
        wq_ld = d_wq.mark()
        wo_ld = d_woL.mark()

        pbf = {i: [] for i in range(8)}
        xw_free = None
        sqs_free = [[], []]
        t1_free = [[], []]
        nrm_free = None
        rn_free = [[], []]
        qh_free = [[], []]
        zr_free = []
        zb_free = [[], []]
        pending_norm = None
        gq_prev = []
        pT_free = [[], []]
        rot_free = []
        rot_ld = None
        for wg in range(4):
            for tb in range(4):
                self.op("gpsimd", (lambda g, tb=tb, wg=wg, rotb=rotb: g.dma_start(
                    out=rotb[:, tb * 512:(tb + 1) * 512],
                    in_=self.rot4d[tb][:, wg * 512:(wg + 1) * 512])),
                    reads=(rot_free if tb == 0 else []), incs=[self.d_rot[0]])
            rot_ld = self.d_rot[0].mark()
            if wg == 0:
                xw_done = self.emit_xw_cast(0, xw, r, scol, xw_free)
            else:
                xw_done = self.xw_done_next

            for g in range(2):
                sched = []
                for fc in range(4):
                    sched.append(("q", fc))
                    sched.append(("k", fc))
                    sched.append(("v", fc))
                ei = 0
                for (kind, fc) in sched:
                    if kind == "v":
                        tc = fc
                        bsel = 4 + (tc % 2)
                        ps = self.bank(bsel)
                        for k in range(8):
                            self.op("tensor", (lambda t, g=g, tc=tc, k=k, ps=ps, wqkv_b=wqkv_b,
                                               xw=xw: t.matmul(
                                ps[:, 0:512],
                                xw[:, k * 512 + tc * 128:k * 512 + tc * 128 + 128],
                                wqkv_b[:, k * 3072 + 2048 + g * 512:k * 3072 + 2048 + g * 512 + 512],
                                start=(k == 0), stop=(k == 7))),
                                reads=([xw_done, wq_ld] + pbf[bsel]) if k == 0 else [],
                                incs=[cp] if k == 7 else [])
                        self.op("vector", (lambda v, g=g, tc=tc, ps=ps, vbc=vbc: v.tensor_copy(
                            vbc[:, (g * 4 + tc) * 512:(g * 4 + tc) * 512 + 512], ps[:, 0:512])),
                            reads=[cp.mark()], incs=[cv])
                        pbf[bsel] = [cv.mark()]
                        continue
                    qk = 0 if kind == "q" else 1
                    if True:
                        gp = ei % 2
                        ei += 1
                        c0 = gp * 512
                        bsel = gp
                        b2 = 2 if gp == 0 else 6
                        b3 = 3 if gp == 0 else 7
                        ps = self.bank(bsel)
                        for k in range(8):
                            self.op("tensor", (lambda t, g=g, qk=qk, fc=fc, k=k, ps=ps,
                                               wqkv_b=wqkv_b, xw=xw: t.matmul(
                                ps[:, 0:512],
                                wqkv_b[:, k * 3072 + qk * 1024 + g * 512 + fc * 128:
                                       k * 3072 + qk * 1024 + g * 512 + fc * 128 + 128],
                                xw[:, k * 512:(k + 1) * 512],
                                start=(k == 0), stop=(k == 7))),
                                reads=([xw_done, wq_ld] + pbf[bsel]) if k == 0 else [],
                                incs=[cp] if k == 7 else [])
                        qkp_done = cp.mark()
                        self.op("scalar", (lambda a, ps=ps, sq_s=sq_s: a.activation(
                            sq_s[:, 0:512], ps[:, 0:512], AF.Square)),
                            reads=[qkp_done] + sqs_free[0], incs=[ca])
                        self.op("tensor", (lambda t, sq_s=sq_s, b2=b2: t.matmul(
                            self.bank(b2)[:, 0:512], A["bones"][:, :], sq_s[:, 0:512],
                            start=True, stop=True)),
                            reads=[ca.mark()] + pbf[b2], incs=[cp])
                        self.op("scalar", (lambda a, t1_s=t1_s, c0=c0, b2=b2: a.activation(
                            t1_s[:, c0:c0 + 512], self.bank(b2)[:, 0:512], AF.Ln,
                            bias=A["epsq"][:, 0:1])),
                            reads=[cp.mark()] + t1_free[gp], incs=[ca])
                        pbf[b2] = [ca.mark()]
                        self.op("scalar", (lambda a, t1_s=t1_s, rn_s=rn_s, c0=c0: a.activation(
                            rn_s[:, c0:c0 + 512], t1_s[:, c0:c0 + 512], AF.Exp, scale=-0.5)),
                            reads=rn_free[gp], incs=[ca])
                        nrm_free = ca.mark()
                        scale_col = A["qs_c"] if qk == 0 else A["ks_c"]
                        self.op("vector", (lambda v, ps=ps, rn_s=rn_s, qh_s=qh_s, c0=c0,
                                           scale_col=scale_col, l=l: v.scalar_tensor_tensor(
                            qh_s[:, 0:512], rn_s[:, c0:c0 + 512], scale_col[:, l:l + 1],
                            ps[:, 0:512], op0=ALU.mult, op1=ALU.mult)),
                            reads=[ca.mark()] + qh_free[0], incs=[cv])
                        rn_free[gp] = [cv.mark()]
                        pbf[bsel] = [cv.mark()]
                        self.op("tensor", (lambda t, qh_s=qh_s, b3=b3: t.matmul(
                            self.bank(b3)[:, 0:512], A["pswap"][:, :], qh_s[:, 0:512],
                            start=True, stop=True)),
                            reads=[cv.mark()] + pbf[b3], incs=[cp])
                        swp_done = cp.mark()
                        qh_free[0] = [swp_done]
                        ctab = rotb[:, (0 if qk == 0 else 2) * 512:
                                    (0 if qk == 0 else 2) * 512 + 512]
                        stabt = rotb[:, (1 if qk == 0 else 3) * 512:
                                     (1 if qk == 0 else 3) * 512 + 512]
                        stt_mark = cv.mark()
                        self.op("vector", (lambda v, t1_s=t1_s, stabt=stabt, c0=c0, b3=b3:
                                v.tensor_mul(t1_s[:, c0:c0 + 512], self.bank(b3)[:, 0:512],
                                             stabt)),
                                reads=[swp_done, rot_ld], incs=[cv])
                        t1mul_mark = cv.mark()
                        pbf[b3] = [t1mul_mark]
                        dest = (qb if qk == 0 else kbc)
                        dap = dest[0:128, (g * 4 + fc) * 512:(g * 4 + fc) * 512 + 512]
                        self.op("gpsimd", (lambda gg, dap=dap, qh_s=qh_s, ctab=ctab:
                                gg.tensor_mul(dap, qh_s[:, 0:512], ctab)),
                                reads=[stt_mark, rot_ld] + gq_prev, incs=[cg])
                        sqs_free[0] = [cp.mark(), cg.mark()]
                        self.op("gpsimd", (lambda gg, dap=dap, t1_s=t1_s, c0=c0:
                                gg.tensor_add(dap, dap, t1_s[:, c0:c0 + 512])),
                                reads=[cg.mark(), t1mul_mark], incs=[cg])
                        gq_prev = [cg.mark()]
                        t1_free[gp] = [cg.mark()]
            prod_done = cv.mark()
            prod_g_done = cg.mark()
            if wg == 3:
                self.wq_free = cp.mark()
            xw_free = cp.mark()
            rot_free = [cv.mark(), cg.mark()]

            for wi in range(2):
                w = 2 * wg + wi
                def flush_norm():
                    nonlocal pending_norm, zr_free
                    if pending_norm is None:
                        return
                    fobank, fob, j, z_done, pair_done = pending_norm
                    pending_norm = None
                    zsl = j % 2
                    self.op("scalar", (lambda a, fobank=fobank, zb=zb, zsl=zsl: a.activation(
                        zb[0:1, zsl * 256:(zsl + 1) * 256], fobank[0:1, 256:512],
                        AF.Ln)),
                        reads=[z_done] + zr_free + zb_free[zsl])
                    self.op("scalar", (lambda a, zb=zb, zr=zr, zsl=zsl: a.activation(
                        zr[0:1, 0:256], zb[0:1, zsl * 256:(zsl + 1) * 256],
                        AF.Exp, scale=-1.0)))
                    self.op("scalar", (lambda a, fobank=fobank, zb=zb, zsl=zsl: a.activation(
                        zb[0:1, zsl * 256:(zsl + 1) * 256], fobank[64:65, 256:512],
                        AF.Ln)))
                    self.op("scalar", (lambda a, zb=zb, zr=zr, zsl=zsl: a.activation(
                        zr[0:1, 256:512], zb[0:1, zsl * 256:(zsl + 1) * 256],
                        AF.Exp, scale=-1.0)),
                        incs=[ca])
                    zrec_done = ca.mark()
                    zps = self.bank(7)
                    for hh in range(2):
                        self.op("tensor", (lambda t, zps=zps, zr=zr, hh=hh: t.matmul(
                            zps[hh * 64:(hh + 1) * 64, 0:256], A["ones1f"][0:1, 0:64],
                            zr[0:1, hh * 256:hh * 256 + 256],
                            start=True, stop=True)),
                            reads=([zrec_done] + pbf[7]) if hh == 0 else [],
                            incs=[cp] if hh == 1 else [])
                    zbc_done = cp.mark()
                    zr_free = [zbc_done]
                    self.op("vector", (lambda v, zps=zps, zb=zb, zsl=zsl: v.tensor_copy(
                        zb[:, zsl * 256:(zsl + 1) * 256], zps[0:128, 0:256])),
                        reads=[zbc_done] + zb_free[zsl])
                    self.op("vector", (lambda v, fobank=fobank, zb=zb, zsl=zsl, j=j, ao=ao:
                            v.tensor_mul(ao[:, j * 256:(j + 1) * 256],
                                         fobank[0:128, 0:256],
                                         zb[:, zsl * 256:(zsl + 1) * 256])),
                        reads=[pair_done], incs=[cv])
                    pbf[fob] = [cv.mark()]
                    pbf[7] = [cv.mark()]
                    zb_free[zsl] = [cv.mark()]
                def emit_masks(hh):
                    sset = (1, 2) if hh % 2 == 0 else (3, 4)
                    mss = [2, 3] if w == 0 else [0, 1, 2, 3]
                    for mi, m in enumerate(mss):
                        ps = self.bank(sset[mi // 2])
                        colr = (mi % 2) * 256
                        self.op("tensor", (lambda t, ps=ps, colr=colr, m=m: t.matmul(
                            ps[:, colr:colr + 256], A["identb"][:, :],
                            A["sTm%d" % m][:, :], start=True, stop=False)),
                            reads=([prod_done, prod_g_done] + pbf[sset[0]] + pbf[sset[1]])
                            if mi == 0 else [])
                ms = [2, 3] if w == 0 else [0, 1, 2, 3]
                nm = len(ms)

                def emit_scores(hh):
                    g, hq = hh // 8, hh % 8
                    fc, pb = hq // 2, (hq % 2) * 64
                    sset = (1, 2) if hh % 2 == 0 else (3, 4)
                    psl = (hh % 2) * 1024
                    for pi in range(nm // 2):
                        ps = self.bank(sset[pi])
                        mname = "sTm01" if (nm == 4 and pi == 0) else "sTm23"
                        self.op("tensor", (lambda t, ps=ps, mname=mname: t.matmul(
                            ps[:, 0:512], A["identb"][:, :],
                            A[mname][:, :], start=True, stop=False)),
                            reads=([prod_done, prod_g_done] + pbf[sset[0]] + pbf[sset[1]])
                            if pi == 0 else [])
                        for sub in range(2):
                            mi = pi * 2 + sub
                            m = ms[mi]
                            colr = sub * 256
                            gc = 2 * w - 2 + m
                            if gc >= 4 * wg:
                                seg = gc - 4 * wg
                                kap = kbc[pb:pb + 64, (g * 4 + fc) * 512 + seg * 128:
                                          (g * 4 + fc) * 512 + seg * 128 + 128]
                            else:
                                seg = gc - (4 * wg - 2)
                                kap = kbp[pb:pb + 64, (g * 4 + fc) * 256 + seg * 128:
                                          (g * 4 + fc) * 256 + seg * 128 + 128]
                            qap = qb[pb:pb + 64, (g * 4 + fc) * 512 + wi * 256:
                                     (g * 4 + fc) * 512 + wi * 256 + 256]
                            self.op("tensor", (lambda t, ps=ps, colr=colr, kap=kap, qap=qap:
                                    t.matmul(ps[:, colr:colr + 256], kap, qap,
                                             start=False, stop=(colr == 256))),
                                    incs=[cp] if mi == nm - 1 else [])
                    return cp.mark()

                sT_next = emit_scores(0)
                for h in range(16):
                    if h % 2 == 1:
                        flush_norm()
                    g, hq = h // 8, h % 8
                    fc, pb = hq // 2, (hq % 2) * 64
                    stset = (1, 2) if h % 2 == 0 else (3, 4)
                    psl = (h % 2) * 1024
                    ob = 5 if (h // 2) % 2 == 0 else 6
                    obank = self.bank(ob)
                    sT_done = sT_next
                    if h < 15:
                        sT_next = emit_scores(h + 1)
                    exp_marks = []
                    for pi in range(nm // 2):
                        ps = self.bank(stset[pi])
                        self.op("scalar", (lambda a, ps=ps, pi=pi, pT=pT, psl=psl:
                                a.activation(
                            pT[:, psl + pi * 512:psl + (pi + 1) * 512], ps[:, 0:512],
                            AF.Exp, scale=QK8)),
                            reads=([sT_done] + pT_free[h % 2]) if pi == 0 else [], incs=[ca])
                        exp_marks.append(ca.mark())
                        exp_marks.append(ca.mark())
                    pbf[stset[0]] = [exp_marks[-1]]
                    pbf[stset[1]] = [exp_marks[-1]]
                    for mi, m in enumerate(ms):
                        self.op("tensor", (lambda t, mi=mi, pT=pT, obank=obank, pb=pb,
                                           nm=nm, psl=psl: t.matmul(
                            obank[pb:pb + 1, 256:512], A["onescolb"][:, 0:1],
                            pT[:, psl + mi * 256:psl + (mi + 1) * 256],
                            start=(mi == 0), stop=(mi == nm - 1))),
                            reads=([exp_marks[mi]] + (pbf[ob] if mi == 0 else [])),
                            incs=[cp] if mi == nm - 1 else [])
                    z_done = cp.mark()
                    for mi, m in enumerate(ms):
                        gc = 2 * w - 2 + m
                        if gc >= 4 * wg:
                            seg = gc - 4 * wg
                            vap = vbc[:, (g * 4 + seg) * 512 + hq * 64:
                                      (g * 4 + seg) * 512 + hq * 64 + 64]
                        else:
                            seg = gc - (4 * wg - 2)
                            vap = vbp[:, (g * 2 + seg) * 512 + hq * 64:
                                      (g * 2 + seg) * 512 + hq * 64 + 64]
                        self.op("tensor", (lambda t, vap=vap, mi=mi, pT=pT, obank=obank, pb=pb,
                                           nm=nm, psl=psl: t.matmul(
                            obank[pb:pb + 64, 0:256], vap,
                            pT[:, psl + mi * 256:psl + (mi + 1) * 256],
                            start=(mi == 0), stop=(mi == nm - 1))),
                            incs=[cp] if mi == nm - 1 else [])
                    pT_free[h % 2] = [cp.mark()]
                    if h % 2 == 1:
                        pending_norm = (obank, ob, h // 2, z_done, cp.mark())
                flush_norm()
                ao_done = cv.mark()
                if wi == 1 and wg < 3:
                    for cc in range(4, 8):
                        off = cc * GCOL + 1 + (wg + 1) * 512 - 1
                        self.op("vector", (lambda v, cc=cc, wg=wg, off=off, scol=scol, r=r:
                                v.tensor_copy(scol[:, wg * 4 + (cc - 4):wg * 4 + (cc - 4) + 1],
                                              r[:, off:off + 1])))
                    self.op("vector", lambda v: v.nop(), incs=[cv])
                    self.xw_done_next = self.emit_xw_cast(wg + 1, xw, r, scol, xw_free)
                for oc in range(8):
                    ps = self.bank(oc)
                    for c2 in range(8):
                        self.op("tensor", (lambda t, oc=oc, c2=c2, ps=ps, wo_b=wo_b, ao=ao:
                                t.matmul(ps[:, 0:256],
                                         wo_b[:, c2 * 1024 + oc * 128:c2 * 1024 + oc * 128 + 128],
                                         ao[:, c2 * 256:(c2 + 1) * 256],
                                         start=(c2 == 0), stop=(c2 == 7))),
                                reads=(([ao_done, wo_ld] + sum((pbf[i] for i in range(8)), []))
                                       if (oc == 0 and c2 == 0) else []),
                                incs=[cp] if c2 == 7 else [])
                    self.op("vector", (lambda v, oc=oc, ps=ps, w=w, r=r: v.tensor_add(
                        r[:, oc * GCOL + 1 + w * 256:oc * GCOL + 1 + (w + 1) * 256],
                        ps[:, 0:256],
                        r[:, oc * GCOL + 1 + w * 256:oc * GCOL + 1 + (w + 1) * 256])),
                        reads=[cp.mark()], incs=[cv])
                    pbf[oc] = [cv.mark()]
            if wg < 3:
                for g in range(2):
                    for fc in range(4):
                        self.op("vector", (lambda v, g=g, fc=fc, kbc=kbc, kbp=kbp: v.tensor_copy(
                            kbp[:, (g * 4 + fc) * 256:(g * 4 + fc) * 256 + 256],
                            kbc[:, (g * 4 + fc) * 512 + 256:(g * 4 + fc) * 512 + 512])),
                            reads=[prod_g_done] if (g == 0 and fc == 0) else [])
                    for u in range(2):
                        self.op("vector", (lambda v, g=g, u=u, vbc=vbc, vbp=vbp: v.tensor_copy(
                            vbp[:, (g * 2 + u) * 512:(g * 2 + u) * 512 + 512],
                            vbc[:, (g * 4 + 2 + u) * 512:(g * 4 + 2 + u) * 512 + 512])))
                self.op("vector", lambda v: v.nop(), incs=[cv])


    def layer_ffn(self, p, l):
        A = self.A
        cp, cv, ca = self.cp, self.cv, self.ca
        d_w1, d_w2 = self.d_w1, self.d_w2
        sc = self.scL
        r = self.r
        xn2 = self.sbs(sc, "xn2", [128, 8 * 1024], BF16)
        ag = self.sbs(sc, "ag", [128, 22 * 1024], BF16)
        w1b = self.sbs(sc, "w1b", [128, 6 * 1024], BF16)
        w2b = self.sbs(sc, "w2b", [128, 3 * 1024], BF16)
        gel = self.sbs(sc, "gel", [128, 1024], BF16)
        m_b = self.sbs(sc, "m_b", [128, 1024], F32)
        rstd_b = self.sbs(sc, "rstd_b", [128, 1024], F32)
        t1f = self.sbs(sc, "t1f", [128, 1024], F32)
        t2f = self.sbs(sc, "t2f", [128, 1024], F32)
        sqs = self.sbs(sc, "sqs", [128, 2 * 1024], BF16)

        CHW = [128] * 21 + [DI - 21 * 128]
        w1_free = [None] * 6
        w2_free = [None, None, None]
        pbf = {i: [] for i in range(8)}
        xn2_free = None
        prev_w2 = None
        gel_free = [None, None]
        for half in (1, 0):
            T0 = half * 1024
            for cc in range(8):
                off = cc * GCOL + 1 + T0 - (1 if cc >= 4 else 0)
                self.op("vector", (lambda v, cc=cc, off=off, xn2=xn2, r=r: v.tensor_copy(
                    xn2[:, cc * 1024:(cc + 1) * 1024], r[:, off:off + 1024])),
                    reads=[xn2_free] if cc == 0 else [])
            self.op("vector", lambda v: v.nop(), incs=[cv])
            rsb_done = cv.mark()
            for cc in range(8):
                for tt in range(2):
                    self.op("tensor", (lambda t, cc=cc, tt=tt, xn2=xn2: t.matmul(
                        self.bank(tt)[:, 0:512], A["onesfull"][:, :],
                        xn2[:, cc * 1024 + tt * 512:cc * 1024 + tt * 512 + 512],
                        start=(cc == 0), stop=(cc == 7))),
                        reads=(([rsb_done] + pbf[tt]) if cc == 0 else []),
                        incs=[cp] if cc == 7 else [])
            sq_free = [None, None]
            for cc in range(8):
                b = cc % 2
                self.op("scalar", (lambda a, cc=cc, b=b, xn2=xn2, sqs=sqs: a.activation(
                    sqs[:, b * 1024:(b + 1) * 1024], xn2[:, cc * 1024:(cc + 1) * 1024],
                    AF.Square)),
                    reads=[rsb_done, sq_free[b]], incs=[ca])
                sq_ca = ca.mark()
                for tt in range(2):
                    self.op("tensor", (lambda t, cc=cc, tt=tt, b=b, sqs=sqs: t.matmul(
                        self.bank(2 + tt)[:, 0:512], A["onesfull"][:, :],
                        sqs[:, b * 1024 + tt * 512:b * 1024 + tt * 512 + 512],
                        start=(cc == 0), stop=(cc == 7))),
                        reads=([sq_ca] if tt == 0 else []) + (pbf[2 + tt] if cc == 0 else []),
                        incs=[cp] if tt == 1 else [])
                sq_free[b] = cp.mark()
            stat_done = cp.mark()
            for tt in range(2):
                self.op("vector", (lambda v, tt=tt, m_b=m_b: v.tensor_scalar(
                    m_b[:, tt * 512:(tt + 1) * 512], self.bank(tt)[:, 0:512],
                    1.0 / D, None, op0=ALU.mult)), reads=[stat_done])
                self.op("vector", (lambda v, tt=tt, t1f=t1f: v.tensor_scalar(
                    t1f[:, tt * 512:(tt + 1) * 512], self.bank(2 + tt)[:, 0:512],
                    1.0 / D, None, op0=ALU.mult)))
                self.op("vector", (lambda v, tt=tt, m_b=m_b, t2f=t2f: v.tensor_mul(
                    t2f[:, tt * 512:(tt + 1) * 512], m_b[:, tt * 512:(tt + 1) * 512],
                    m_b[:, tt * 512:(tt + 1) * 512])))
                self.op("vector", (lambda v, tt=tt, t1f=t1f, t2f=t2f: v.tensor_sub(
                    t1f[:, tt * 512:(tt + 1) * 512], t1f[:, tt * 512:(tt + 1) * 512],
                    t2f[:, tt * 512:(tt + 1) * 512])), incs=[cv])
            self.op("scalar", (lambda a, t1f=t1f, t2f=t2f: a.activation(
                t2f[:, 0:1024], t1f[:, 0:1024], AF.Ln, bias=A["epsc"][:, 0:1])),
                reads=[cv.mark()], incs=[ca])
            self.op("scalar", (lambda a, t2f=t2f, rstd_b=rstd_b: a.activation(
                rstd_b[:, 0:1024], t2f[:, 0:1024], AF.Exp, scale=-0.5)), incs=[ca])
            self.op("vector", lambda v: v.nop(), reads=[ca.mark()], incs=[cv])
            var_done = cv.mark()
            for i in range(4):
                pbf[i] = [var_done]
            for cc in range(8):
                off = cc * GCOL + 1 + T0 - (1 if cc >= 4 else 0)
                self.op("vector", (lambda v, cc=cc, off=off, t1f=t1f, m_b=m_b, r=r: v.tensor_sub(
                    t1f[:, 0:1024], r[:, off:off + 1024], m_b[:, 0:1024])),
                    reads=[var_done] if cc == 0 else [])
                self.op("vector", (lambda v, t1f=t1f, rstd_b=rstd_b: v.tensor_mul(
                    t1f[:, 0:1024], t1f[:, 0:1024], rstd_b[:, 0:1024])))
                self.op("vector", (lambda v, cc=cc, l=l, t1f=t1f, xn2=xn2: v.tensor_scalar(
                    xn2[:, cc * 1024:(cc + 1) * 1024], t1f[:, 0:1024],
                    A["lng_c"][:, l * 8 + cc:l * 8 + cc + 1],
                    A["lnb_c"][:, l * 8 + cc:l * 8 + cc + 1],
                    op0=ALU.mult, op1=ALU.add)), incs=[cv])
            xn2_done = cv.mark()
            first_ag = True
            for j in range(22):
                wda = CHW[j]
                for agx in range(2):
                    slot = (j % 3) * 2 + agx
                    c0 = agx * DI + j * 128
                    jj = j * 2 + agx
                    key = ("w1", l, jj)
                    if key not in self.wb_marks:
                        self.op("gpsimd", (lambda g, l=l, c0=c0, wda=wda, slot=slot, w1b=w1b:
                                g.dma_start(
                            out=w1b[:, slot * 1024:slot * 1024 + 8 * wda].rearrange(
                                "p (k f) -> p k f", k=8),
                            in_=self.d["L0_ffn_W1"][l][:, c0:c0 + wda].rearrange(
                                "(k p) f -> p k f", p=128))),
                            reads=[w1_free[slot]] + self.w1_slot_guard[slot],
                            incs=[d_w1[slot]])
                        ld = d_w1[slot].mark()
                        self.op("sync", (lambda s, l=l, jj=jj, wda=wda, slot=slot, w1b=w1b:
                                s.dma_start(
                            out=self.w1cd[l][:, jj, :, 0:wda],
                            in_=w1b[:, slot * 1024:slot * 1024 + 8 * wda].rearrange(
                                "p (k f) -> p k f", k=8))),
                            reads=[ld], incs=[self.d_wc])
                        self.wb_marks[key] = self.d_wc.mark()
                        self.w1_slot_guard[slot] = [self.wb_marks[key]]
                    else:
                        qeng = "sync" if j % 2 == 0 else "gpsimd"
                        self.op(qeng, (lambda g, l=l, jj=jj, wda=wda, slot=slot, w1b=w1b:
                                g.dma_start(
                            out=w1b[:, slot * 1024:slot * 1024 + 8 * wda].rearrange(
                                "p (k f) -> p k f", k=8),
                            in_=self.w1cd[l][:, jj, :, 0:wda])),
                            reads=[w1_free[slot], self.wb_marks[key]], incs=[d_w1[slot]])
                w1_ld = [d_w1[(j % 3) * 2].mark(), d_w1[(j % 3) * 2 + 1].mark()]
                for agx in range(2):
                    slot = (j % 3) * 2 + agx
                    bs0 = (0 if j % 2 == 0 else 4) + agx * 2
                    for tt in range(2):
                        ps = self.bank(bs0 + tt)
                        for k in range(8):
                            self.op("tensor", (lambda t, j=j, k=k, tt=tt, slot=slot, wda=wda,
                                               ps=ps, w1b=w1b, xn2=xn2: t.matmul(
                                ps[0:wda, 0:512],
                                w1b[:, slot * 1024 + k * wda:slot * 1024 + (k + 1) * wda],
                                xn2[:, k * 1024 + tt * 512:k * 1024 + tt * 512 + 512],
                                start=(k == 0), stop=(k == 7))),
                                reads=(([xn2_done, w1_ld[agx]] + pbf[bs0 + tt] +
                                        ([prev_w2] if first_ag else []))
                                       if k == 0 else []),
                                incs=[cp] if k == 7 else [])
                first_ag = False
                hg_done = cp.mark()
                w1_free[(j % 3) * 2] = hg_done
                w1_free[(j % 3) * 2 + 1] = hg_done
                bsa = (0 if j % 2 == 0 else 4)
                for tt in range(2):
                    psg = self.bank(bsa + 2 + tt)
                    psa = self.bank(bsa + tt)
                    self.op("scalar", (lambda a, j=j, tt=tt, l=l, wda=wda, psg=psg, gel=gel:
                            a.activation(gel[0:wda, tt * 512:(tt + 1) * 512],
                                         psg[0:wda, 0:512], AF.Gelu,
                                         bias=A["b1_c"][0:wda, l * 44 + 22 + j:l * 44 + 23 + j])),
                            reads=[hg_done, gel_free[tt]], incs=[ca])
                    self.op("vector", (lambda v, j=j, tt=tt, l=l, wda=wda, psa=psa, gel=gel,
                                       ag=ag: v.scalar_tensor_tensor(
                        ag[0:wda, j * 1024 + tt * 512:j * 1024 + tt * 512 + 512],
                        psa[0:wda, 0:512],
                        A["b1_c"][0:wda, l * 44 + j:l * 44 + j + 1],
                        gel[0:wda, tt * 512:(tt + 1) * 512],
                        op0=ALU.add, op1=ALU.mult)),
                        reads=[ca.mark()], incs=[cv])
                    pbf[bsa + tt] = [cv.mark()]
                    pbf[bsa + 2 + tt] = [cv.mark()]
                    gel_free[tt] = cv.mark()
            ag_done = cv.mark()
            for tt in range(2):
                for c2 in range(22):
                    wda = CHW[c2]
                    slot = c2 % 3
                    key = ("w2", l, c2)
                    if key not in self.wb_marks:
                        self.op("gpsimd", (lambda g, l=l, c2=c2, wda=wda, slot=slot, w2b=w2b:
                                g.dma_start(
                            out=w2b[0:wda, slot * 1024:slot * 1024 + 1024],
                            in_=self.d["L0_ffn_W2"][l][c2 * 128:c2 * 128 + wda, :])),
                            reads=[w2_free[slot]] + self.w2_slot_guard[slot],
                            incs=[d_w2[slot]])
                        ld = d_w2[slot].mark()
                        self.op("sync", (lambda s, l=l, c2=c2, wda=wda, slot=slot, w2b=w2b:
                                s.dma_start(
                            out=self.w2cd[l][c2][0:wda, :],
                            in_=w2b[0:wda, slot * 1024:slot * 1024 + 1024])),
                            reads=[ld], incs=[self.d_wc])
                        self.wb_marks[key] = self.d_wc.mark()
                        self.w2_slot_guard[slot] = [self.wb_marks[key]]
                    else:
                        qeng = "sync" if c2 % 2 == 0 else "gpsimd"
                        self.op(qeng, (lambda g, l=l, c2=c2, wda=wda, slot=slot, w2b=w2b:
                                g.dma_start(
                            out=w2b[0:wda, slot * 1024:slot * 1024 + 1024],
                            in_=self.w2cd[l][c2][0:wda, :])),
                            reads=[w2_free[slot], self.wb_marks[key]], incs=[d_w2[slot]])
                    w2_ld = d_w2[slot].mark()
                    for oc in range(8):
                        ps = self.bank(oc)
                        self.op("tensor", (lambda t, c2=c2, oc=oc, tt=tt, wda=wda, slot=slot,
                                           ps=ps, w2b=w2b, ag=ag: t.matmul(
                            ps[:, 0:512],
                            w2b[0:wda, slot * 1024 + oc * 128:slot * 1024 + oc * 128 + 128],
                            ag[0:wda, c2 * 1024 + tt * 512:c2 * 1024 + tt * 512 + 512],
                            start=(c2 == 0), stop=(c2 == 21))),
                            reads=(([ag_done, w2_ld] + sum((pbf[i] for i in range(8)), []))
                                   if (c2 == 0 and oc == 0) else
                                   ([w2_ld] if oc == 0 else [])),
                            incs=[cp] if oc == 7 else [])
                    w2_free[slot] = cp.mark()
                w2_done = cp.mark()
                for oc in range(8):
                    ps = self.bank(oc)
                    off = oc * GCOL + 1 + T0 + tt * 512
                    self.op("vector", (lambda v, oc=oc, l=l, off=off, ps=ps, r=r:
                            v.scalar_tensor_tensor(
                        r[:, off:off + 512], ps[:, 0:512],
                        A["b2_c"][:, l * 8 + oc:l * 8 + oc + 1],
                        r[:, off:off + 512], op0=ALU.add, op1=ALU.add)),
                        reads=[w2_done], incs=[cv])
                    pbf[oc] = [cv.mark()]
            xn2_free = cp.mark()
            prev_w2 = cp.mark()

    def g_store(self, p):
        ca = self.ca
        d_g = self.d_g
        r = self.r
        gsc = contextlib.ExitStack()
        gtmp = self.sbs(gsc, "gtmp", [128, 2 * 2048], BF16)
        self.fence("scalar")
        gt_free = [None, None]
        for cc in range(8):
            b = cc % 2
            self.op("scalar", (lambda a, cc=cc, b=b, r=r, gtmp=gtmp: a.activation(
                gtmp[:, b * 2048:(b + 1) * 2048],
                r[:, cc * GCOL + 1:cc * GCOL + 1 + 2048], AF.Silu)),
                reads=[gt_free[b]], incs=[ca])
            self.op("sync", (lambda s, cc=cc, b=b, gtmp=gtmp: s.dma_start(
                out=self.gscrd[cc * 128:(cc + 1) * 128, :],
                in_=gtmp[:, b * 2048:(b + 1) * 2048])),
                reads=[ca.mark()], incs=[d_g[b]])
            gt_free[b] = d_g[b].mark()
        gsc.close()

    def phase_C(self, p):
        A, d = self.A, self.d
        cp, cv, ca = self.cp, self.cv, self.ca
        d_xc, d_gt, d_wb, d_wo = self.d_xc, self.d_gt, self.d_wb, self.d_wo
        d_yr, d_out, d_tap = self.d_yr, self.d_out, self.d_tap
        sc = self.scC
        winb = self.sbs(sc, "winb", [128, 4 * 1024], BF16)
        self.sbs(sc, "lnsq", [128, 1024], BF16)
        woutt_b = self.sbs(sc, "woutt_b", [128, 2 * 1024], BF16)
        xc = self.sbs(sc, "xc", [128, 4 * 1024], F32)
        xntc = self.sbs(sc, "xntc", [128, 4 * 1024], BF16)
        xnfc = self.sbs(sc, "xnfc", [128, 8 * 512], BF16)
        gtok = self.sbs(sc, "gtok", [128, 4 * 1024], BF16)
        xhb = self.sbs(sc, "xhb", [128, 16 * (CW + 4)], BF16)
        xhs = self.sbs(sc, "xhs", [128, 16 * 512], BF16)
        gateb = self.sbs(sc, "gateb", [128, 16 * 512], BF16)
        ubf = self.sbs(sc, "ubf", [NS, 512], BF16)
        diag = self.sbs(sc, "diag", [128, 64 * 128], BF16)
        ylsb = self.sbs(sc, "ylsb", [8, TSC], F32)
        yrow = self.sbs(sc, "yrow", [1, 512], F32)
        m1s = self.sbs(sc, "m1s", [128, 2 * 512], F32)
        t2s = self.sbs(sc, "t2s", [128, 512], F32)
        sgb = self.sbs(sc, "sgb", [128, 2 * 512], BF16)
        outst = self.sbs(sc, "outst", [128, 4 * 1024], F32)
        identb = A["identb"]

        self.op("vector", (lambda v, xhb=xhb: v.memset(
            xhb[:, :].rearrange("p (s c) -> p s c", s=16)[:, :, 0:4], 0.0)))
        self.op("vector", lambda v: v.memset(A["hcur"][0:NS, 0:1], 0.0))
        for f in range(16):
            for k in range(4):
                self.op("vector", (lambda v, f=f, k=k, diag=diag, identb=identb: v.tensor_scalar(
                    diag[:, (f * 4 + k) * 128:(f * 4 + k) * 128 + 128],
                    identb[:, :], A["convw_c"][:, k * 16 + f:k * 16 + f + 1], None,
                    op0=ALU.mult)), incs=[cv] if (f == 15 and k == 3) else [])
        diag_done = cv.mark()

        xc_free = None
        gtok_free = None
        outst_free = None
        woutt_free = [None, None]
        winb_free = [None] * 4
        xz_free = [None, None]
        conv_free = [None, None]
        sg_free = [None, None]
        prev_convmm = None
        prev_oproj = None
        prev_comb = None
        for ch in range(NCHK):
            t0 = ch * CW
            self.op("sync", (lambda s, t0=t0, p=p, xc=xc: s.dma_start(
                out=xc[:, :].rearrange("p (n f) -> p n f", n=4),
                in_=self.d["L0_attn_x"][p, t0:t0 + CW, :].rearrange("(n p) f -> p n f", p=128))),
                reads=[xc_free], incs=[d_xc])
            xld = d_xc.mark()
            for sub in range(4):
                self.op("sync", (lambda s, t0=t0, sub=sub, gtok=gtok: s.dma_start(
                    out=gtok[:, sub * 1024:(sub + 1) * 1024],
                    in_=self.gscrd[:, t0 + sub * 128:t0 + (sub + 1) * 128], transpose=True)),
                    reads=[gtok_free] if sub == 0 else [], incs=[d_gt])
            gld = d_gt.mark()
            xn_subs = []
            for sub in range(4):
                self.ln_tok(xc[:, sub * 1024:(sub + 1) * 1024],
                            xntc[:, sub * 1024:(sub + 1) * 1024], sub % 2,
                            extra_reads=[xld])
                xn_subs.append(cv.mark())
            xn_done = cv.mark()
            pt_free = [prev_comb, prev_comb]
            for sub in range(4):
                psb = self.bankb(4 + sub % 2)
                for cc2 in range(8):
                    self.op("tensor", (lambda t, sub=sub, cc2=cc2, psb=psb, xntc=xntc, identb=identb:
                            t.transpose(
                                psb[:, cc2 * 128:(cc2 + 1) * 128],
                                xntc[:, sub * 1024 + cc2 * 128:sub * 1024 + (cc2 + 1) * 128],
                                identb[:, :])),
                            reads=[xn_subs[sub], pt_free[sub % 2]] if cc2 == 0 else [],
                            incs=[cp] if cc2 == 7 else [])
                def evt(v, sub=sub, psb=psb, xnfc=xnfc):
                    last = None
                    for cc2 in range(8):
                        last = v.tensor_copy(
                            xnfc[:, cc2 * 512 + sub * 128:cc2 * 512 + (sub + 1) * 128],
                            psb[:, cc2 * 128:(cc2 + 1) * 128])
                    return last
                self.op("vector", evt, reads=[cp.mark()], incs=[cv])
                pt_free[sub % 2] = cv.mark()
            xnf_done = cv.mark()
            for f in range(32):
                wb = f % 4
                self.op("gpsimd", (lambda g, f=f, wb=wb, winb=winb: g.dma_start(
                    out=winb[:, wb * 1024:wb * 1024 + 1024].rearrange("p (k f) -> p k f", k=8),
                    in_=self.wintd[:, f * 128:(f + 1) * 128].rearrange("(k p) f -> p k f", p=128))),
                    reads=[winb_free[wb]], incs=[d_wb[wb]])
                wld = d_wb[wb].mark()
                bsel = f % 2
                ps = self.bank(bsel)
                for k in range(8):
                    self.op("tensor", (lambda t, f=f, k=k, ps=ps, wb=wb, winb=winb, xnfc=xnfc:
                            t.matmul(
                                ps[:, 0:512], winb[:, wb * 1024 + k * 128:wb * 1024 + k * 128 + 128],
                                xnfc[:, k * 512:(k + 1) * 512],
                                start=(k == 0), stop=(k == 7))),
                            reads=[xnf_done, wld, xz_free[bsel]] if k == 0 else [],
                            incs=[cp] if k == 7 else [])
                winb_free[wb] = cp.mark()
                if f < 16:
                    self.op("vector", (lambda v, f=f, ps=ps, xhb=xhb: v.tensor_scalar(
                        xhb[:, f * (CW + 4) + 4:f * (CW + 4) + 4 + CW],
                        ps[:, 0:512], A["bin_c"][:, f:f + 1], None, op0=ALU.add)),
                        reads=[cp.mark(), prev_convmm], incs=[cv])
                else:
                    self.op("vector", (lambda v, f=f, ps=ps, gateb=gateb: v.tensor_scalar(
                        gateb[:, (f - 16) * 512:(f - 16) * 512 + 512],
                        ps[:, 0:512], A["bin_c"][:, f:f + 1], None, op0=ALU.add)),
                        reads=[cp.mark()], incs=[cv])
                xz_free[bsel] = cv.mark()
            xz_done = cv.mark()
            for f in range(16):
                bsel = 2 + f % 2
                ps = self.bank(bsel)
                for k in range(4):
                    self.op("tensor", (lambda t, f=f, k=k, ps=ps, diag=diag, xhb=xhb: t.matmul(
                        ps[:, 0:512], diag[:, (f * 4 + k) * 128:(f * 4 + k) * 128 + 128],
                        xhb[:, f * (CW + 4) + 1 + k:f * (CW + 4) + 1 + k + CW],
                        start=(k == 0), stop=(k == 3))),
                        reads=([xz_done, diag_done, conv_free[f % 2]] if k == 0 else []),
                        incs=[cp] if k == 3 else [])
                self.op("scalar", (lambda a, f=f, ps=ps, xhs=xhs: a.activation(
                    xhs[:, f * 512:(f + 1) * 512], ps[:, 0:512], AF.Silu,
                    bias=A["convb_c"][:, f:f + 1])),
                    reads=[cp.mark(), prev_oproj], incs=[ca])
                conv_free[f % 2] = ca.mark()
            conv_done = ca.mark()
            prev_convmm = cp.mark()
            if ch < NCHK - 1:
                self.op("vector", (lambda v, xhb=xhb: v.tensor_copy(
                    xhb[:, :].rearrange("p (s c) -> p s c", s=16)[:, :, 0:4],
                    xhb[:, :].rearrange("p (s c) -> p s c", s=16)[:, :, CW:CW + 4])),
                    reads=[prev_convmm])
            pu = self.bank(6)
            for k in range(16):
                self.op("tensor", (lambda t, k=k, pu=pu, xhs=xhs: t.matmul(
                    pu[0:NS, 0:512], A["wssmt"][:, k * NS:(k + 1) * NS],
                    xhs[:, k * 512:(k + 1) * 512], start=(k == 0), stop=(k == 15))),
                    reads=[conv_done, cv.mark()] if k == 0 else [],
                    incs=[cp] if k == 15 else [])
            self.op("vector", (lambda v, pu=pu, ubf=ubf: v.tensor_scalar(
                ubf[0:NS, 0:512], pu[0:NS, 0:512], A["bssm_c"][0:NS, 0:1], None,
                op0=ALU.add)), reads=[cp.mark()], incs=[cv])
            u_done = cv.mark()
            ph = self.bank(6)
            for j in range(TSC):
                self.op("tensor", (lambda t, j=j, ph=ph, ubf=ubf: t.matmul(
                    ph[0:NS, 0:8],
                    A["pbext"][0:NS, (TSC - 1 - j) * NS:(TSC - j) * NS],
                    ubf[0:NS, j:512:TSC], start=(j == 0), stop=(j == TSC - 1))),
                    reads=[u_done] if j == 0 else [],
                    incs=[cp] if j == TSC - 1 else [])
            self.op("vector", (lambda v, ph=ph: v.tensor_copy(A["hdsb"][0:NS, 0:8], ph[0:NS, 0:8])),
                    reads=[cp.mark()])
            self.op("vector", (lambda v: v.tensor_copy(A["Hsb"][0:NS, 0:1], A["hcur"][0:NS, 0:1])),
                    incs=[cv])
            hd_done = cv.mark()
            py = self.bank(7)
            for j in range(TSC):
                if j < 8:
                    self.op("tensor", (lambda t, j=j, ph=ph: t.matmul(
                        ph[0:NS, 8 + (j % 2) * 2:9 + (j % 2) * 2], A["q64f"][0:NS, 0:NS],
                        A["Hsb"][0:NS, j:j + 1], start=True, stop=False)),
                        reads=[hd_done] if j == 0 else [cv.mark()])
                    self.op("tensor", (lambda t, j=j, ph=ph: t.matmul(
                        ph[0:NS, 8 + (j % 2) * 2:9 + (j % 2) * 2], A["i32f"][0:NS, 0:NS],
                        A["hdsb"][0:NS, j:j + 1], start=False, stop=True)),
                        incs=[cp])
                    self.op("vector", (lambda v, j=j, ph=ph: v.tensor_copy(
                        A["Hsb"][0:NS, j + 1:j + 2], ph[0:NS, 8 + (j % 2) * 2:9 + (j % 2) * 2])),
                        reads=[cp.mark()], incs=[cv])
                self.op("tensor", (lambda t, j=j, py=py, ubf=ubf: t.matmul(
                    py[0:8, 0:TSC], ubf[0:NS, j:512:TSC],
                    A["k0ext"][0:NS, TSC - j:2 * TSC - j], start=(j == 0), stop=False)),
                    reads=[u_done, cv.mark()] if j == 0 else [])
            self.op("vector", (lambda v: v.tensor_copy(A["HBb"][0:NS, 0:8], A["Hsb"][0:NS, 0:8])),
                    reads=[cv.mark()], incs=[cv])
            self.op("vector", (lambda v: v.tensor_copy(A["hcur"][0:NS, 0:1], A["Hsb"][0:NS, 8:9])))
            self.op("tensor", (lambda t, py=py: t.matmul(
                py[0:8, 0:TSC], A["HBb"][0:NS, 0:8], A["w2t"][0:NS, 0:TSC],
                start=False, stop=True)), reads=[cv.mark()], incs=[cp])
            self.op("vector", (lambda v, py=py, ylsb=ylsb: v.tensor_copy(
                ylsb[0:8, 0:TSC], py[0:8, 0:TSC])),
                reads=[cp.mark()], incs=[cv])
            self.op("sync", (lambda s, ylsb=ylsb, yrow=yrow: s.dma_start(
                out=yrow[0:1, 0:512].rearrange("o (c t) -> o c t", c=8),
                in_=ylsb[0:8, 0:TSC])),
                reads=[cv.mark()], incs=[d_yr])
            pyb = self.bank(2)
            self.op("tensor", (lambda t, pyb=pyb, yrow=yrow: t.matmul(
                pyb[:, 0:512], A["ones1f"][0:1, 0:128], yrow[0:1, 0:512],
                start=True, stop=True)),
                reads=[d_yr.mark(), conv_free[0], conv_free[1]], incs=[cp])
            ybc_done = cp.mark()
            for f in range(16):
                b = f % 2
                self.op("scalar", (lambda a, f=f, b=b, sgb=sgb, gateb=gateb: a.activation(
                    sgb[:, b * 512:(b + 1) * 512], gateb[:, f * 512:(f + 1) * 512],
                    AF.Silu)),
                    reads=[sg_free[b]], incs=[ca])
                self.op("vector", (lambda v, f=f, pyb=pyb, m1s=m1s: v.tensor_scalar(
                    m1s[:, 0:512], pyb[:, 0:512], A["dv_c"][:, f:f + 1], None,
                    op0=ALU.add)), reads=[ybc_done])
                self.op("vector", (lambda v, f=f, m1s=m1s, t2s=t2s, xhs=xhs: v.tensor_mul(
                    t2s[:, 0:512], m1s[:, 0:512], xhs[:, f * 512:(f + 1) * 512])))
                self.op("vector", (lambda v, f=f, b=b, t2s=t2s, xhs=xhs, sgb=sgb: v.tensor_mul(
                    xhs[:, f * 512:(f + 1) * 512], t2s[:, 0:512],
                    sgb[:, b * 512:(b + 1) * 512])),
                    reads=[ca.mark()], incs=[cv])
                sg_free[b] = cv.mark()
            yfin_done = cv.mark()
            for sub in range(4):
                for half in range(2):
                    bsel = sub * 2 + half
                    ps = self.bank(bsel)
                    self.op("tensor", (lambda t, half=half, ps=ps: t.matmul(
                        ps[:, 0:512], A["ones1b"][0:1, :],
                        A["bout_rowb"][0:1, half * 512:(half + 1) * 512],
                        start=True, stop=False)),
                        reads=([yfin_done, xz_free[0], xz_free[1]]
                               if (sub == 0 and half == 0) else []))
            for hk in range(16):
                b = hk % 2
                self.op("gpsimd", (lambda g, hk=hk, b=b, woutt_b=woutt_b: g.dma_start(
                    out=woutt_b[:, b * 1024:(b + 1) * 1024],
                    in_=self.wouttd[hk * 128:(hk + 1) * 128, :])),
                    reads=[woutt_free[b]], incs=[d_wo[b]])
                woutt_ld = d_wo[b].mark()
                for sub in range(4):
                    for half in range(2):
                        bsel = sub * 2 + half
                        ps = self.bank(bsel)
                        first = (hk == 0 and sub == 0 and half == 0)
                        self.op("tensor", (lambda t, hk=hk, b=b, sub=sub, half=half, ps=ps,
                                           xhs=xhs, woutt_b=woutt_b:
                                t.matmul(ps[:, 0:512],
                                         xhs[:, hk * 512 + sub * 128:hk * 512 + (sub + 1) * 128],
                                         woutt_b[:, b * 1024 + half * 512:b * 1024 + (half + 1) * 512],
                                         start=False, stop=(hk == 15))),
                                reads=([yfin_done, woutt_ld, xz_free[0], xz_free[1]]
                                       if first else ([woutt_ld] if (sub == 0 and half == 0) else [])),
                                incs=[cp] if (sub == 3 and half == 1) else [])
                woutt_free[b] = cp.mark()
            oproj_done = cp.mark()
            prev_oproj = oproj_done
            for sub in range(4):
                for half in range(2):
                    bsel = sub * 2 + half
                    ps = self.bank(bsel)
                    self.op("vector", (lambda v, sub=sub, half=half, ps=ps, m1s=m1s, xntc=xntc:
                            v.tensor_add(
                        m1s[:, 512:1024], ps[:, 0:512],
                        xntc[:, sub * 1024 + half * 512:sub * 1024 + (half + 1) * 512])),
                        reads=[oproj_done, gld, outst_free] if (sub == 0 and half == 0) else [])
                    self.op("vector", (lambda v, sub=sub, half=half, t2s=t2s, m1s=m1s, gtok=gtok:
                            v.tensor_mul(
                                t2s[:, 0:512], m1s[:, 512:1024],
                                gtok[:, sub * 1024 + half * 512:sub * 1024 + (half + 1) * 512])))
                    self.op("vector", (lambda v, sub=sub, half=half, outst=outst, t2s=t2s, xc=xc:
                            v.tensor_add(
                                outst[:, sub * 1024 + half * 512:sub * 1024 + (half + 1) * 512],
                                t2s[:, 0:512],
                                xc[:, sub * 1024 + half * 512:sub * 1024 + (half + 1) * 512])),
                            incs=[cv] if half == 1 else [])
            comb_done = cv.mark()
            prev_comb = comb_done
            xz_free = [comb_done, comb_done]
            conv_free = [comb_done, comb_done]
            self.op("sync", (lambda s, t0=t0, p=p, outst=outst: s.dma_start(
                out=self.outd[p, t0:t0 + CW, :].rearrange("(n p) f -> p n f", p=128),
                in_=outst[:, :].rearrange("p (n f) -> p n f", n=4))),
                reads=[comb_done], incs=[d_out])
            outst_free = d_out.mark()
            xc_free = comb_done
            gtok_free = comb_done

        tapsrc = {"xnf": xnfc, "xh": xhb, "gate": gateb, "u": ubf, "yrowt": yrow, "yfin": xhs}
        for nm, hnd in tapsrc.items():
            if nm in self.taps:
                sh = list(hnd.shape)
                dt = hnd.ap().dtype
                tpp = self.mktap(nm, [self.npass] + sh, dt)
                n = sh[0]
                self.op("sync", (lambda s, tpp=tpp, hnd=hnd, n=n, p=p: s.dma_start(
                    out=tpp[p][0:n, :], in_=hnd[0:n, :])),
                    reads=[cv.mark(), ca.mark()], incs=[d_tap])



_BUILD_CACHE = {}


def get_build(npass=4, depth=6, taps=()):
    key = (npass, depth, tuple(sorted(taps)))
    if key not in _BUILD_CACHE:
        _BUILD_CACHE[key] = KB(npass, depth, taps).build()
    return _BUILD_CACHE[key]


def kernel(**inputs):
    kb = get_build(4, 6)
    x = np.ascontiguousarray(inputs["L0_attn_x"])
    in_maps = []
    for c in range(8):
        m = {k: np.ascontiguousarray(v) for k, v in inputs.items() if k != "L0_attn_x"}
        m["L0_attn_x"] = np.ascontiguousarray(x[c * 4:(c + 1) * 4])
        in_maps.append(m)
    res = run_bass_kernel_spmd(kb.nc, in_maps, core_ids=list(range(8)))
    return np.concatenate([r["out"] for r in res.results], axis=0)


# version 24 (final)
# speedup vs baseline: 1.0612x; 1.0093x over previous
import contextlib
import numpy as np
import ml_dtypes
import concourse.bass as bass
import concourse.mybir as mybir
from concourse.bass_utils import run_bass_kernel_spmd

F32 = mybir.dt.float32
BF16 = mybir.dt.bfloat16
AF = mybir.ActivationFunctionType
ALU = mybir.AluOpType
AX = mybir.AxisListType

S = 2048
D = 1024
H2 = 2048
NS = 32
TSC = 64
CW = 512
NCHK = S // CW
WIN = 256
HEADS = 16
DH = 64
DI = 2730
MSK = -1.0e30
GCOL = 2049

ENGINES = ("tensor", "vector", "scalar", "gpsimd", "sync")


class Chan:
    __slots__ = ("name", "sem", "count", "step")

    def __init__(self, name, sem, step):
        self.name = name
        self.sem = sem
        self.count = 0
        self.step = step

    def mark(self):
        return (self, self.count)


class KB:

    def __init__(self, npass=4, depth=6, taps=()):
        self.npass = npass
        self.depth = depth
        self.taps = set(taps)
        self.nc = bass.Bass("TRN2", target_bir_lowering=False)
        self.st = contextlib.ExitStack()
        self.ops = {e: [] for e in ENGINES}
        self._waited = {e: {} for e in ENGINES}
        self.chans = []
        self.A = {}
        self.tap_d = {}

    def chan(self, name, step=1):
        sem = self.st.enter_context(self.nc.semaphore(name))
        c = Chan(name, sem, step)
        self.chans.append(c)
        return c

    def op(self, engine, fn, reads=(), incs=()):
        waits = []
        w = self._waited[engine]
        for rd in reads:
            if rd is None:
                continue
            c, cnt = rd
            if c is None or cnt <= 0:
                continue
            if w.get(c.name, 0) < cnt:
                w[c.name] = cnt
                waits.append((c, cnt))
        incl = []
        selfw = None
        auto = {"vector": self.cv, "scalar": self.ca}.get(engine)
        if auto is not None:
            selfw = (auto, auto.count)
            auto.count += auto.step
            incl.append((auto, auto.step))
            for c in incs:
                if c is auto:
                    continue
                c.count += c.step
                incl.append((c, c.step))
        else:
            for c in incs:
                c.count += c.step
                incl.append((c, c.step))
        self.ops[engine].append((fn, waits, incl, selfw))

    def fence(self, engine):
        rd = [(c, c.count) for c in self.chans if c.count > 0]
        if rd:
            self.op(engine, lambda e: None, reads=rd)

    def bar(self, engines=ENGINES):
        rd = [(c, c.count) for c in self.chans if c.count > 0]
        for e in engines:
            self.op(e, lambda eng: None, reads=rd)

    def emit(self):
        for c in self.chans:
            assert c.count < 60000, f"chan {c.name} count {c.count} too high"
        with self.nc.Block() as block:
            def run(eng_obj, name):
                for (fn, waits, incl, selfw) in self.ops[name]:
                    if selfw is not None and selfw[1] > 0:
                        eng_obj.wait_ge(selfw[0].sem, selfw[1])
                    for (c, cnt) in waits:
                        eng_obj.wait_ge(c.sem, cnt)
                    ins = fn(eng_obj)
                    if ins is None and incl:
                        ins = eng_obj.nop()
                    for i, (c, stp) in enumerate(incl):
                        if i == 0:
                            ins.then_inc(c.sem, stp)
                        else:
                            eng_obj.nop().then_inc(c.sem, stp)
            block.tensor(lambda t: run(t, "tensor"))
            block.vector(lambda v: run(v, "vector"))
            block.scalar(lambda a: run(a, "scalar"))
            block.gpsimd(lambda g: run(g, "gpsimd"))
            block.sync(lambda s: run(s, "sync"))

    def sb(self, name, shape, dtype):
        t = self.nc.alloc_sbuf_tensor(name, shape, dtype)
        self.A[name] = t
        return t

    _uid = 0

    def sbs(self, scope, name, shape, dtype, side=None):
        KB._uid += 1
        t = scope.enter_context(self.nc.sbuf_tensor(f"{name}__{KB._uid}", shape, dtype, side=side))
        self.A[name] = t
        return t

    def bank(self, i):
        return self.PB[i].ap()

    def bankb(self, i):
        return self.PB[i].ap().bitcast(BF16)

    def mktap(self, name, shape, dtype=F32):
        if name in self.taps and name not in self.tap_d:
            self.tap_d[name] = self.nc.dram_tensor("tap_" + name, shape, dtype,
                                                   kind="ExternalOutput").ap()
        return self.tap_d.get(name)

    def build(self):
        nc = self.nc
        d = {}
        specs = [
            ("L0_attn_x", [4, S, D]), ("L0_attn_W_in", [2 * H2, D]), ("L0_attn_b_in", [2 * H2]),
            ("L0_attn_conv_w", [H2, 1, 4]), ("L0_attn_conv_b", [H2]),
            ("L0_attn_W_ssm", [NS, H2]), ("L0_attn_b_ssm", [NS]), ("L0_attn_A", [NS, NS]),
            ("L0_attn_Bv", [NS]), ("L0_attn_Cv", [NS]), ("L0_attn_Dv", [H2]),
            ("L0_attn_W_out", [D, H2]), ("L0_attn_b_out", [D]),
            ("L0_ffn_Wqkv", [6, D, 3 * D]), ("L0_ffn_q_scale", [6, DH]), ("L0_ffn_k_scale", [6, DH]),
            ("L0_ffn_Wo", [6, D, D]), ("L0_ffn_ln_g", [6, D]), ("L0_ffn_ln_b", [6, D]),
            ("L0_ffn_W1", [6, D, 2 * DI]), ("L0_ffn_b1", [6, 2 * DI]),
            ("L0_ffn_W2", [6, DI, D]), ("L0_ffn_b2", [6, D]),
        ]
        for (n, sh) in specs:
            d[n] = nc.dram_tensor(n, sh, F32, kind="ExternalInput").ap()
        self.d = d
        self.outd = nc.dram_tensor("out", [4, S, D], F32, kind="ExternalOutput").ap()
        self.wintd = nc.dram_tensor("wint_scr", [D, 2 * H2], BF16, kind="Internal").ap()
        self.wouttd = nc.dram_tensor("woutt_scr", [H2, D], BF16, kind="Internal").ap()
        self.gscrd = nc.dram_tensor("g_scr", [D, S], BF16, kind="Internal").ap()
        self.w1cd = nc.dram_tensor("w1_cache", [6, 128, 44, 8, 128], BF16, kind="Internal").ap()
        self.w2cd = nc.dram_tensor("w2_cache", [6, 22, 128, 1024], BF16, kind="Internal").ap()
        self.wqcd = nc.dram_tensor("wq_cache", [6, 2, 128, 4 * 3072], BF16, kind="Internal").ap()
        self.wocd = nc.dram_tensor("wo_cache", [6, 128, 8 * 1024], BF16, kind="Internal").ap()
        self.wb_marks = {}
        self.w1_slot_guard = {i: [] for i in range(6)}
        self.w2_slot_guard = {i: [] for i in range(4)}
        self.wq_guard = []
        self.wo_guard = []

        self.PB = [self.nc.alloc_psum_tensor(f"pbank{i}", [128, 512], F32) for i in range(8)]

        self.cp = self.chan("cp")
        self.cv = self.chan("cv")
        self.ca = self.chan("ca")
        self.cg = self.chan("cg")
        self.d_cst = self.chan("d_cst", 16)
        self.d_row = self.chan("d_row", 16)
        self.d_wst = self.chan("d_wst", 16)
        self.d_tst = self.chan("d_tst", 16)
        self.d_af = self.chan("d_af", 16)
        self.d_xa = [self.chan("d_xa0", 16), self.chan("d_xa1", 16)]
        self.d_g = [self.chan("d_g0", 16), self.chan("d_g1", 16)]
        self.d_xc = self.chan("d_xc", 16)
        self.d_gt = self.chan("d_gt", 16)
        self.d_wb = [self.chan(f"d_wb{i}", 16) for i in range(4)]
        self.d_wc = self.chan("d_wc", 16)
        self.d_wo = [self.chan("d_wo0", 16), self.chan("d_wo1", 16)]
        self.d_yr = self.chan("d_yr", 16)
        self.d_out = self.chan("d_out", 16)
        self.d_tap = self.chan("d_tap", 16)
        self.d_rot = [self.chan("d_rot0", 16), self.chan("d_rot1", 16)]
        self.d_wq = self.chan("d_wq", 16)
        self.d_woL = self.chan("d_woL", 16)
        self.d_br = [self.chan("d_br0", 16), self.chan("d_br1", 16)]
        self.d_w1 = [self.chan(f"d_w1{i}", 16) for i in range(6)]
        self.d_w2 = [self.chan("d_w2%d" % i, 16) for i in range(4)]
        self.store_chans = [self.d_tst, self.d_g[0], self.d_g[1], self.d_out, self.d_tap]
        self.ticks = {e: self.chan("tk_" + e) for e in ENGINES}

        self.consts_setup()
        self.globals_alloc()
        with contextlib.ExitStack() as sc0:
            self.phase0(sc0)
        self.bar()
        for p in range(self.npass):
            self.cp = self.chan(f"cp{p}")
            self.cv = self.chan(f"cv{p}")
            self.ca = self.chan(f"ca{p}")
            self.cg = self.chan(f"cg{p}")
            with contextlib.ExitStack() as scAB:
                self.scAB = scAB
                self.wq_free = None
                self.phase_A(p)
                self.bar()
                for l in range(self.depth):
                    with contextlib.ExitStack() as scL:
                        self.scL = scL
                        self.layer_attn(p, l)
                    self.bar()
                    with contextlib.ExitStack() as scL:
                        self.scL = scL
                        self.layer_ffn(p, l)
                    self.bar()
                self.g_store(p)
                self.bar()
            with contextlib.ExitStack() as scC:
                self.scC = scC
                self.phase_C(p)
                self.bar()
        self.op("sync", lambda s: None, reads=[c.mark() for c in self.store_chans])
        self.emit()
        self.st.close()
        return self

    def consts_setup(self):
        half = np.arange(0, DH, 2, dtype=np.float64)
        inv_freq = 1.0 / 10000.0 ** (half / DH)
        t = np.arange(S, dtype=np.float64)
        freqs = np.concatenate([t[:, None] * inv_freq[None, :]] * 2, -1)
        base_scale = (half + 0.4 * DH) / (1.4 * DH)
        power = (t - S // 2) / (WIN // 2)
        scale = np.concatenate([base_scale[None, :] ** power[:, None]] * 2, -1)
        cos, sin = np.cos(freqs), np.sin(freqs)
        sgn = np.where(np.arange(DH) < DH // 2, -1.0, 1.0)[None, :]
        def dup(tab):
            return np.ascontiguousarray(np.concatenate([tab, tab], 0).astype(np.float32))
        CS = dup((cos * scale).T)
        SS = dup((sin * scale * sgn).T)
        CI = dup((cos / scale).T)
        SI = dup((sin / scale * sgn).T)
        kp = np.arange(128)[:, None]
        qf = np.arange(256)[None, :]
        sTm = []
        for m in range(4):
            rel = m * 128 + kp - 256
            valid = (qf - rel >= 0) & (qf - rel <= WIN)
            sTm.append(np.where(valid, 0.0, MSK).astype(ml_dtypes.bfloat16))
        qp = np.arange(128)[:, None]
        s1m = []
        for qc in range(2):
            cols = []
            for mi in range(3):
                m = qc + mi
                kpp = np.arange(128)[None, :]
                rel = m * 128 + kpp - 256
                qg = qc * 128 + qp
                valid = (qg - rel >= 0) & (qg - rel <= WIN)
                cols.append(np.where(valid, 0.0, MSK))
            s1m.append(np.concatenate(cols, 1).astype(ml_dtypes.bfloat16))
        pswap = np.zeros((128, 128), dtype=ml_dtypes.bfloat16)
        for dd in range(128):
            blk = (dd // 64) * 64
            pswap[blk + ((dd - blk + 32) % 64), dd] = 1.0
        bones = np.zeros((128, 128), dtype=ml_dtypes.bfloat16)
        bones[0:64, 0:64] = 1.0
        bones[64:128, 64:128] = 1.0
        onesfull = np.ones((128, 128), dtype=ml_dtypes.bfloat16)
        self.consts = {
            "identb": np.eye(128, dtype=ml_dtypes.bfloat16),
            "onesfull": onesfull,
            "bones": bones, "pswap": pswap,
            "onescolb": np.ones((128, 1), dtype=ml_dtypes.bfloat16),
            "ones1b": np.ones((1, 128), dtype=ml_dtypes.bfloat16),
            "ones1f": np.ones((1, 128), dtype=np.float32),
            "one11f": np.ones((1, 1), dtype=np.float32),
            "i32f": np.eye(NS, dtype=np.float32),
            "sTm01": np.ascontiguousarray(np.concatenate([sTm[0], sTm[1]], 1)),
            "sTm23": np.ascontiguousarray(np.concatenate([sTm[2], sTm[3]], 1)),
        }
        self.cdram = {k: self.nc.inline_tensor(np.ascontiguousarray(v), name="c_" + k)
                      for k, v in self.consts.items()}
        rot4 = np.concatenate([t.astype(ml_dtypes.bfloat16)[None] for t in (CS, SS, CI, SI)], 0)
        self.rot4d = self.nc.inline_tensor(np.ascontiguousarray(rot4), name="c_rot4").ap()

    def globals_alloc(self):
        for k, v in self.consts.items():
            dt = BF16 if v.dtype == ml_dtypes.bfloat16 else F32
            self.sb(k, list(v.shape), dt)
        for (n, sh) in [("bin_c", [128, 32]), ("convw_c", [128, 64]), ("convb_c", [128, 16]),
                        ("dv_c", [128, 16]), ("bssm_c", [NS, 1]), ("bv_c", [NS, 1]),
                        ("cv_c", [NS, 1]), ("lng_c", [128, 48]),
                        ("lnb_c", [128, 48]), ("b1_c", [128, 264]), ("b2_c", [128, 48]),
                        ("qs_c", [128, 6]), ("ks_c", [128, 6]), ("q64f", [NS, NS]),
                        ("a_f", [NS, NS]), ("wchain", [NS, TSC + 1]), ("hcur", [NS, 1]),
                        ("hdsb", [NS, 8]), ("Hsb", [NS, 9]), ("lncol", [128, 16]), ("epsc", [128, 1]), ("epsq", [128, 1])]:
            self.sb(n, sh, F32)
        self.sb("bout_rowb", [1, D], BF16)
        for (n, sh) in [("wssmt", [128, 16 * NS]), ("k0ext", [NS, 2 * TSC]),
                        ("pbext", [NS, TSC * NS]), ("w2t", [NS, TSC]), ("HBb", [NS, 8])]:
            self.sb(n, sh, BF16)

    def phase0(self, sc0):
        A, d = self.A, self.d
        cp, cv, ca = self.cp, self.cv, self.ca
        d_cst, d_row, d_wst, d_tst = self.d_cst, self.d_row, self.d_wst, self.d_tst
        self.op("vector", lambda v: v.memset(A["epsc"][:, 0:1], 1e-5))
        self.op("vector", lambda v: v.memset(A["epsq"][:, 0:1], 1e-24))
        for k in self.consts:
            src = self.cdram[k].ap()
            npart = self.consts[k].shape[0]
            self.op("gpsimd", (lambda g, a=A[k], s=src, n=npart: g.dma_start(out=a[0:n, :], in_=s)),
                    incs=[d_cst])
        const_ld = d_cst.mark()
        rb = self.sbs(sc0, "rowbuf", [1, 8192], F32)
        qtmp = self.sbs(sc0, "qtmp", [NS, 2 * NS], F32)

        def load_row(dst, src):
            self.op("sync", (lambda s, dd=dst, ii=src: s.dma_start(out=dd, in_=ii)),
                    reads=[cp.mark()], incs=[d_row])

        def colz(dst_fn, srcs, psel=0):
            ps = self.bank(psel)
            for j, src in enumerate(srcs):
                w = src.shape[-1]
                self.op("tensor", (lambda t, ap=src, w=w, j=j, ps=ps: t.matmul(
                    ps[0:w, j:j + 1], ap, A["one11f"][0:1, 0:1], start=True, stop=True)),
                    reads=[d_row.mark(), cv.mark(), const_ld] if j == 0 else [],
                    incs=[cp] if j == len(srcs) - 1 else [])
            n = len(srcs)
            widths = [s.shape[-1] for s in srcs]
            def ev(v, n=n, ps=ps, widths=widths):
                last = None
                for j in range(n):
                    dst = dst_fn(j)
                    last = v.tensor_copy(dst, ps[0:widths[j], j:j + 1])
                return last
            self.op("vector", ev, reads=[cp.mark()], incs=[cv])

        load_row(rb[0:1, 0:4096], d["L0_attn_b_in"].rearrange("(o f) -> o f", o=1))
        colz(lambda j: A["bin_c"][:, j:j + 1], [rb[0:1, j * 128:(j + 1) * 128] for j in range(32)])
        load_row(rb[0:1, 0:8192], d["L0_attn_conv_w"].rearrange("c o k -> o (c k)"))
        for k in range(4):
            colz((lambda j, k=k: A["convw_c"][:, k * 16 + j:k * 16 + j + 1]),
                 [rb[0:1, (j * 512 + k):(j * 512 + k) + 509:4] for j in range(16)], psel=k % 2)
        load_row(rb[0:1, 0:2048], d["L0_attn_conv_b"].rearrange("(o f) -> o f", o=1))
        colz(lambda j: A["convb_c"][:, j:j + 1], [rb[0:1, j * 128:(j + 1) * 128] for j in range(16)])
        load_row(rb[0:1, 0:2048], d["L0_attn_Dv"].rearrange("(o f) -> o f", o=1))
        colz(lambda j: A["dv_c"][:, j:j + 1], [rb[0:1, j * 128:(j + 1) * 128] for j in range(16)], psel=1)
        load_row(rb[0:1, 0:32], d["L0_attn_b_ssm"].rearrange("(o f) -> o f", o=1))
        load_row(rb[0:1, 32:64], d["L0_attn_Bv"].rearrange("(o f) -> o f", o=1))
        load_row(rb[0:1, 64:96], d["L0_attn_Cv"].rearrange("(o f) -> o f", o=1))
        colz(lambda j: [A["bssm_c"], A["bv_c"], A["cv_c"]][j][0:NS, 0:1],
             [rb[0:1, 0:NS], rb[0:1, 32:64], rb[0:1, 64:96]])
        load_row(rb[0:1, 0:1024], d["L0_attn_b_out"].rearrange("(o f) -> o f", o=1))
        self.op("vector", (lambda v: v.tensor_copy(A["bout_rowb"][0:1, 0:D], rb[0:1, 0:1024])),
                reads=[d_row.mark()], incs=[cv])
        self.op("tensor", lambda t: t.nop(), reads=[cv.mark()], incs=[cp])
        for l in range(6):
            load_row(rb[0:1, 0:1024], d["L0_ffn_ln_g"][l].rearrange("(o f) -> o f", o=1))
            colz((lambda j, l=l: A["lng_c"][:, l * 8 + j:l * 8 + j + 1]),
                 [rb[0:1, j * 128:(j + 1) * 128] for j in range(8)])
            load_row(rb[0:1, 0:1024], d["L0_ffn_ln_b"][l].rearrange("(o f) -> o f", o=1))
            colz((lambda j, l=l: A["lnb_c"][:, l * 8 + j:l * 8 + j + 1]),
                 [rb[0:1, j * 128:(j + 1) * 128] for j in range(8)], psel=1)
            load_row(rb[0:1, 0:1024], d["L0_ffn_b2"][l].rearrange("(o f) -> o f", o=1))
            colz((lambda j, l=l: A["b2_c"][:, l * 8 + j:l * 8 + j + 1]),
                 [rb[0:1, j * 128:(j + 1) * 128] for j in range(8)])
            load_row(rb[0:1, 0:5460], d["L0_ffn_b1"][l].rearrange("(o f) -> o f", o=1))
            srcs = []
            for j in range(22):
                w = min(128, DI - j * 128)
                srcs.append(rb[0:1, j * 128:j * 128 + w])
            for j in range(22):
                w = min(128, DI - j * 128)
                srcs.append(rb[0:1, DI + j * 128:DI + j * 128 + w])
            colz((lambda j, l=l: A["b1_c"][0:(128 if (j % 22) < 21 else DI - 21 * 128),
                                           l * 44 + j:l * 44 + j + 1]), srcs, psel=1)
            load_row(rb[0:1, 0:64], d["L0_ffn_q_scale"][l].rearrange("(o f) -> o f", o=1))
            load_row(rb[0:1, 64:128], d["L0_ffn_k_scale"][l].rearrange("(o f) -> o f", o=1))
            for (nm, off) in (("qs_c", 0), ("ks_c", 64)):
                self.op("tensor", (lambda t, o=off: t.matmul(
                    self.bank(0)[0:64, 0:1], rb[0:1, o:o + 64], A["one11f"][0:1, 0:1],
                    start=True, stop=True)),
                    reads=[d_row.mark(), cv.mark()])
                self.op("tensor", (lambda t, o=off: t.matmul(
                    self.bank(0)[64:128, 0:1], rb[0:1, o:o + 64], A["one11f"][0:1, 0:1],
                    start=True, stop=True)), incs=[cp])
                self.op("vector", (lambda v, nm=nm, l=l: v.tensor_copy(
                    A[nm][:, l:l + 1], self.bank(0)[0:128, 0:1])), reads=[cp.mark()], incs=[cv])

        wstage = self.sbs(sc0, "wstage", [128, 2048], BF16)
        tstage = self.sbs(sc0, "tstage", [128, 2048], BF16)

        def transpose_weight(src2d, rows, cols, dst_dram):
            nrc, ncc = rows // 128, cols // 128
            for rc in range(nrc):
                self.op("gpsimd", (lambda g, rc=rc, src2d=src2d, cols=cols: g.dma_start(
                    out=wstage[:, 0:cols], in_=src2d[rc * 128:(rc + 1) * 128, :])),
                    reads=[cp.mark()], incs=[d_wst])
                ld = d_wst.mark()
                for cc in range(ncc):
                    psb = self.bankb(4) if (cc // 8) % 2 == 0 else self.bankb(5)
                    self.op("tensor", (lambda t, cc=cc, psb=psb: t.transpose(
                        psb[:, (cc % 8) * 128:(cc % 8) * 128 + 128],
                        wstage[:, cc * 128:(cc + 1) * 128], A["identb"][:, :])),
                        reads=[ld, cv.mark()] if cc % 8 == 0 else [],
                        incs=[cp] if (cc % 8 == 7 or cc == ncc - 1) else [])
                    if cc % 8 == 7 or cc == ncc - 1:
                        lo = (cc // 8) * 8
                        def ev(v, lo=lo, hi=cc, psb=psb):
                            last = None
                            for c2 in range(lo, hi + 1):
                                last = v.tensor_copy(tstage[:, c2 * 128:(c2 + 1) * 128],
                                                     psb[:, (c2 % 8) * 128:(c2 % 8) * 128 + 128])
                            return last
                        self.op("vector", ev, reads=[cp.mark(), d_tst.mark()], incs=[cv])
                self.op("sync", (lambda s, rc=rc, dst=dst_dram, ncc=ncc: s.dma_start(
                    out=dst.rearrange("(cc p) r -> p cc r", p=128)[:, :, rc * 128:(rc + 1) * 128],
                    in_=tstage[:, 0:ncc * 128].rearrange("p (cc r) -> p cc r", cc=ncc))),
                    reads=[cv.mark()], incs=[d_tst])

        transpose_weight(d["L0_attn_W_in"], 2 * H2, D, self.wintd)
        transpose_weight(d["L0_attn_W_out"], D, H2, self.wouttd)
        self.op("gpsimd", (lambda g: g.dma_start(out=wstage[0:NS, 0:H2], in_=d["L0_attn_W_ssm"])),
                reads=[cp.mark()], incs=[d_wst])
        wld = d_wst.mark()
        for cc in range(16):
            psb = self.bankb(4) if cc % 2 == 0 else self.bankb(5)
            self.op("tensor", (lambda t, cc=cc, psb=psb: t.transpose(
                psb[:, 0:NS], wstage[0:NS, cc * 128:(cc + 1) * 128], A["identb"][0:NS, 0:NS])),
                reads=[wld, cv.mark()], incs=[cp])
            self.op("vector", (lambda v, cc=cc, psb=psb: v.tensor_copy(
                A["wssmt"][:, cc * NS:(cc + 1) * NS], psb[:, 0:NS])),
                reads=[cp.mark()], incs=[cv])

        self.op("sync", (lambda s: s.dma_start(out=A["a_f"][0:NS, 0:NS], in_=d["L0_attn_A"])),
                incs=[self.d_af])
        ph32 = self.bank(6)
        self.op("vector", lambda v: v.tensor_copy(A["wchain"][0:NS, 0:1], A["cv_c"][0:NS, 0:1]),
                reads=[cv.mark()], incs=[cv])
        for d2 in range(TSC):
            self.op("tensor", (lambda t, d2=d2: t.matmul(
                ph32[0:NS, 0:1], A["a_f"][0:NS, 0:NS], A["wchain"][0:NS, d2:d2 + 1],
                start=True, stop=True)),
                reads=[cv.mark(), self.d_af.mark()], incs=[cp])
            self.op("vector", (lambda v, d2=d2: v.tensor_copy(
                A["wchain"][0:NS, d2 + 1:d2 + 2], ph32[0:NS, 0:1])),
                reads=[cp.mark()], incs=[cv])
        self.op("vector", lambda v: v.memset(A["k0ext"][0:NS, 0:TSC], 0.0))
        self.op("vector", (lambda v: v.tensor_scalar(
            A["k0ext"][0:NS, TSC:2 * TSC], A["wchain"][0:NS, 0:TSC], A["bv_c"][0:NS, 0:1], None,
            op0=ALU.mult)))
        self.op("vector", lambda v: v.tensor_copy(A["w2t"][0:NS, 0:TSC], A["wchain"][0:NS, 1:TSC + 1]))
        self.op("vector", lambda v: v.tensor_copy(qtmp[0:NS, 0:NS], A["i32f"][0:NS, 0:NS]), incs=[cv])
        for e in range(TSC + 1):
            cur = (e % 2) * NS
            nxt = ((e + 1) % 2) * NS
            if e < TSC:
                self.op("vector", (lambda v, e=e, cur=cur: v.tensor_scalar(
                    A["pbext"][0:NS, e * NS:(e + 1) * NS], qtmp[0:NS, cur:cur + NS],
                    A["bv_c"][0:NS, 0:1], None, op0=ALU.mult)))
            if e == TSC:
                self.op("vector", (lambda v, cur=cur: v.tensor_copy(
                    A["q64f"][0:NS, 0:NS], qtmp[0:NS, cur:cur + NS])), incs=[cv])
                break
            self.op("tensor", (lambda t, cur=cur: t.matmul(
                ph32[0:NS, 0:NS], A["a_f"][0:NS, 0:NS], qtmp[0:NS, cur:cur + NS],
                start=True, stop=True)),
                reads=[cv.mark()], incs=[cp])
            self.op("vector", (lambda v, nxt=nxt: v.tensor_copy(
                qtmp[0:NS, nxt:nxt + NS], ph32[0:NS, 0:NS])),
                reads=[cp.mark()], incs=[cv])

    def ln_tok(self, src_ap, dst_ap_bf, parity, extra_reads=()):
        A, cv, ca = self.A, self.cv, self.ca
        col = A["lncol"][:, parity * 8:parity * 8 + 8]
        self.op("vector", (lambda v, s=src_ap, col=col: v.reduce_sum(col[:, 0:1], s, axis=AX.X)),
                reads=list(extra_reads))
        self.op("vector", (lambda v, s=src_ap, col=col: v.scalar_tensor_tensor(
            A["lnsq"][:, 0:1024], s, 1.0, s, op0=ALU.mult, op1=ALU.mult,
            accum_out=col[:, 1:2])))
        self.op("vector", (lambda v, col=col: v.tensor_scalar(col[:, 0:1], col[:, 0:1], 1.0 / D, None, op0=ALU.mult)))
        self.op("vector", (lambda v, col=col: v.tensor_scalar(col[:, 1:2], col[:, 1:2], 1.0 / D, None, op0=ALU.mult)))
        self.op("vector", (lambda v, col=col: v.tensor_scalar(col[:, 2:3], col[:, 0:1], col[:, 0:1], None, op0=ALU.mult)))
        self.op("vector", (lambda v, col=col: v.tensor_sub(col[:, 3:4], col[:, 1:2], col[:, 2:3])), incs=[cv])
        self.op("scalar", (lambda a, col=col: a.activation(col[:, 4:5], col[:, 3:4], AF.Ln, bias=A["epsc"][:, 0:1])),
                reads=[cv.mark()], incs=[ca])
        self.op("scalar", (lambda a, col=col: a.activation(col[:, 5:6], col[:, 4:5], AF.Exp, scale=-0.5)),
                incs=[ca])
        self.op("vector", (lambda v, s=src_ap, dd=dst_ap_bf, col=col: v.tensor_scalar(
            dd, s, col[:, 0:1], col[:, 5:6], op0=ALU.subtract, op1=ALU.mult)),
            reads=[ca.mark()], incs=[cv])

    def phase_A(self, p):
        cp, cv = self.cp, self.cv
        d_xa = self.d_xa
        r = self.sbs(self.scAB, "r", [128, 8 * GCOL], F32, side="right")
        self.scA_local = contextlib.ExitStack()
        xa = self.sbs(self.scA_local, "xa", [128, 2 * 1024], F32)
        xnb = self.sbs(self.scA_local, "xnb", [128, 2 * 1024], BF16)
        self.sbs(self.scA_local, "lnsq", [128, 1024], BF16)
        identb = self.A["identb"]
        self.bar()
        for cc in range(8):
            self.op("vector", (lambda v, cc=cc, r=r: v.memset(r[:, cc * GCOL:cc * GCOL + 1], 0.0)))
        xa_free = [None, None]
        xnb_free = [None, None]
        pt_free = [None, None]
        for tt in range(16):
            b = tt % 2
            self.op("sync", (lambda s, tt=tt, b=b, p=p, xa=xa: s.dma_start(
                out=xa[:, b * 1024:(b + 1) * 1024],
                in_=self.d["L0_attn_x"][p, tt * 128:(tt + 1) * 128, :])),
                reads=[xa_free[b]], incs=[d_xa[b]])
            self.ln_tok(xa[:, b * 1024:(b + 1) * 1024], xnb[:, b * 1024:(b + 1) * 1024],
                        b, extra_reads=[d_xa[b].mark(), xnb_free[b]])
            xa_free[b] = cv.mark()
            psb = self.bankb(4) if b == 0 else self.bankb(5)
            for cc in range(8):
                self.op("tensor", (lambda t, cc=cc, b=b, psb=psb, xnb=xnb, identb=identb: t.transpose(
                    psb[:, cc * 128:(cc + 1) * 128],
                    xnb[:, b * 1024 + cc * 128:b * 1024 + (cc + 1) * 128], identb[:, :])),
                    reads=[cv.mark(), pt_free[b]] if cc == 0 else [],
                    incs=[cp] if cc == 7 else [])
            xnb_free[b] = cp.mark()
            def ev(v, tt=tt, psb=psb, r=r):
                last = None
                for cc in range(8):
                    last = v.tensor_copy(r[:, cc * GCOL + 1 + tt * 128:cc * GCOL + 1 + (tt + 1) * 128],
                                         psb[:, cc * 128:(cc + 1) * 128])
                return last
            self.op("vector", ev, reads=[cp.mark()], incs=[cv])
            pt_free[b] = cv.mark()
        self.r = r
        self.scA_local.close()

    def emit_xw_cast(self, wg, xw, r, scol, xw_free):
        cv = self.cv
        for cc in range(8):
            off = cc * GCOL + 1 + wg * 512 - (1 if cc >= 4 else 0)
            self.op("vector", (lambda v, cc=cc, off=off, xw=xw, r=r: v.tensor_copy(
                xw[:, cc * 512:(cc + 1) * 512], r[:, off:off + 512])),
                reads=[xw_free] if cc == 0 else [])
        if wg > 0:
            for cc in range(4, 8):
                self.op("vector", (lambda v, cc=cc, wg=wg, xw=xw, scol=scol: v.tensor_copy(
                    xw[:, cc * 512:cc * 512 + 1],
                    scol[:, (wg - 1) * 4 + (cc - 4):(wg - 1) * 4 + (cc - 4) + 1])))
        self.op("vector", lambda v: v.nop(), incs=[cv])
        return cv.mark()

    def layer_attn(self, p, l):
        A = self.A
        cp, cv, ca, cg = self.cp, self.cv, self.ca, self.cg
        d_wq, d_woL, d_br = self.d_wq, self.d_woL, self.d_br
        sc = self.scL
        r = self.r
        wqkv_b = self.sbs(sc, "wqkv_b", [128, 8 * 3072], BF16)
        wo_b = self.sbs(sc, "wo_b", [128, 8 * 1024], BF16)
        xw = self.sbs(sc, "xw", [128, 8 * 512], BF16)
        qb = self.sbs(sc, "qb", [128, 8 * 512], BF16)
        kbc = self.sbs(sc, "kbc", [128, 8 * 512], BF16)
        kbp = self.sbs(sc, "kbp", [128, 8 * 256], BF16)
        vbc = self.sbs(sc, "vbc", [128, 8 * 512], BF16)
        vbp = self.sbs(sc, "vbp", [128, 4 * 512], BF16)
        pT = self.sbs(sc, "pT", [128, 2 * 4 * 256], BF16)
        ao = self.sbs(sc, "ao", [128, 8 * 256], BF16)
        sq_s = self.sbs(sc, "sq_s", [128, 512], BF16)
        rn_s = self.sbs(sc, "rn_s", [128, 2 * 512], F32)
        qh_s = sq_s
        t1_s = self.sbs(sc, "t1_s", [128, 2 * 512], F32)
        zr = self.sbs(sc, "zr", [1, 512], F32)
        zb = self.sbs(sc, "zb", [128, 2 * 256], F32)
        scol = self.sbs(sc, "scol", [128, 4 * 4], F32)
        rotb = self.sbs(sc, "rotb", [128, 4 * 512], BF16)
        QK8 = 8.0

        for half in range(2):
            key = ("wq", l, half)
            if key not in self.wb_marks:
                self.op("gpsimd", (lambda g, half=half, l=l, wqkv_b=wqkv_b: g.dma_start(
                    out=wqkv_b[:, half * 4 * 3072:(half + 1) * 4 * 3072].rearrange(
                        "p (k f) -> p k f", k=4),
                    in_=self.d["L0_ffn_Wqkv"][l][half * 512:(half + 1) * 512, :].rearrange(
                        "(k p) f -> p k f", p=128))),
                    reads=[self.wq_free] + self.wq_guard, incs=[d_wq])
                ld = d_wq.mark()
                self.op("sync", (lambda s, half=half, l=l, wqkv_b=wqkv_b: s.dma_start(
                    out=self.wqcd[l][half],
                    in_=wqkv_b[:, half * 4 * 3072:(half + 1) * 4 * 3072])),
                    reads=[ld], incs=[self.d_wc])
                self.wb_marks[key] = self.d_wc.mark()
                self.wq_guard = [self.wb_marks[key]]
            else:
                qeng = "gpsimd" if half == 0 else "sync"
                self.op(qeng, (lambda g, half=half, l=l, wqkv_b=wqkv_b: g.dma_start(
                    out=wqkv_b[:, half * 4 * 3072:(half + 1) * 4 * 3072],
                    in_=self.wqcd[l][half])),
                    reads=[self.wq_free, self.wb_marks[key]], incs=[d_wq])
        key = ("wo", l)
        if key not in self.wb_marks:
            self.op("gpsimd", (lambda g, l=l, wo_b=wo_b: g.dma_start(
                out=wo_b[:, :].rearrange("p (k f) -> p k f", k=8),
                in_=self.d["L0_ffn_Wo"][l].rearrange("(k p) f -> p k f", p=128))),
                reads=self.wo_guard, incs=[d_woL])
            ld = d_woL.mark()
            self.op("sync", (lambda s, l=l, wo_b=wo_b: s.dma_start(
                out=self.wocd[l], in_=wo_b[:, :])),
                reads=[ld], incs=[self.d_wc])
            self.wb_marks[key] = self.d_wc.mark()
            self.wo_guard = [self.wb_marks[key]]
        else:
            for hh in range(2):
                qeng = "gpsimd" if hh == 0 else "sync"
                self.op(qeng, (lambda g, hh=hh, l=l, wo_b=wo_b: g.dma_start(
                    out=wo_b[:, hh * 4096:(hh + 1) * 4096],
                    in_=self.wocd[l][:, hh * 4096:(hh + 1) * 4096])),
                    reads=[self.wb_marks[key]] if hh == 0 else [], incs=[d_woL])
        wq_ld = d_wq.mark()
        wo_ld = d_woL.mark()

        pbf = {i: [] for i in range(8)}
        xw_free = None
        sqs_free = [[], []]
        t1_free = [[], []]
        nrm_free = None
        rn_free = [[], []]
        qh_free = [[], []]
        zr_free = []
        zb_free = [[], []]
        pending_norm = None
        gq_prev = []
        pT_free = [[], []]
        rot_free = []
        rot_ld = None
        for wg in range(4):
            for tb in range(4):
                self.op("gpsimd", (lambda g, tb=tb, wg=wg, rotb=rotb: g.dma_start(
                    out=rotb[:, tb * 512:(tb + 1) * 512],
                    in_=self.rot4d[tb][:, wg * 512:(wg + 1) * 512])),
                    reads=(rot_free if tb == 0 else []), incs=[self.d_rot[0]])
            rot_ld = self.d_rot[0].mark()
            if wg == 0:
                xw_done = self.emit_xw_cast(0, xw, r, scol, xw_free)
            else:
                xw_done = self.xw_done_next

            for g in range(2):
                sched = []
                for fc in range(4):
                    sched.append(("q", fc))
                    sched.append(("k", fc))
                    sched.append(("v", fc))
                ei = 0
                for (kind, fc) in sched:
                    if kind == "v":
                        tc = fc
                        bsel = 4 + (tc % 2)
                        ps = self.bank(bsel)
                        for k in range(8):
                            self.op("tensor", (lambda t, g=g, tc=tc, k=k, ps=ps, wqkv_b=wqkv_b,
                                               xw=xw: t.matmul(
                                ps[:, 0:512],
                                xw[:, k * 512 + tc * 128:k * 512 + tc * 128 + 128],
                                wqkv_b[:, k * 3072 + 2048 + g * 512:k * 3072 + 2048 + g * 512 + 512],
                                start=(k == 0), stop=(k == 7))),
                                reads=([xw_done, wq_ld] + pbf[bsel]) if k == 0 else [],
                                incs=[cp] if k == 7 else [])
                        self.op("vector", (lambda v, g=g, tc=tc, ps=ps, vbc=vbc: v.tensor_copy(
                            vbc[:, (g * 4 + tc) * 512:(g * 4 + tc) * 512 + 512], ps[:, 0:512])),
                            reads=[cp.mark()], incs=[cv])
                        pbf[bsel] = [cv.mark()]
                        continue
                    qk = 0 if kind == "q" else 1
                    if True:
                        gp = ei % 2
                        ei += 1
                        c0 = gp * 512
                        bsel = gp
                        b2 = 2 if gp == 0 else 6
                        b3 = 3 if gp == 0 else 7
                        ps = self.bank(bsel)
                        for k in range(8):
                            self.op("tensor", (lambda t, g=g, qk=qk, fc=fc, k=k, ps=ps,
                                               wqkv_b=wqkv_b, xw=xw: t.matmul(
                                ps[:, 0:512],
                                wqkv_b[:, k * 3072 + qk * 1024 + g * 512 + fc * 128:
                                       k * 3072 + qk * 1024 + g * 512 + fc * 128 + 128],
                                xw[:, k * 512:(k + 1) * 512],
                                start=(k == 0), stop=(k == 7))),
                                reads=([xw_done, wq_ld] + pbf[bsel]) if k == 0 else [],
                                incs=[cp] if k == 7 else [])
                        qkp_done = cp.mark()
                        self.op("scalar", (lambda a, ps=ps, sq_s=sq_s: a.activation(
                            sq_s[:, 0:512], ps[:, 0:512], AF.Square)),
                            reads=[qkp_done] + sqs_free[0], incs=[ca])
                        self.op("tensor", (lambda t, sq_s=sq_s, b2=b2: t.matmul(
                            self.bank(b2)[:, 0:512], A["bones"][:, :], sq_s[:, 0:512],
                            start=True, stop=True)),
                            reads=[ca.mark()] + pbf[b2], incs=[cp])
                        self.op("scalar", (lambda a, t1_s=t1_s, c0=c0, b2=b2: a.activation(
                            t1_s[:, c0:c0 + 512], self.bank(b2)[:, 0:512], AF.Ln,
                            bias=A["epsq"][:, 0:1])),
                            reads=[cp.mark()] + t1_free[gp], incs=[ca])
                        pbf[b2] = [ca.mark()]
                        self.op("scalar", (lambda a, t1_s=t1_s, rn_s=rn_s, c0=c0: a.activation(
                            rn_s[:, c0:c0 + 512], t1_s[:, c0:c0 + 512], AF.Exp, scale=-0.5)),
                            reads=rn_free[gp], incs=[ca])
                        nrm_free = ca.mark()
                        scale_col = A["qs_c"] if qk == 0 else A["ks_c"]
                        self.op("vector", (lambda v, ps=ps, rn_s=rn_s, qh_s=qh_s, c0=c0,
                                           scale_col=scale_col, l=l: v.scalar_tensor_tensor(
                            qh_s[:, 0:512], rn_s[:, c0:c0 + 512], scale_col[:, l:l + 1],
                            ps[:, 0:512], op0=ALU.mult, op1=ALU.mult)),
                            reads=[ca.mark()] + qh_free[0], incs=[cv])
                        rn_free[gp] = [cv.mark()]
                        pbf[bsel] = [cv.mark()]
                        self.op("tensor", (lambda t, qh_s=qh_s, b3=b3: t.matmul(
                            self.bank(b3)[:, 0:512], A["pswap"][:, :], qh_s[:, 0:512],
                            start=True, stop=True)),
                            reads=[cv.mark()] + pbf[b3], incs=[cp])
                        swp_done = cp.mark()
                        qh_free[0] = [swp_done]
                        ctab = rotb[:, (0 if qk == 0 else 2) * 512:
                                    (0 if qk == 0 else 2) * 512 + 512]
                        stabt = rotb[:, (1 if qk == 0 else 3) * 512:
                                     (1 if qk == 0 else 3) * 512 + 512]
                        stt_mark = cv.mark()
                        self.op("vector", (lambda v, t1_s=t1_s, stabt=stabt, c0=c0, b3=b3:
                                v.tensor_mul(t1_s[:, c0:c0 + 512], self.bank(b3)[:, 0:512],
                                             stabt)),
                                reads=[swp_done, rot_ld], incs=[cv])
                        t1mul_mark = cv.mark()
                        pbf[b3] = [t1mul_mark]
                        dest = (qb if qk == 0 else kbc)
                        dap = dest[0:128, (g * 4 + fc) * 512:(g * 4 + fc) * 512 + 512]
                        self.op("gpsimd", (lambda gg, dap=dap, qh_s=qh_s, ctab=ctab:
                                gg.tensor_mul(dap, qh_s[:, 0:512], ctab)),
                                reads=[stt_mark, rot_ld] + gq_prev, incs=[cg])
                        sqs_free[0] = [cp.mark(), cg.mark()]
                        self.op("gpsimd", (lambda gg, dap=dap, t1_s=t1_s, c0=c0:
                                gg.tensor_add(dap, dap, t1_s[:, c0:c0 + 512])),
                                reads=[cg.mark(), t1mul_mark], incs=[cg])
                        gq_prev = [cg.mark()]
                        t1_free[gp] = [cg.mark()]
            prod_done = cv.mark()
            prod_g_done = cg.mark()
            if wg == 3:
                self.wq_free = cp.mark()
            xw_free = cp.mark()
            rot_free = [cv.mark(), cg.mark()]

            for wi in range(2):
                w = 2 * wg + wi
                def flush_norm():
                    nonlocal pending_norm, zr_free
                    if pending_norm is None:
                        return
                    fobank, fob, j, z_done, pair_done = pending_norm
                    pending_norm = None
                    zsl = j % 2
                    self.op("scalar", (lambda a, fobank=fobank, zb=zb, zsl=zsl: a.activation(
                        zb[0:1, zsl * 256:(zsl + 1) * 256], fobank[0:1, 256:512],
                        AF.Ln)),
                        reads=[z_done] + zr_free + zb_free[zsl])
                    self.op("scalar", (lambda a, zb=zb, zr=zr, zsl=zsl: a.activation(
                        zr[0:1, 0:256], zb[0:1, zsl * 256:(zsl + 1) * 256],
                        AF.Exp, scale=-1.0)))
                    self.op("scalar", (lambda a, fobank=fobank, zb=zb, zsl=zsl: a.activation(
                        zb[0:1, zsl * 256:(zsl + 1) * 256], fobank[64:65, 256:512],
                        AF.Ln)))
                    self.op("scalar", (lambda a, zb=zb, zr=zr, zsl=zsl: a.activation(
                        zr[0:1, 256:512], zb[0:1, zsl * 256:(zsl + 1) * 256],
                        AF.Exp, scale=-1.0)),
                        incs=[ca])
                    zrec_done = ca.mark()
                    zps = self.bank(7)
                    for hh in range(2):
                        self.op("tensor", (lambda t, zps=zps, zr=zr, hh=hh: t.matmul(
                            zps[hh * 64:(hh + 1) * 64, 0:256], A["ones1f"][0:1, 0:64],
                            zr[0:1, hh * 256:hh * 256 + 256],
                            start=True, stop=True)),
                            reads=([zrec_done] + pbf[7]) if hh == 0 else [],
                            incs=[cp] if hh == 1 else [])
                    zbc_done = cp.mark()
                    zr_free = [zbc_done]
                    self.op("vector", (lambda v, zps=zps, zb=zb, zsl=zsl: v.tensor_copy(
                        zb[:, zsl * 256:(zsl + 1) * 256], zps[0:128, 0:256])),
                        reads=[zbc_done] + zb_free[zsl])
                    self.op("vector", (lambda v, fobank=fobank, zb=zb, zsl=zsl, j=j, ao=ao:
                            v.tensor_mul(ao[:, j * 256:(j + 1) * 256],
                                         fobank[0:128, 0:256],
                                         zb[:, zsl * 256:(zsl + 1) * 256])),
                        reads=[pair_done], incs=[cv])
                    pbf[fob] = [cv.mark()]
                    pbf[7] = [cv.mark()]
                    zb_free[zsl] = [cv.mark()]
                def emit_masks(hh):
                    sset = (1, 2) if hh % 2 == 0 else (3, 4)
                    mss = [2, 3] if w == 0 else [0, 1, 2, 3]
                    for mi, m in enumerate(mss):
                        ps = self.bank(sset[mi // 2])
                        colr = (mi % 2) * 256
                        self.op("tensor", (lambda t, ps=ps, colr=colr, m=m: t.matmul(
                            ps[:, colr:colr + 256], A["identb"][:, :],
                            A["sTm%d" % m][:, :], start=True, stop=False)),
                            reads=([prod_done, prod_g_done] + pbf[sset[0]] + pbf[sset[1]])
                            if mi == 0 else [])
                ms = [2, 3] if w == 0 else [0, 1, 2, 3]
                nm = len(ms)

                def emit_scores(hh):
                    g, hq = hh // 8, hh % 8
                    fc, pb = hq // 2, (hq % 2) * 64
                    sset = (1, 2) if hh % 2 == 0 else (3, 4)
                    psl = (hh % 2) * 1024
                    for pi in range(nm // 2):
                        ps = self.bank(sset[pi])
                        mname = "sTm01" if (nm == 4 and pi == 0) else "sTm23"
                        self.op("tensor", (lambda t, ps=ps, mname=mname: t.matmul(
                            ps[:, 0:512], A["identb"][:, :],
                            A[mname][:, :], start=True, stop=False)),
                            reads=([prod_done, prod_g_done] + pbf[sset[0]] + pbf[sset[1]])
                            if pi == 0 else [])
                        for sub in range(2):
                            mi = pi * 2 + sub
                            m = ms[mi]
                            colr = sub * 256
                            gc = 2 * w - 2 + m
                            if gc >= 4 * wg:
                                seg = gc - 4 * wg
                                kap = kbc[pb:pb + 64, (g * 4 + fc) * 512 + seg * 128:
                                          (g * 4 + fc) * 512 + seg * 128 + 128]
                            else:
                                seg = gc - (4 * wg - 2)
                                kap = kbp[pb:pb + 64, (g * 4 + fc) * 256 + seg * 128:
                                          (g * 4 + fc) * 256 + seg * 128 + 128]
                            qap = qb[pb:pb + 64, (g * 4 + fc) * 512 + wi * 256:
                                     (g * 4 + fc) * 512 + wi * 256 + 256]
                            self.op("tensor", (lambda t, ps=ps, colr=colr, kap=kap, qap=qap:
                                    t.matmul(ps[:, colr:colr + 256], kap, qap,
                                             start=False, stop=(colr == 256))),
                                    incs=[cp] if mi == nm - 1 else [])
                    return cp.mark()

                sT_next = emit_scores(0)
                for h in range(16):
                    if h % 2 == 1:
                        flush_norm()
                    g, hq = h // 8, h % 8
                    fc, pb = hq // 2, (hq % 2) * 64
                    stset = (1, 2) if h % 2 == 0 else (3, 4)
                    psl = (h % 2) * 1024
                    ob = 5 if (h // 2) % 2 == 0 else 6
                    obank = self.bank(ob)
                    sT_done = sT_next
                    if h < 15:
                        sT_next = emit_scores(h + 1)
                    exp_marks = []
                    for pi in range(nm // 2):
                        ps = self.bank(stset[pi])
                        self.op("scalar", (lambda a, ps=ps, pi=pi, pT=pT, psl=psl:
                                a.activation(
                            pT[:, psl + pi * 512:psl + (pi + 1) * 512], ps[:, 0:512],
                            AF.Exp, scale=QK8)),
                            reads=([sT_done] + pT_free[h % 2]) if pi == 0 else [], incs=[ca])
                        exp_marks.append(ca.mark())
                        exp_marks.append(ca.mark())
                    pbf[stset[0]] = [exp_marks[-1]]
                    pbf[stset[1]] = [exp_marks[-1]]
                    for mi, m in enumerate(ms):
                        self.op("tensor", (lambda t, mi=mi, pT=pT, obank=obank, pb=pb,
                                           nm=nm, psl=psl: t.matmul(
                            obank[pb:pb + 1, 256:512], A["onescolb"][:, 0:1],
                            pT[:, psl + mi * 256:psl + (mi + 1) * 256],
                            start=(mi == 0), stop=(mi == nm - 1))),
                            reads=([exp_marks[mi]] + (pbf[ob] if mi == 0 else [])),
                            incs=[cp] if mi == nm - 1 else [])
                    z_done = cp.mark()
                    for mi, m in enumerate(ms):
                        gc = 2 * w - 2 + m
                        if gc >= 4 * wg:
                            seg = gc - 4 * wg
                            vap = vbc[:, (g * 4 + seg) * 512 + hq * 64:
                                      (g * 4 + seg) * 512 + hq * 64 + 64]
                        else:
                            seg = gc - (4 * wg - 2)
                            vap = vbp[:, (g * 2 + seg) * 512 + hq * 64:
                                      (g * 2 + seg) * 512 + hq * 64 + 64]
                        self.op("tensor", (lambda t, vap=vap, mi=mi, pT=pT, obank=obank, pb=pb,
                                           nm=nm, psl=psl: t.matmul(
                            obank[pb:pb + 64, 0:256], vap,
                            pT[:, psl + mi * 256:psl + (mi + 1) * 256],
                            start=(mi == 0), stop=(mi == nm - 1))),
                            incs=[cp] if mi == nm - 1 else [])
                    pT_free[h % 2] = [cp.mark()]
                    if h % 2 == 1:
                        pending_norm = (obank, ob, h // 2, z_done, cp.mark())
                flush_norm()
                ao_done = cv.mark()
                if wi == 1 and wg < 3:
                    for cc in range(4, 8):
                        off = cc * GCOL + 1 + (wg + 1) * 512 - 1
                        self.op("vector", (lambda v, cc=cc, wg=wg, off=off, scol=scol, r=r:
                                v.tensor_copy(scol[:, wg * 4 + (cc - 4):wg * 4 + (cc - 4) + 1],
                                              r[:, off:off + 1])))
                    self.op("vector", lambda v: v.nop(), incs=[cv])
                    self.xw_done_next = self.emit_xw_cast(wg + 1, xw, r, scol, xw_free)
                for oc in range(8):
                    ps = self.bank(oc)
                    for c2 in range(8):
                        self.op("tensor", (lambda t, oc=oc, c2=c2, ps=ps, wo_b=wo_b, ao=ao:
                                t.matmul(ps[:, 0:256],
                                         wo_b[:, c2 * 1024 + oc * 128:c2 * 1024 + oc * 128 + 128],
                                         ao[:, c2 * 256:(c2 + 1) * 256],
                                         start=(c2 == 0), stop=(c2 == 7))),
                                reads=(([ao_done, wo_ld] + sum((pbf[i] for i in range(8)), []))
                                       if (oc == 0 and c2 == 0) else []),
                                incs=[cp] if c2 == 7 else [])
                    self.op("vector", (lambda v, oc=oc, ps=ps, w=w, r=r: v.tensor_add(
                        r[:, oc * GCOL + 1 + w * 256:oc * GCOL + 1 + (w + 1) * 256],
                        ps[:, 0:256],
                        r[:, oc * GCOL + 1 + w * 256:oc * GCOL + 1 + (w + 1) * 256])),
                        reads=[cp.mark()], incs=[cv])
                    pbf[oc] = [cv.mark()]
            if wg < 3:
                for g in range(2):
                    for fc in range(4):
                        self.op("vector", (lambda v, g=g, fc=fc, kbc=kbc, kbp=kbp: v.tensor_copy(
                            kbp[:, (g * 4 + fc) * 256:(g * 4 + fc) * 256 + 256],
                            kbc[:, (g * 4 + fc) * 512 + 256:(g * 4 + fc) * 512 + 512])),
                            reads=[prod_g_done] if (g == 0 and fc == 0) else [])
                    for u in range(2):
                        self.op("vector", (lambda v, g=g, u=u, vbc=vbc, vbp=vbp: v.tensor_copy(
                            vbp[:, (g * 2 + u) * 512:(g * 2 + u) * 512 + 512],
                            vbc[:, (g * 4 + 2 + u) * 512:(g * 4 + 2 + u) * 512 + 512])))
                self.op("vector", lambda v: v.nop(), incs=[cv])


    def layer_ffn(self, p, l):
        A = self.A
        cp, cv, ca = self.cp, self.cv, self.ca
        d_w1, d_w2 = self.d_w1, self.d_w2
        sc = self.scL
        r = self.r
        xn2 = self.sbs(sc, "xn2", [128, 8 * 1024], BF16)
        ag = self.sbs(sc, "ag", [128, 22 * 1024], BF16)
        w1b = self.sbs(sc, "w1b", [128, 6 * 1024], BF16)
        w2b = self.sbs(sc, "w2b", [128, 4 * 1024], BF16)
        gel = self.sbs(sc, "gel", [128, 1024], BF16)
        m_b = self.sbs(sc, "m_b", [128, 1024], F32)
        rstd_b = self.sbs(sc, "rstd_b", [128, 1024], F32)
        t1f = self.sbs(sc, "t1f", [128, 1024], F32)
        t2f = self.sbs(sc, "t2f", [128, 1024], F32)
        sqs = self.sbs(sc, "sqs", [128, 2 * 1024], BF16)

        CHW = [128] * 21 + [DI - 21 * 128]
        w1_free = [None] * 6
        w2_free = [None, None, None, None]
        pbf = {i: [] for i in range(8)}
        xn2_free = None
        prev_w2 = None
        gel_free = [None, None]
        for half in (1, 0):
            T0 = half * 1024
            for cc in range(8):
                off = cc * GCOL + 1 + T0 - (1 if cc >= 4 else 0)
                self.op("vector", (lambda v, cc=cc, off=off, xn2=xn2, r=r: v.tensor_copy(
                    xn2[:, cc * 1024:(cc + 1) * 1024], r[:, off:off + 1024])),
                    reads=[xn2_free] if cc == 0 else [])
            self.op("vector", lambda v: v.nop(), incs=[cv])
            rsb_done = cv.mark()
            for cc in range(8):
                for tt in range(2):
                    self.op("tensor", (lambda t, cc=cc, tt=tt, xn2=xn2: t.matmul(
                        self.bank(tt)[:, 0:512], A["onesfull"][:, :],
                        xn2[:, cc * 1024 + tt * 512:cc * 1024 + tt * 512 + 512],
                        start=(cc == 0), stop=(cc == 7))),
                        reads=(([rsb_done] + pbf[tt]) if cc == 0 else []),
                        incs=[cp] if cc == 7 else [])
            sq_free = [None, None]
            for cc in range(8):
                b = cc % 2
                self.op("scalar", (lambda a, cc=cc, b=b, xn2=xn2, sqs=sqs: a.activation(
                    sqs[:, b * 1024:(b + 1) * 1024], xn2[:, cc * 1024:(cc + 1) * 1024],
                    AF.Square)),
                    reads=[rsb_done, sq_free[b]], incs=[ca])
                sq_ca = ca.mark()
                for tt in range(2):
                    self.op("tensor", (lambda t, cc=cc, tt=tt, b=b, sqs=sqs: t.matmul(
                        self.bank(2 + tt)[:, 0:512], A["onesfull"][:, :],
                        sqs[:, b * 1024 + tt * 512:b * 1024 + tt * 512 + 512],
                        start=(cc == 0), stop=(cc == 7))),
                        reads=([sq_ca] if tt == 0 else []) + (pbf[2 + tt] if cc == 0 else []),
                        incs=[cp] if tt == 1 else [])
                sq_free[b] = cp.mark()
            stat_done = cp.mark()
            for tt in range(2):
                self.op("vector", (lambda v, tt=tt, m_b=m_b: v.tensor_scalar(
                    m_b[:, tt * 512:(tt + 1) * 512], self.bank(tt)[:, 0:512],
                    1.0 / D, None, op0=ALU.mult)), reads=[stat_done])
                self.op("vector", (lambda v, tt=tt, t1f=t1f: v.tensor_scalar(
                    t1f[:, tt * 512:(tt + 1) * 512], self.bank(2 + tt)[:, 0:512],
                    1.0 / D, None, op0=ALU.mult)))
                self.op("vector", (lambda v, tt=tt, m_b=m_b, t2f=t2f: v.tensor_mul(
                    t2f[:, tt * 512:(tt + 1) * 512], m_b[:, tt * 512:(tt + 1) * 512],
                    m_b[:, tt * 512:(tt + 1) * 512])))
                self.op("vector", (lambda v, tt=tt, t1f=t1f, t2f=t2f: v.tensor_sub(
                    t1f[:, tt * 512:(tt + 1) * 512], t1f[:, tt * 512:(tt + 1) * 512],
                    t2f[:, tt * 512:(tt + 1) * 512])), incs=[cv])
            self.op("scalar", (lambda a, t1f=t1f, t2f=t2f: a.activation(
                t2f[:, 0:1024], t1f[:, 0:1024], AF.Ln, bias=A["epsc"][:, 0:1])),
                reads=[cv.mark()], incs=[ca])
            self.op("scalar", (lambda a, t2f=t2f, rstd_b=rstd_b: a.activation(
                rstd_b[:, 0:1024], t2f[:, 0:1024], AF.Exp, scale=-0.5)), incs=[ca])
            self.op("vector", lambda v: v.nop(), reads=[ca.mark()], incs=[cv])
            var_done = cv.mark()
            for i in range(4):
                pbf[i] = [var_done]
            for cc in range(8):
                off = cc * GCOL + 1 + T0 - (1 if cc >= 4 else 0)
                self.op("vector", (lambda v, cc=cc, off=off, t1f=t1f, m_b=m_b, r=r: v.tensor_sub(
                    t1f[:, 0:1024], r[:, off:off + 1024], m_b[:, 0:1024])),
                    reads=[var_done] if cc == 0 else [])
                self.op("vector", (lambda v, t1f=t1f, rstd_b=rstd_b: v.tensor_mul(
                    t1f[:, 0:1024], t1f[:, 0:1024], rstd_b[:, 0:1024])))
                self.op("vector", (lambda v, cc=cc, l=l, t1f=t1f, xn2=xn2: v.tensor_scalar(
                    xn2[:, cc * 1024:(cc + 1) * 1024], t1f[:, 0:1024],
                    A["lng_c"][:, l * 8 + cc:l * 8 + cc + 1],
                    A["lnb_c"][:, l * 8 + cc:l * 8 + cc + 1],
                    op0=ALU.mult, op1=ALU.add)), incs=[cv])
            xn2_done = cv.mark()
            first_ag = True
            for j in range(22):
                wda = CHW[j]
                for agx in range(2):
                    slot = (j % 3) * 2 + agx
                    c0 = agx * DI + j * 128
                    jj = j * 2 + agx
                    key = ("w1", l, jj)
                    if key not in self.wb_marks:
                        self.op("gpsimd", (lambda g, l=l, c0=c0, wda=wda, slot=slot, w1b=w1b:
                                g.dma_start(
                            out=w1b[:, slot * 1024:slot * 1024 + 8 * wda].rearrange(
                                "p (k f) -> p k f", k=8),
                            in_=self.d["L0_ffn_W1"][l][:, c0:c0 + wda].rearrange(
                                "(k p) f -> p k f", p=128))),
                            reads=[w1_free[slot]] + self.w1_slot_guard[slot],
                            incs=[d_w1[slot]])
                        ld = d_w1[slot].mark()
                        self.op("sync", (lambda s, l=l, jj=jj, wda=wda, slot=slot, w1b=w1b:
                                s.dma_start(
                            out=self.w1cd[l][:, jj, :, 0:wda],
                            in_=w1b[:, slot * 1024:slot * 1024 + 8 * wda].rearrange(
                                "p (k f) -> p k f", k=8))),
                            reads=[ld], incs=[self.d_wc])
                        self.wb_marks[key] = self.d_wc.mark()
                        self.w1_slot_guard[slot] = [self.wb_marks[key]]
                    else:
                        qeng = "sync" if j % 2 == 0 else "gpsimd"
                        self.op(qeng, (lambda g, l=l, jj=jj, wda=wda, slot=slot, w1b=w1b:
                                g.dma_start(
                            out=w1b[:, slot * 1024:slot * 1024 + 8 * wda].rearrange(
                                "p (k f) -> p k f", k=8),
                            in_=self.w1cd[l][:, jj, :, 0:wda])),
                            reads=[w1_free[slot], self.wb_marks[key]], incs=[d_w1[slot]])
                w1_ld = [d_w1[(j % 3) * 2].mark(), d_w1[(j % 3) * 2 + 1].mark()]
                for agx in range(2):
                    slot = (j % 3) * 2 + agx
                    bs0 = (0 if j % 2 == 0 else 4) + agx * 2
                    for tt in range(2):
                        ps = self.bank(bs0 + tt)
                        for k in range(8):
                            self.op("tensor", (lambda t, j=j, k=k, tt=tt, slot=slot, wda=wda,
                                               ps=ps, w1b=w1b, xn2=xn2: t.matmul(
                                ps[0:wda, 0:512],
                                w1b[:, slot * 1024 + k * wda:slot * 1024 + (k + 1) * wda],
                                xn2[:, k * 1024 + tt * 512:k * 1024 + tt * 512 + 512],
                                start=(k == 0), stop=(k == 7))),
                                reads=(([xn2_done, w1_ld[agx]] + pbf[bs0 + tt] +
                                        ([prev_w2] if first_ag else []))
                                       if k == 0 else []),
                                incs=[cp] if k == 7 else [])
                first_ag = False
                hg_done = cp.mark()
                w1_free[(j % 3) * 2] = hg_done
                w1_free[(j % 3) * 2 + 1] = hg_done
                bsa = (0 if j % 2 == 0 else 4)
                for tt in range(2):
                    psg = self.bank(bsa + 2 + tt)
                    psa = self.bank(bsa + tt)
                    self.op("scalar", (lambda a, j=j, tt=tt, l=l, wda=wda, psg=psg, gel=gel:
                            a.activation(gel[0:wda, tt * 512:(tt + 1) * 512],
                                         psg[0:wda, 0:512], AF.Gelu,
                                         bias=A["b1_c"][0:wda, l * 44 + 22 + j:l * 44 + 23 + j])),
                            reads=[hg_done, gel_free[tt]], incs=[ca])
                    self.op("vector", (lambda v, j=j, tt=tt, l=l, wda=wda, psa=psa, gel=gel,
                                       ag=ag: v.scalar_tensor_tensor(
                        ag[0:wda, j * 1024 + tt * 512:j * 1024 + tt * 512 + 512],
                        psa[0:wda, 0:512],
                        A["b1_c"][0:wda, l * 44 + j:l * 44 + j + 1],
                        gel[0:wda, tt * 512:(tt + 1) * 512],
                        op0=ALU.add, op1=ALU.mult)),
                        reads=[ca.mark()], incs=[cv])
                    pbf[bsa + tt] = [cv.mark()]
                    pbf[bsa + 2 + tt] = [cv.mark()]
                    gel_free[tt] = cv.mark()
            ag_done = cv.mark()
            for tt in range(2):
                for c2 in range(22):
                    wda = CHW[c2]
                    slot = c2 % 4
                    key = ("w2", l, c2)
                    if key not in self.wb_marks:
                        self.op("gpsimd", (lambda g, l=l, c2=c2, wda=wda, slot=slot, w2b=w2b:
                                g.dma_start(
                            out=w2b[0:wda, slot * 1024:slot * 1024 + 1024],
                            in_=self.d["L0_ffn_W2"][l][c2 * 128:c2 * 128 + wda, :])),
                            reads=[w2_free[slot]] + self.w2_slot_guard[slot],
                            incs=[d_w2[slot]])
                        ld = d_w2[slot].mark()
                        self.op("sync", (lambda s, l=l, c2=c2, wda=wda, slot=slot, w2b=w2b:
                                s.dma_start(
                            out=self.w2cd[l][c2][0:wda, :],
                            in_=w2b[0:wda, slot * 1024:slot * 1024 + 1024])),
                            reads=[ld], incs=[self.d_wc])
                        self.wb_marks[key] = self.d_wc.mark()
                        self.w2_slot_guard[slot] = [self.wb_marks[key]]
                    else:
                        qeng = "sync" if c2 % 2 == 0 else "gpsimd"
                        self.op(qeng, (lambda g, l=l, c2=c2, wda=wda, slot=slot, w2b=w2b:
                                g.dma_start(
                            out=w2b[0:wda, slot * 1024:slot * 1024 + 1024],
                            in_=self.w2cd[l][c2][0:wda, :])),
                            reads=[w2_free[slot], self.wb_marks[key]], incs=[d_w2[slot]])
                    w2_ld = d_w2[slot].mark()
                    for oc in range(8):
                        ps = self.bank(oc)
                        self.op("tensor", (lambda t, c2=c2, oc=oc, tt=tt, wda=wda, slot=slot,
                                           ps=ps, w2b=w2b, ag=ag: t.matmul(
                            ps[:, 0:512],
                            w2b[0:wda, slot * 1024 + oc * 128:slot * 1024 + oc * 128 + 128],
                            ag[0:wda, c2 * 1024 + tt * 512:c2 * 1024 + tt * 512 + 512],
                            start=(c2 == 0), stop=(c2 == 21))),
                            reads=(([ag_done, w2_ld] + sum((pbf[i] for i in range(8)), []))
                                   if (c2 == 0 and oc == 0) else
                                   ([w2_ld] if oc == 0 else [])),
                            incs=[cp] if oc == 7 else [])
                    w2_free[slot] = cp.mark()
                w2_done = cp.mark()
                for oc in range(8):
                    ps = self.bank(oc)
                    off = oc * GCOL + 1 + T0 + tt * 512
                    self.op("vector", (lambda v, oc=oc, l=l, off=off, ps=ps, r=r:
                            v.scalar_tensor_tensor(
                        r[:, off:off + 512], ps[:, 0:512],
                        A["b2_c"][:, l * 8 + oc:l * 8 + oc + 1],
                        r[:, off:off + 512], op0=ALU.add, op1=ALU.add)),
                        reads=[w2_done], incs=[cv])
                    pbf[oc] = [cv.mark()]
            xn2_free = cp.mark()
            prev_w2 = cp.mark()

    def g_store(self, p):
        ca = self.ca
        d_g = self.d_g
        r = self.r
        gsc = contextlib.ExitStack()
        gtmp = self.sbs(gsc, "gtmp", [128, 2 * 2048], BF16)
        self.fence("scalar")
        gt_free = [None, None]
        for cc in range(8):
            b = cc % 2
            self.op("scalar", (lambda a, cc=cc, b=b, r=r, gtmp=gtmp: a.activation(
                gtmp[:, b * 2048:(b + 1) * 2048],
                r[:, cc * GCOL + 1:cc * GCOL + 1 + 2048], AF.Silu)),
                reads=[gt_free[b]], incs=[ca])
            self.op("sync", (lambda s, cc=cc, b=b, gtmp=gtmp: s.dma_start(
                out=self.gscrd[cc * 128:(cc + 1) * 128, :],
                in_=gtmp[:, b * 2048:(b + 1) * 2048])),
                reads=[ca.mark()], incs=[d_g[b]])
            gt_free[b] = d_g[b].mark()
        gsc.close()

    def phase_C(self, p):
        A, d = self.A, self.d
        cp, cv, ca = self.cp, self.cv, self.ca
        d_xc, d_gt, d_wb, d_wo = self.d_xc, self.d_gt, self.d_wb, self.d_wo
        d_yr, d_out, d_tap = self.d_yr, self.d_out, self.d_tap
        sc = self.scC
        winb = self.sbs(sc, "winb", [128, 4 * 1024], BF16)
        self.sbs(sc, "lnsq", [128, 1024], BF16)
        woutt_b = self.sbs(sc, "woutt_b", [128, 2 * 1024], BF16)
        xc = self.sbs(sc, "xc", [128, 4 * 1024], F32)
        xntc = self.sbs(sc, "xntc", [128, 4 * 1024], BF16)
        xnfc = self.sbs(sc, "xnfc", [128, 8 * 512], BF16)
        gtok = self.sbs(sc, "gtok", [128, 4 * 1024], BF16)
        xhb = self.sbs(sc, "xhb", [128, 16 * (CW + 4)], BF16)
        xhs = self.sbs(sc, "xhs", [128, 16 * 512], BF16)
        gateb = self.sbs(sc, "gateb", [128, 16 * 512], BF16)
        ubf = self.sbs(sc, "ubf", [NS, 512], BF16)
        diag = self.sbs(sc, "diag", [128, 64 * 128], BF16)
        ylsb = self.sbs(sc, "ylsb", [8, TSC], F32)
        yrow = self.sbs(sc, "yrow", [1, 512], F32)
        m1s = self.sbs(sc, "m1s", [128, 2 * 512], F32)
        t2s = self.sbs(sc, "t2s", [128, 512], F32)
        sgb = self.sbs(sc, "sgb", [128, 2 * 512], BF16)
        outst = self.sbs(sc, "outst", [128, 4 * 1024], F32)
        identb = A["identb"]

        self.op("vector", (lambda v, xhb=xhb: v.memset(
            xhb[:, :].rearrange("p (s c) -> p s c", s=16)[:, :, 0:4], 0.0)))
        self.op("vector", lambda v: v.memset(A["hcur"][0:NS, 0:1], 0.0))
        for f in range(16):
            for k in range(4):
                self.op("vector", (lambda v, f=f, k=k, diag=diag, identb=identb: v.tensor_scalar(
                    diag[:, (f * 4 + k) * 128:(f * 4 + k) * 128 + 128],
                    identb[:, :], A["convw_c"][:, k * 16 + f:k * 16 + f + 1], None,
                    op0=ALU.mult)), incs=[cv] if (f == 15 and k == 3) else [])
        diag_done = cv.mark()

        xc_free = None
        gtok_free = None
        outst_free = None
        woutt_free = [None, None]
        winb_free = [None] * 4
        xz_free = [None, None]
        conv_free = [None, None]
        sg_free = [None, None]
        prev_convmm = None
        prev_oproj = None
        prev_comb = None
        for ch in range(NCHK):
            t0 = ch * CW
            self.op("sync", (lambda s, t0=t0, p=p, xc=xc: s.dma_start(
                out=xc[:, :].rearrange("p (n f) -> p n f", n=4),
                in_=self.d["L0_attn_x"][p, t0:t0 + CW, :].rearrange("(n p) f -> p n f", p=128))),
                reads=[xc_free], incs=[d_xc])
            xld = d_xc.mark()
            for sub in range(4):
                self.op("sync", (lambda s, t0=t0, sub=sub, gtok=gtok: s.dma_start(
                    out=gtok[:, sub * 1024:(sub + 1) * 1024],
                    in_=self.gscrd[:, t0 + sub * 128:t0 + (sub + 1) * 128], transpose=True)),
                    reads=[gtok_free] if sub == 0 else [], incs=[d_gt])
            gld = d_gt.mark()
            xn_subs = []
            for sub in range(4):
                self.ln_tok(xc[:, sub * 1024:(sub + 1) * 1024],
                            xntc[:, sub * 1024:(sub + 1) * 1024], sub % 2,
                            extra_reads=[xld])
                xn_subs.append(cv.mark())
            xn_done = cv.mark()
            pt_free = [prev_comb, prev_comb]
            for sub in range(4):
                psb = self.bankb(4 + sub % 2)
                for cc2 in range(8):
                    self.op("tensor", (lambda t, sub=sub, cc2=cc2, psb=psb, xntc=xntc, identb=identb:
                            t.transpose(
                                psb[:, cc2 * 128:(cc2 + 1) * 128],
                                xntc[:, sub * 1024 + cc2 * 128:sub * 1024 + (cc2 + 1) * 128],
                                identb[:, :])),
                            reads=[xn_subs[sub], pt_free[sub % 2]] if cc2 == 0 else [],
                            incs=[cp] if cc2 == 7 else [])
                def evt(v, sub=sub, psb=psb, xnfc=xnfc):
                    last = None
                    for cc2 in range(8):
                        last = v.tensor_copy(
                            xnfc[:, cc2 * 512 + sub * 128:cc2 * 512 + (sub + 1) * 128],
                            psb[:, cc2 * 128:(cc2 + 1) * 128])
                    return last
                self.op("vector", evt, reads=[cp.mark()], incs=[cv])
                pt_free[sub % 2] = cv.mark()
            xnf_done = cv.mark()
            for f in range(32):
                wb = f % 4
                self.op("gpsimd", (lambda g, f=f, wb=wb, winb=winb: g.dma_start(
                    out=winb[:, wb * 1024:wb * 1024 + 1024].rearrange("p (k f) -> p k f", k=8),
                    in_=self.wintd[:, f * 128:(f + 1) * 128].rearrange("(k p) f -> p k f", p=128))),
                    reads=[winb_free[wb]], incs=[d_wb[wb]])
                wld = d_wb[wb].mark()
                bsel = f % 2
                ps = self.bank(bsel)
                for k in range(8):
                    self.op("tensor", (lambda t, f=f, k=k, ps=ps, wb=wb, winb=winb, xnfc=xnfc:
                            t.matmul(
                                ps[:, 0:512], winb[:, wb * 1024 + k * 128:wb * 1024 + k * 128 + 128],
                                xnfc[:, k * 512:(k + 1) * 512],
                                start=(k == 0), stop=(k == 7))),
                            reads=[xnf_done, wld, xz_free[bsel]] if k == 0 else [],
                            incs=[cp] if k == 7 else [])
                winb_free[wb] = cp.mark()
                if f < 16:
                    self.op("vector", (lambda v, f=f, ps=ps, xhb=xhb: v.tensor_scalar(
                        xhb[:, f * (CW + 4) + 4:f * (CW + 4) + 4 + CW],
                        ps[:, 0:512], A["bin_c"][:, f:f + 1], None, op0=ALU.add)),
                        reads=[cp.mark(), prev_convmm], incs=[cv])
                else:
                    self.op("vector", (lambda v, f=f, ps=ps, gateb=gateb: v.tensor_scalar(
                        gateb[:, (f - 16) * 512:(f - 16) * 512 + 512],
                        ps[:, 0:512], A["bin_c"][:, f:f + 1], None, op0=ALU.add)),
                        reads=[cp.mark()], incs=[cv])
                xz_free[bsel] = cv.mark()
            xz_done = cv.mark()
            for f in range(16):
                bsel = 2 + f % 2
                ps = self.bank(bsel)
                for k in range(4):
                    self.op("tensor", (lambda t, f=f, k=k, ps=ps, diag=diag, xhb=xhb: t.matmul(
                        ps[:, 0:512], diag[:, (f * 4 + k) * 128:(f * 4 + k) * 128 + 128],
                        xhb[:, f * (CW + 4) + 1 + k:f * (CW + 4) + 1 + k + CW],
                        start=(k == 0), stop=(k == 3))),
                        reads=([xz_done, diag_done, conv_free[f % 2]] if k == 0 else []),
                        incs=[cp] if k == 3 else [])
                self.op("scalar", (lambda a, f=f, ps=ps, xhs=xhs: a.activation(
                    xhs[:, f * 512:(f + 1) * 512], ps[:, 0:512], AF.Silu,
                    bias=A["convb_c"][:, f:f + 1])),
                    reads=[cp.mark(), prev_oproj], incs=[ca])
                conv_free[f % 2] = ca.mark()
            conv_done = ca.mark()
            prev_convmm = cp.mark()
            if ch < NCHK - 1:
                self.op("vector", (lambda v, xhb=xhb: v.tensor_copy(
                    xhb[:, :].rearrange("p (s c) -> p s c", s=16)[:, :, 0:4],
                    xhb[:, :].rearrange("p (s c) -> p s c", s=16)[:, :, CW:CW + 4])),
                    reads=[prev_convmm])
            pu = self.bank(6)
            for k in range(16):
                self.op("tensor", (lambda t, k=k, pu=pu, xhs=xhs: t.matmul(
                    pu[0:NS, 0:512], A["wssmt"][:, k * NS:(k + 1) * NS],
                    xhs[:, k * 512:(k + 1) * 512], start=(k == 0), stop=(k == 15))),
                    reads=[conv_done, cv.mark()] if k == 0 else [],
                    incs=[cp] if k == 15 else [])
            self.op("vector", (lambda v, pu=pu, ubf=ubf: v.tensor_scalar(
                ubf[0:NS, 0:512], pu[0:NS, 0:512], A["bssm_c"][0:NS, 0:1], None,
                op0=ALU.add)), reads=[cp.mark()], incs=[cv])
            u_done = cv.mark()
            ph = self.bank(6)
            for j in range(TSC):
                self.op("tensor", (lambda t, j=j, ph=ph, ubf=ubf: t.matmul(
                    ph[0:NS, 0:8],
                    A["pbext"][0:NS, (TSC - 1 - j) * NS:(TSC - j) * NS],
                    ubf[0:NS, j:512:TSC], start=(j == 0), stop=(j == TSC - 1))),
                    reads=[u_done] if j == 0 else [],
                    incs=[cp] if j == TSC - 1 else [])
            self.op("vector", (lambda v, ph=ph: v.tensor_copy(A["hdsb"][0:NS, 0:8], ph[0:NS, 0:8])),
                    reads=[cp.mark()])
            self.op("vector", (lambda v: v.tensor_copy(A["Hsb"][0:NS, 0:1], A["hcur"][0:NS, 0:1])),
                    incs=[cv])
            hd_done = cv.mark()
            py = self.bank(7)
            for j in range(TSC):
                if j < 8:
                    self.op("tensor", (lambda t, j=j, ph=ph: t.matmul(
                        ph[0:NS, 8 + (j % 2) * 2:9 + (j % 2) * 2], A["q64f"][0:NS, 0:NS],
                        A["Hsb"][0:NS, j:j + 1], start=True, stop=False)),
                        reads=[hd_done] if j == 0 else [cv.mark()])
                    self.op("tensor", (lambda t, j=j, ph=ph: t.matmul(
                        ph[0:NS, 8 + (j % 2) * 2:9 + (j % 2) * 2], A["i32f"][0:NS, 0:NS],
                        A["hdsb"][0:NS, j:j + 1], start=False, stop=True)),
                        incs=[cp])
                    self.op("vector", (lambda v, j=j, ph=ph: v.tensor_copy(
                        A["Hsb"][0:NS, j + 1:j + 2], ph[0:NS, 8 + (j % 2) * 2:9 + (j % 2) * 2])),
                        reads=[cp.mark()], incs=[cv])
                self.op("tensor", (lambda t, j=j, py=py, ubf=ubf: t.matmul(
                    py[0:8, 0:TSC], ubf[0:NS, j:512:TSC],
                    A["k0ext"][0:NS, TSC - j:2 * TSC - j], start=(j == 0), stop=False)),
                    reads=[u_done, cv.mark()] if j == 0 else [])
            self.op("vector", (lambda v: v.tensor_copy(A["HBb"][0:NS, 0:8], A["Hsb"][0:NS, 0:8])),
                    reads=[cv.mark()], incs=[cv])
            self.op("vector", (lambda v: v.tensor_copy(A["hcur"][0:NS, 0:1], A["Hsb"][0:NS, 8:9])))
            self.op("tensor", (lambda t, py=py: t.matmul(
                py[0:8, 0:TSC], A["HBb"][0:NS, 0:8], A["w2t"][0:NS, 0:TSC],
                start=False, stop=True)), reads=[cv.mark()], incs=[cp])
            self.op("vector", (lambda v, py=py, ylsb=ylsb: v.tensor_copy(
                ylsb[0:8, 0:TSC], py[0:8, 0:TSC])),
                reads=[cp.mark()], incs=[cv])
            self.op("sync", (lambda s, ylsb=ylsb, yrow=yrow: s.dma_start(
                out=yrow[0:1, 0:512].rearrange("o (c t) -> o c t", c=8),
                in_=ylsb[0:8, 0:TSC])),
                reads=[cv.mark()], incs=[d_yr])
            pyb = self.bank(2)
            self.op("tensor", (lambda t, pyb=pyb, yrow=yrow: t.matmul(
                pyb[:, 0:512], A["ones1f"][0:1, 0:128], yrow[0:1, 0:512],
                start=True, stop=True)),
                reads=[d_yr.mark(), conv_free[0], conv_free[1]], incs=[cp])
            ybc_done = cp.mark()
            for f in range(16):
                b = f % 2
                self.op("scalar", (lambda a, f=f, b=b, sgb=sgb, gateb=gateb: a.activation(
                    sgb[:, b * 512:(b + 1) * 512], gateb[:, f * 512:(f + 1) * 512],
                    AF.Silu)),
                    reads=[sg_free[b]], incs=[ca])
                self.op("vector", (lambda v, f=f, pyb=pyb, m1s=m1s: v.tensor_scalar(
                    m1s[:, 0:512], pyb[:, 0:512], A["dv_c"][:, f:f + 1], None,
                    op0=ALU.add)), reads=[ybc_done])
                self.op("vector", (lambda v, f=f, m1s=m1s, t2s=t2s, xhs=xhs: v.tensor_mul(
                    t2s[:, 0:512], m1s[:, 0:512], xhs[:, f * 512:(f + 1) * 512])))
                self.op("vector", (lambda v, f=f, b=b, t2s=t2s, xhs=xhs, sgb=sgb: v.tensor_mul(
                    xhs[:, f * 512:(f + 1) * 512], t2s[:, 0:512],
                    sgb[:, b * 512:(b + 1) * 512])),
                    reads=[ca.mark()], incs=[cv])
                sg_free[b] = cv.mark()
            yfin_done = cv.mark()
            for sub in range(4):
                for half in range(2):
                    bsel = sub * 2 + half
                    ps = self.bank(bsel)
                    self.op("tensor", (lambda t, half=half, ps=ps: t.matmul(
                        ps[:, 0:512], A["ones1b"][0:1, :],
                        A["bout_rowb"][0:1, half * 512:(half + 1) * 512],
                        start=True, stop=False)),
                        reads=([yfin_done, xz_free[0], xz_free[1]]
                               if (sub == 0 and half == 0) else []))
            for hk in range(16):
                b = hk % 2
                self.op("gpsimd", (lambda g, hk=hk, b=b, woutt_b=woutt_b: g.dma_start(
                    out=woutt_b[:, b * 1024:(b + 1) * 1024],
                    in_=self.wouttd[hk * 128:(hk + 1) * 128, :])),
                    reads=[woutt_free[b]], incs=[d_wo[b]])
                woutt_ld = d_wo[b].mark()
                for sub in range(4):
                    for half in range(2):
                        bsel = sub * 2 + half
                        ps = self.bank(bsel)
                        first = (hk == 0 and sub == 0 and half == 0)
                        self.op("tensor", (lambda t, hk=hk, b=b, sub=sub, half=half, ps=ps,
                                           xhs=xhs, woutt_b=woutt_b:
                                t.matmul(ps[:, 0:512],
                                         xhs[:, hk * 512 + sub * 128:hk * 512 + (sub + 1) * 128],
                                         woutt_b[:, b * 1024 + half * 512:b * 1024 + (half + 1) * 512],
                                         start=False, stop=(hk == 15))),
                                reads=([yfin_done, woutt_ld, xz_free[0], xz_free[1]]
                                       if first else ([woutt_ld] if (sub == 0 and half == 0) else [])),
                                incs=[cp] if (sub == 3 and half == 1) else [])
                woutt_free[b] = cp.mark()
            oproj_done = cp.mark()
            prev_oproj = oproj_done
            for sub in range(4):
                for half in range(2):
                    bsel = sub * 2 + half
                    ps = self.bank(bsel)
                    self.op("vector", (lambda v, sub=sub, half=half, ps=ps, m1s=m1s, xntc=xntc:
                            v.tensor_add(
                        m1s[:, 512:1024], ps[:, 0:512],
                        xntc[:, sub * 1024 + half * 512:sub * 1024 + (half + 1) * 512])),
                        reads=[oproj_done, gld, outst_free] if (sub == 0 and half == 0) else [])
                    self.op("vector", (lambda v, sub=sub, half=half, t2s=t2s, m1s=m1s, gtok=gtok:
                            v.tensor_mul(
                                t2s[:, 0:512], m1s[:, 512:1024],
                                gtok[:, sub * 1024 + half * 512:sub * 1024 + (half + 1) * 512])))
                    self.op("vector", (lambda v, sub=sub, half=half, outst=outst, t2s=t2s, xc=xc:
                            v.tensor_add(
                                outst[:, sub * 1024 + half * 512:sub * 1024 + (half + 1) * 512],
                                t2s[:, 0:512],
                                xc[:, sub * 1024 + half * 512:sub * 1024 + (half + 1) * 512])),
                            incs=[cv] if half == 1 else [])
            comb_done = cv.mark()
            prev_comb = comb_done
            xz_free = [comb_done, comb_done]
            conv_free = [comb_done, comb_done]
            self.op("sync", (lambda s, t0=t0, p=p, outst=outst: s.dma_start(
                out=self.outd[p, t0:t0 + CW, :].rearrange("(n p) f -> p n f", p=128),
                in_=outst[:, :].rearrange("p (n f) -> p n f", n=4))),
                reads=[comb_done], incs=[d_out])
            outst_free = d_out.mark()
            xc_free = comb_done
            gtok_free = comb_done

        tapsrc = {"xnf": xnfc, "xh": xhb, "gate": gateb, "u": ubf, "yrowt": yrow, "yfin": xhs}
        for nm, hnd in tapsrc.items():
            if nm in self.taps:
                sh = list(hnd.shape)
                dt = hnd.ap().dtype
                tpp = self.mktap(nm, [self.npass] + sh, dt)
                n = sh[0]
                self.op("sync", (lambda s, tpp=tpp, hnd=hnd, n=n, p=p: s.dma_start(
                    out=tpp[p][0:n, :], in_=hnd[0:n, :])),
                    reads=[cv.mark(), ca.mark()], incs=[d_tap])



_BUILD_CACHE = {}


def get_build(npass=4, depth=6, taps=()):
    key = (npass, depth, tuple(sorted(taps)))
    if key not in _BUILD_CACHE:
        _BUILD_CACHE[key] = KB(npass, depth, taps).build()
    return _BUILD_CACHE[key]


def kernel(**inputs):
    kb = get_build(4, 6)
    x = np.ascontiguousarray(inputs["L0_attn_x"])
    in_maps = []
    for c in range(8):
        m = {k: np.ascontiguousarray(v) for k, v in inputs.items() if k != "L0_attn_x"}
        m["L0_attn_x"] = np.ascontiguousarray(x[c * 4:(c + 1) * 4])
        in_maps.append(m)
    res = run_bass_kernel_spmd(kb.nc, in_maps, core_ids=list(range(8)))
    return np.concatenate([r["out"] for r in res.results], axis=0)
